# version 20
# baseline (speedup 1.0000x reference)
import types
import numpy as np
import ml_dtypes
from contextlib import ExitStack
import concourse.bass as bass
import concourse.mybir as mybir
from concourse.bass_utils import run_bass_kernel_spmd


def _freeze(fn):
    if fn.__closure__:
        cells = []
        for c in fn.__closure__:
            try:
                cells.append(types.CellType(c.cell_contents))
            except ValueError:
                cells.append(c)
        fn = types.FunctionType(fn.__code__, fn.__globals__, fn.__name__,
                                fn.__defaults__, tuple(cells))
    return fn

F32 = mybir.dt.float32
BF16 = mybir.dt.bfloat16
AF = mybir.ActivationFunctionType
ALU = mybir.AluOpType
AX = mybir.AxisListType
BF = ml_dtypes.bfloat16

NCORES = 8
T = 2048
NSEQ = 2
NTOK = T * NSEQ
D = 1024
NCH = D // 128
CW = 512
NEG = -1e30


class _Scope:
    def __init__(self, pb):
        self.pb = pb
        self.stack = ExitStack()

    def __enter__(self):
        return self

    def sb(self, p, f, dtype=F32):
        self.pb.n_sb += 1
        return self.stack.enter_context(
            self.pb.nc.sbuf_tensor(f"sb{self.pb.n_sb}", [p, f], dtype))

    def ps(self, p, f, dtype=F32):
        self.pb.n_sb += 1
        return self.stack.enter_context(
            self.pb.nc.psum_tensor(f"ps{self.pb.n_sb}", [p, f], dtype))

    def __exit__(self, *a):
        self.stack.close()
        return False


EPOCH_TICKS = 26000


class PB:
    def __init__(self, name):
        self.name = name
        self.nc = bass.Bass("TRN2", target_bir_lowering=False, debug=False)
        self.stack = ExitStack()
        self.ops = []
        self.dram_aps = {}
        self.n_sb = 0
        self.epochs = []
        self._new_epoch()

    def _new_epoch(self):
        e = len(self.epochs)
        self.epochs.append({
            "tick": self.stack.enter_context(self.nc.semaphore(f"tick{e}")),
            "gtick": self.stack.enter_context(self.nc.semaphore(f"gtick{e}")),
            "cnt": 0, "gcnt": 0})

    def dram(self, name, shape, dtype=F32, kind="ExternalInput"):
        if name in self.dram_aps:
            return self.dram_aps[name]
        ap = self.nc.dram_tensor(name, list(shape), dtype, kind=kind).ap()
        self.dram_aps[name] = ap
        return ap

    def sb(self, p, f, dtype=F32):
        self.n_sb += 1
        return self.stack.enter_context(
            self.nc.sbuf_tensor(f"sb{self.n_sb}", [p, f], dtype))

    def ps(self, p, f, dtype=F32):
        self.n_sb += 1
        return self.stack.enter_context(
            self.nc.psum_tensor(f"ps{self.n_sb}", [p, f], dtype))

    def scope(self):
        return _Scope(self)

    _chain = None

    def chain_begin(self):
        ep = self.epochs[-1]
        if ep["cnt"] + ep["gcnt"] > EPOCH_TICKS - 500:
            self._new_epoch()
        self._chain = len(self.ops)

    def chain_end(self):
        start, self._chain = self._chain, None
        n = len(self.ops) - start
        if n <= 1:
            return
        engs = {r[1] for r in self.ops[start:]}
        whichs = {r[4] for r in self.ops[start:]}
        assert len(engs) == 1 and whichs == {"t"}, (engs, whichs)
        for i in range(start, len(self.ops)):
            rec = list(self.ops[i])
            rec[6] = "head" if i == start else ("end" if i == len(self.ops) - 1 else "mid")
            self.ops[i] = tuple(rec)

    def op(self, eng, fn, inc=1, _gdma=False):
        ep = self.epochs[-1]
        if self._chain is None and ep["cnt"] + ep["gcnt"] + inc > EPOCH_TICKS:
            self._new_epoch()
            ep = self.epochs[-1]
        self.ops.append((len(self.epochs) - 1, eng, _freeze(fn),
                         (ep["cnt"], ep["gcnt"]), "g" if _gdma else "t", inc, None))
        ep["gcnt" if _gdma else "cnt"] += inc

    def dma(self, out, in_, q="sync"):
        self.op(q, lambda e, o=out, i=in_: e.dma_start(out=o, in_=i), 16,
                _gdma=(q == "gpsimd"))

    NPOOL = 24

    def _side_init(self):
        if not hasattr(self, "_pool"):
            self._pool = [self.stack.enter_context(self.nc.semaphore(f"pool{j}"))
                          for j in range(self.NPOOL)]
            self._pool_cnt = [0] * self.NPOOL
            self._rr_hw = 0
            self._rr_sw = 0
            self._rr_c = 0

    def mark(self):
        ep = self.epochs[-1]
        return (len(self.epochs) - 1, ep["cnt"], ep["gcnt"])

    def side_dma(self, out, in_, q="sync", at=None, joins=()):
        self._side_init()
        if q == "gpsimd":
            j = 10 + self._rr_sw
            self._rr_sw = (self._rr_sw + 1) % 6
        else:
            j = self._rr_hw
            self._rr_hw = (self._rr_hw + 1) % 10
        self._pool_cnt[j] += 16
        tok = (j, self._pool_cnt[j])
        if at is None:
            at = self.mark()
        ep_i, wt, wg = at
        pool = self._pool
        def fn(e, o=out, i=in_, tok_joins=tuple(joins)):
            for (jj, val) in tok_joins:
                e.wait_ge(pool[jj], val)
            return e.dma_start(out=o, in_=i)
        sem = self._pool[j]
        self.ops.append((len(self.epochs) - 1, q, _freeze(fn),
                         (None, None), "side", 16,
                         ("side", ep_i, wt, wg, sem)))
        return tok

    def side_compute(self, eng, fns, joins=(), at=None):
        self._side_init()
        j = 16 + self._rr_c
        self._rr_c = (self._rr_c + 1) % 8
        self._pool_cnt[j] += 1
        tok = (j, self._pool_cnt[j])
        if at is None:
            at = self.mark()
        ep_i, wt, wg = at
        pool = self._pool
        def fn(e, fns=tuple(fns), tok_joins=tuple(joins)):
            for (jj, val) in tok_joins:
                e.wait_ge(pool[jj], val)
            ins = None
            for f in fns:
                ins = f(e)
            return ins
        self.ops.append((len(self.epochs) - 1, eng, _freeze(fn),
                         (None, None), "side", 1,
                         ("side", ep_i, wt, wg, self._pool[j])))
        return tok

    def join(self, tok, eng):
        self._side_init()
        j, val = tok
        sem = self._pool[j]
        self.op(eng, lambda e, sem=sem, val=val: e.wait_ge(sem, val), inc=1)

    def join_many(self, toks, eng):
        self._side_init()
        pool = self._pool
        best = {}
        for (j, val) in toks:
            best[j] = max(best.get(j, 0), val)
        snap = tuple(best.items())
        def fn(e, snap=snap):
            ins = e.wait_ge(pool[0], 0)
            for j, val in snap:
                ins = e.wait_ge(pool[j], val)
            return ins
        self.op(eng, fn, inc=1)

    def side_fence(self, eng="vector"):
        self._side_init()
        snap = tuple((j, c) for j, c in enumerate(self._pool_cnt) if c)
        pool = self._pool
        def fn(e, snap=snap):
            ins = e.wait_ge(pool[0], 0)
            for j, c in snap:
                ins = e.wait_ge(pool[j], c)
            return ins
        self.op(eng, fn, inc=1)

    def finalize(self):
        nc = self.nc
        block = self.stack.enter_context(nc.Block())
        by_eng = {}
        for rec in self.ops:
            by_eng.setdefault(rec[1], []).append(rec)
        eps = self.epochs
        for engn, sub in by_eng.items():
            @getattr(block, engn)
            def _(eng, sub=sub):
                cur_ep = 0
                last_t, last_g = -1, -1
                pend = 0
                for (epi, _e, fn, (wt, wg), which, inc, role) in sub:
                    if which == "side":
                        _tag, sep_i, swt, swg, sem = role
                        if sep_i > 0:
                            prev = eps[sep_i - 1]
                            eng.wait_ge(prev["tick"], prev["cnt"])
                            if prev["gcnt"]:
                                eng.wait_ge(prev["gtick"], prev["gcnt"])
                        if swt > 0:
                            eng.wait_ge(eps[sep_i]["tick"], swt)
                        if swg > 0:
                            eng.wait_ge(eps[sep_i]["gtick"], swg)
                        fn(eng).then_inc(sem, inc)
                        continue
                    if epi != cur_ep:
                        prev = eps[epi - 1]
                        eng.wait_ge(prev["tick"], prev["cnt"])
                        if prev["gcnt"]:
                            eng.wait_ge(prev["gtick"], prev["gcnt"])
                        cur_ep = epi
                        last_t, last_g = -1, -1
                    ep = eps[epi]
                    if role in (None, "head"):
                        if wt > last_t:
                            eng.wait_ge(ep["tick"], wt)
                            last_t = wt
                        if wg > last_g and wg > 0:
                            eng.wait_ge(ep["gtick"], wg)
                            last_g = wg
                    ins = fn(eng)
                    if role in ("head", "mid"):
                        pend += inc
                    elif role == "end":
                        ins.then_inc(ep["tick"], pend + inc)
                        pend = 0
                    else:
                        ins.then_inc(ep["gtick"] if which == "g" else ep["tick"], inc)
        @block.sync
        def _(e):
            last = eps[-1]
            e.wait_ge(last["tick"], last["cnt"])
            if last["gcnt"]:
                e.wait_ge(last["gtick"], last["gcnt"])
            if hasattr(self, "_pool"):
                for j, c in enumerate(self._pool_cnt):
                    if c:
                        e.wait_ge(self._pool[j], c)
        self.stack.close()
        return self.nc



def _hilo(x64):
    hi = np.asarray(x64, np.float32)
    lo = np.asarray(x64 - hi.astype(np.float64), np.float32)
    return hi, lo


def host_constants():
    c = {}
    c["idn_bf"] = np.eye(128, dtype=BF)
    c["idn_f"] = np.eye(128, dtype=np.float32)
    q = np.arange(128)[:, None]
    k = np.arange(128)[None, :]
    c["tril_bf"] = np.where(k <= q, 0.0, NEG).astype(BF)
    c["triu_bf"] = np.where(k >= q, 0.0, NEG).astype(BF)
    half = np.arange(0, 64, 2, dtype=np.float64)
    inv_freq = 1.0 / 10000.0 ** (half / 64.0)
    t = np.arange(T, dtype=np.float64)
    fr = t[None, :] * inv_freq[:, None]
    freqs = np.concatenate([fr, fr], 0)
    base_scale = (half + 0.4 * 64.0) / (1.4 * 64.0)
    power = (t - T // 2) / 128.0
    sc = base_scale[:, None] ** power[None, :]
    scale = np.concatenate([sc, sc], 0)
    stack2 = lambda a: np.ascontiguousarray(np.concatenate([a, a], 0))
    for nm, arr in [("ctq", np.cos(freqs) * scale), ("stq", np.sin(freqs) * scale),
                    ("ctk", np.cos(freqs) / scale), ("stk", np.sin(freqs) / scale)]:
        hi, lo = _hilo(stack2(arr))
        c[nm + "_hi"], c[nm + "_lo"] = hi, lo
    r2t = np.zeros((128, 128), np.float64)
    for b in (0, 64):
        r2t[b + np.arange(32), b + np.arange(32) + 32] = 1.0
        r2t[b + np.arange(32) + 32, b + np.arange(32)] = -1.0
    c["r2t_bf"] = r2t.astype(BF)
    sel8 = np.zeros((8, 1024), np.float32)
    for e in range(8):
        sel8[e, e * 128:(e + 1) * 128] = 1.0
    c["sel8"] = sel8
    return c


def bias_panel(b):
    b = np.asarray(b, np.float32).ravel()
    n = len(b)
    pad = ((n + 127) // 128) * 128
    out = np.zeros(pad, np.float32)
    out[:n] = b
    return np.ascontiguousarray(out.reshape(-1, 128))


def host_prep(inputs):
    h = {}
    x = np.asarray(inputs["L0_attn_x"], np.float32)
    h["xfm"] = [np.ascontiguousarray(x[2 * i:2 * i + 2].reshape(NTOK, D).T)
                for i in range(NCORES)]
    h["W_attn"] = np.ascontiguousarray(inputs["L0_attn_W_attn"], np.float32)
    h["b_attn_p"] = bias_panel(inputs["L0_attn_b_attn"])
    h["W_proj"] = np.ascontiguousarray(inputs["L0_attn_W_proj"], np.float32)
    h["b_proj_p"] = bias_panel(inputs["L0_attn_b_proj"])
    for i in range(6):
        h[f"Wqkv{i}"] = np.ascontiguousarray(inputs["L0_ffn_Wqkv"][i], np.float32)
        h[f"Wo{i}"] = np.ascontiguousarray(inputs["L0_ffn_Wo"][i], np.float32)
        W1 = np.asarray(inputs["L0_ffn_W1"][i], np.float32)
        h[f"W1a{i}"] = np.ascontiguousarray(W1[:, :2730])
        h[f"W1g{i}"] = np.ascontiguousarray(W1[:, 2730:])
        h[f"W2{i}"] = np.ascontiguousarray(inputs["L0_ffn_W2"][i], np.float32)
        b1 = np.asarray(inputs["L0_ffn_b1"][i], np.float32)
        h[f"b1a_p{i}"] = bias_panel(b1[:2730])
        h[f"b1g_p{i}"] = bias_panel(b1[2730:])
        h[f"b2_p{i}"] = bias_panel(inputs["L0_ffn_b2"][i])
    qs = np.asarray(inputs["L0_ffn_q_scale"], np.float32)
    ks = np.asarray(inputs["L0_ffn_k_scale"], np.float32)
    h["qs2"] = np.ascontiguousarray(np.concatenate([qs.T, qs.T], 0))
    h["ks2"] = np.ascontiguousarray(np.concatenate([ks.T, ks.T], 0))
    qsr = np.roll(qs, 32, axis=1)
    ksr = np.roll(ks, 32, axis=1)
    h["qs2r"] = np.ascontiguousarray(np.concatenate([qsr.T, qsr.T], 0))
    h["ks2r"] = np.ascontiguousarray(np.concatenate([ksr.T, ksr.T], 0))
    h["ln_gT"] = np.ascontiguousarray(np.asarray(inputs["L0_ffn_ln_g"], np.float32).T)
    h["ln_bT"] = np.ascontiguousarray(np.asarray(inputs["L0_ffn_ln_b"], np.float32).T)
    h["W_dq"] = np.ascontiguousarray(inputs["L1_attn_W_dq"], np.float32)
    h["W_uq"] = np.ascontiguousarray(inputs["L1_attn_W_uq"], np.float32)
    h["W_dkvT"] = np.ascontiguousarray(np.asarray(inputs["L1_attn_W_dkv"], np.float32).T)
    h["W_uk"] = np.ascontiguousarray(inputs["L1_attn_W_uk"], np.float32)
    h["W_uv"] = np.ascontiguousarray(inputs["L1_attn_W_uv"], np.float32)
    h["W_oT"] = np.ascontiguousarray(np.asarray(inputs["L1_attn_W_o"], np.float32).T)
    h["wiT"] = np.ascontiguousarray(np.asarray(inputs["L1_ffn_wi"], np.float32).T)
    h["bi_p"] = bias_panel(inputs["L1_ffn_bi"])
    h["woT"] = np.ascontiguousarray(np.asarray(inputs["L1_ffn_wo"], np.float32).T)
    h["bo_p"] = bias_panel(inputs["L1_ffn_bo"])
    lA = np.asarray(inputs["L1_ffn_lora_A"], np.float32)
    lB = np.asarray(inputs["L1_ffn_lora_B"], np.float32)
    for g in range(4):
        ap = np.zeros((D, 64), np.float32)
        bp = np.zeros((64, 4096), np.float32)
        for j in range(2):
            e = 2 * g + j
            ap[:, 32 * j:32 * j + 16] = lA[e].T
            bp[32 * j:32 * j + 16, :] = lB[e].T
        h[f"Apack{g}"] = np.ascontiguousarray(ap)
        h[f"Bpack{g}"] = np.ascontiguousarray(bp)
    h["gate_wT"] = np.ascontiguousarray(np.asarray(inputs["L1_ffn_gate_w"], np.float32).T)
    h["gate_b"] = np.ascontiguousarray(np.asarray(inputs["L1_ffn_gate_b"], np.float32).reshape(1, 8))
    h.update(host_constants())
    return h



class Common:
    def __init__(self, pb, masks=False):
        self.pb = pb
        self.idn_bf = pb.sb(128, 128, BF16)
        pb.dma(self.idn_bf[:], pb.dram("idn_bf", [128, 128], BF16))
        self.idn_f = pb.sb(128, 128, F32)
        pb.dma(self.idn_f[:], pb.dram("idn_f", [128, 128], F32))
        self.ones_f = pb.sb(128, 1, F32)
        pb.op("vector", lambda e: e.memset(self.ones_f[:], 1.0))
        self.ones_row_f = pb.sb(65, 128, F32)
        pb.op("vector", lambda e: e.memset(self.ones_row_f[:], 1.0))
        self.ones_bf = pb.sb(128, 1, BF16)
        pb.op("vector", lambda e: e.memset(self.ones_bf[:], 1.0))
        self.ones_row_bf = pb.sb(1, 128, BF16)
        pb.op("vector", lambda e: e.memset(self.ones_row_bf[:], 1.0))
        self.consts = pb.sb(128, 4, F32)
        pb.op("vector", lambda e: e.memset(self.consts[:, 0:1], 1e-5))
        self.eps_ln = self.consts[:, 0:1]
        if masks:
            self.tril = pb.sb(128, 128, BF16)
            pb.dma(self.tril[:], pb.dram("tril_bf", [128, 128], BF16))
            self.triu = pb.sb(128, 128, BF16)
            pb.dma(self.triu[:], pb.dram("triu_bf", [128, 128], BF16))

    def panel(self, name, nch, scope=None):
        pb = self.pb
        d = pb.dram(name, [nch, 128], F32)
        out = (scope or pb).sb(128, nch, F32)
        with pb.scope() as sc:
            raw = sc.sb(nch, 128, F32)
            pb.dma(raw[:], d)
            pt = sc.ps(128, 128, F32)
            pb.op("tensor", lambda e: e.transpose(pt[0:128, 0:nch], raw[:], self.idn_f[0:nch, 0:nch]))
            pb.op("vector", lambda e: e.tensor_copy(out[:], pt[:, 0:nch]))
        return out


class CastLoader:
    def __init__(self, pb, scope, elems, eng="scalar", slots=2):
        self.pb = pb
        self.eng = eng
        self.stage = [scope.sb(128, elems, F32) for _ in range(slots)]
        self.prev = [None] * slots
        self.i = 0

    def load(self, dst_bf, src_f32, n, joins=(), view=None):
        pb = self.pb
        sl = self.i % len(self.stage)
        self.i += 1
        st = self.stage[sl]
        j1 = tuple(joins) + ((self.prev[sl],) if self.prev[sl] else ())
        dst_dma = view(st) if view else st[:, 0:n]
        t1 = pb.side_dma(dst_dma, src_f32, q="sync", joins=j1)
        np_ = dst_bf.partition_size()
        t2 = pb.side_compute(self.eng, [
            lambda e, st=st, dst_bf=dst_bf, n=n, np_=np_: e.activation(
                dst_bf, st[0:np_, 0:n], AF.Identity)], joins=(t1,))
        self.prev[sl] = t2
        return t2


def emit_ln(pb, cm, x_tiles, W, out_tiles, g_col=None, b_col=None):
    with pb.scope() as sc:
        s1 = sc.ps(1, W, F32)
        s2 = sc.ps(1, W, F32)
        sq = sc.sb(128, W, F32)
        pb.chain_begin()
        for c in range(NCH):
            pb.op("tensor", lambda e, c=c: e.matmul(
                s1[:], cm.ones_f[:], x_tiles[c][:, 0:W], start=(c == 0), stop=(c == NCH - 1)))
        pb.chain_end()
        for c in range(NCH):
            pb.op("scalar", lambda e, c=c: e.activation(
                sq[:], x_tiles[c][:, 0:W], AF.Square))
            pb.op("tensor", lambda e, c=c: e.matmul(
                s2[:], cm.ones_f[:], sq[:],
                start=(c == 0), stop=(c == NCH - 1)))
        rows = sc.sb(1, 3 * W, F32)
        mu = rows[0:1, 0:W]
        var = rows[0:1, W:2 * W]
        rstd = rows[0:1, 2 * W:3 * W]
        pb.op("vector", lambda e: e.tensor_scalar_mul(mu[:], s1[:], 1.0 / D))
        pb.op("vector", lambda e: e.tensor_scalar_mul(var[:], s2[:], 1.0 / D))
        pb.op("scalar", lambda e: e.activation(rstd[:], mu[:], AF.Square))
        pb.op("vector", lambda e: e.tensor_sub(var[:], var[:], rstd[:]))
        pb.op("scalar", lambda e: e.activation(rstd[:], var[:], AF.Ln, bias=cm.eps_ln[0:1, 0:1]))
        pb.op("scalar", lambda e: e.activation(rstd[:], rstd[:], AF.Exp, scale=-0.5))
        pb.op("vector", lambda e: e.tensor_mul(mu[:], mu[:], rstd[:]))
        pbc = sc.ps(128, 2 * W, F32)
        pb.op("tensor", lambda e: e.matmul(pbc[:, 0:W], cm.ones_row_f[0:1, :], rstd[:], start=True, stop=True))
        pb.op("tensor", lambda e: e.matmul(pbc[:, W:2 * W], cm.ones_row_f[0:1, :], mu[:], start=True, stop=True))
        tmp = sq
        for c in range(NCH):
            pb.op("vector", lambda e, c=c: e.tensor_mul(tmp[:], x_tiles[c][:, 0:W], pbc[:, 0:W]))
            if g_col is not None:
                pb.op("vector", lambda e: e.tensor_sub(tmp[:], tmp[:], pbc[:, W:2 * W]))
                pb.op("vector", lambda e, c=c: e.tensor_scalar(
                    out_tiles[c], tmp[:], g_col(c), b_col(c), ALU.mult, ALU.add))
            else:
                pb.op("vector", lambda e, c=c: e.tensor_sub(out_tiles[c], tmp[:], pbc[:, W:2 * W]))


def emit_proj(pb, cm, w_dram, n_in, n_out, x_sb, xw, cols, out_cb):
    nkc = (n_in + 127) // 128
    nmc = (n_out + 127) // 128
    with pb.scope() as sc:
        wcol = [sc.sb(128, nkc * 128, BF16) for _ in range(2)]
        pcol = sc.ps(128, 1024, F32)
        cl = CastLoader(pb, sc, nkc * 128)
        def wload(m):
            mw = min(128, n_out - m * 128)
            return cl.load(
                wcol[m % 2][:, 0:nkc * mw],
                w_dram[:, m * 128:m * 128 + mw].rearrange("(k p) m -> p k m", p=128),
                nkc * mw,
                view=lambda t, mw=mw: t[:, 0:nkc * mw].rearrange("p (k m) -> p k m", k=nkc))
        toks = {0: wload(0)}
        units = [(m, ci) for m in range(nmc) for ci in range(len(cols))]
        etoks = {}
        def emit_chain(u):
            m, ci = units[u]
            mw = min(128, n_out - m * 128)
            off, W = cols[ci]
            if ci == 0:
                if m + 1 < nmc:
                    toks[m + 1] = wload(m + 1)
                pb.join(toks[m], "tensor")
            if u - 2 in etoks:
                pb.join(etoks[u - 2], "tensor")
            wc = wcol[m % 2]
            par = (u % 2) * 512
            pb.chain_begin()
            for k in range(nkc):
                pb.op("tensor", lambda e, wc=wc, k=k, off=off, W=W, mw=mw, par=par: e.matmul(
                    pcol[0:mw, par:par + W], wc[:, k * mw:(k + 1) * mw],
                    x_sb[:, k * xw + off:k * xw + off + W],
                    start=(k == 0), stop=(k == nkc - 1)))
            pb.chain_end()
        emit_chain(0)
        for u, (m, ci) in enumerate(units):
            if u + 1 < len(units):
                emit_chain(u + 1)
            mw = min(128, n_out - m * 128)
            eng, fns = out_cb(m, ci, pcol[:, (u % 2) * 512:(u % 2) * 512 + 512], mw)
            etoks[u] = pb.side_compute(eng, fns)
        return list(etoks.values())


def emit_causal_softmax_pv(pb, cm, psq, Psb, Pt, pst, nmax, ssum, rcp,
                           nk, v_lhsT, ctx_out, scale=1.0):
    pb.op("vector", lambda e: e.reduce_max(nmax[:], psq[:, 0:nk], axis=AX.X, negate=True))
    if scale != 1.0:
        pb.op("vector", lambda e: e.tensor_scalar_mul(nmax[:], nmax[:], scale))
    pb.op("scalar", lambda e: e.activation(
        Psb[:, 0:nk], psq[:, 0:nk], AF.Exp, scale=scale, bias=nmax[:, 0:1], accum_out=ssum[:, 0:1]))
    pb.op("vector", lambda e: e.reciprocal(rcp[:], ssum[:]))
    pb.op("vector", lambda e: e.tensor_scalar_mul(Psb[:, 0:nk], Psb[:, 0:nk], rcp[:, 0:1]))
    nkc = nk // 128
    for g0 in range(0, nkc, 4):
        gn = min(4, nkc - g0)
        pb.chain_begin()
        for j in range(gn):
            pb.op("tensor", lambda e, j=j, g0=g0: e.transpose(
                pst[:, j * 128:(j + 1) * 128],
                Psb[:, (g0 + j) * 128:(g0 + j + 1) * 128], cm.idn_bf[:]))
        pb.chain_end()
        pb.op("vector", lambda e, g0=g0, gn=gn: e.tensor_copy(
            Pt[:, g0 * 128:(g0 + gn) * 128], pst[:, 0:gn * 128]))



def emit_l0attn(pb, cm, xfm, res1, loc0):
    W_attn = pb.dram("W_attn", [D, 3 * D], F32)
    W_proj = pb.dram("W_proj", [D, D], F32)
    b_attn_c = cm.panel("b_attn_p", 24)
    b_proj_c = cm.panel("b_proj_p", 8)
    pb.op("vector", lambda e: e.tensor_scalar_mul(
        b_attn_c[:, 0:NCH], b_attn_c[:, 0:NCH], 0.125))

    for s in range(NSEQ):
        toff = s * T
        with pb.scope() as sA:
            pb.side_fence()
            qsb = sA.sb(128, NCH * T, BF16)
            ksb = sA.sb(128, NCH * T, BF16)
            vsb = sA.sb(128, NCH * T, BF16)
            ctx = sA.sb(128, NCH * T, BF16)
            with pb.scope() as sX:
                xbf = sX.sb(128, NCH * T, BF16)
                with pb.scope() as s1:
                    xt2 = [s1.sb(128, NCH * CW, F32) for _ in range(2)]
                    xfmr = xfm.rearrange("(c p) t -> p c t", p=128)
                    def xload(col):
                        t0 = toff + col * CW
                        return pb.side_dma(
                            xt2[col % 2][:].rearrange("p (c w) -> p c w", c=NCH),
                            xfmr[:, :, t0:t0 + CW])
                    toks = {0: xload(0)}
                    for col in range(T // CW):
                        if col + 1 < T // CW:
                            toks[col + 1] = xload(col + 1)
                        pb.join(toks[col], "tensor")
                        xa = xt2[col % 2]
                        xt = [xa[:, c * CW:(c + 1) * CW] for c in range(NCH)]
                        outs = [xbf[:, c * T + col * CW:c * T + col * CW + CW]
                                for c in range(NCH)]
                        emit_ln(pb, cm, xt, CW, outs)
                cols = [(col * CW, CW) for col in range(T // CW)]
                def qkv_cb(m, ci, ps, mw):
                    tgt = (qsb, ksb, vsb)[m // NCH]
                    c = m % NCH
                    off = c * T + ci * CW
                    sc_ = 0.125 if m < NCH else 1.0
                    return ("scalar", [lambda e, m=m, tgt=tgt, off=off, ps=ps, sc_=sc_:
                            e.activation(tgt[:, off:off + CW], ps[:, 0:CW], AF.Identity,
                                         scale=sc_, bias=b_attn_c[:, m:m + 1])])
                qtoks = emit_proj(pb, cm, W_attn, D, 3 * D, xbf, T, cols, qkv_cb)
                pb.join_many(qtoks, "tensor")
            with pb.scope() as s3:
                vtm = s3.sb(128, 16 * 64, BF16)
                psq = s3.ps(128, 2048, F32)
                pst = s3.ps(128, 1024, BF16)
                psv = s3.ps(128, 512, F32)
                Pt = s3.sb(128, 2048, BF16)
                Psb2 = [s3.sb(128, 2048, BF16) for _ in range(2)]
                nmax = s3.sb(128, 2, F32)
                ssum = s3.sb(128, 2, F32)
                rcp = s3.sb(128, 2, F32)
                for h in range(16):
                    c, p = h // 2, (h % 2) * 64
                    for g0 in range(0, 16, 4):
                        pb.chain_begin()
                        for j in range(4):
                            pb.op("tensor", lambda e, j=j, g0=g0, c=c, p=p: e.transpose(
                                pst[:, j * 64:(j + 1) * 64],
                                vsb[p:p + 64, c * T + (g0 + j) * 128:c * T + (g0 + j + 1) * 128],
                                cm.idn_bf[p:p + 64, p:p + 64]))
                        pb.chain_end()
                        pb.op("vector", lambda e, g0=g0: e.tensor_copy(
                            vtm[:, g0 * 64:(g0 + 4) * 64], pst[:, 0:256]))
                    smt = {}
                    def emit_qk(qi, c=c, p=p):
                        nk = 128 * (qi + 1)
                        q0 = qi * 128
                        if qi - 1 in smt:
                            pb.join(smt[qi - 1][0], "tensor")
                        pb.chain_begin()
                        pb.op("tensor", lambda e, q0=q0: e.matmul(
                            psq[:, q0:q0 + 128], cm.idn_bf[:], cm.tril[:],
                            start=True, stop=False, skip_group_check=True))
                        for ks_ in range(0, nk, 512):
                            w = min(512, nk - ks_)
                            diag = (ks_ + w == nk)
                            segs = [(ks_, w, True)] if not diag else (
                                ([(ks_, q0 - ks_, False)] if q0 > ks_ else []) + [(q0, 128, False)])
                            for (s0, sw, st_) in segs:
                                pb.op("tensor", lambda e, s0=s0, sw=sw, st_=st_, q0=q0, c=c, p=p: e.matmul(
                                    psq[:, s0:s0 + sw], qsb[p:p + 64, c * T + q0:c * T + q0 + 128],
                                    ksb[p:p + 64, c * T + s0:c * T + s0 + sw],
                                    start=st_, stop=True, skip_group_check=True))
                        pb.chain_end()
                    def emit_smax(qi):
                        nk = 128 * (qi + 1)
                        sl = qi % 2
                        Ps = Psb2[sl]
                        jm = (smt[qi - 2][0],) if qi - 2 in smt else ()
                        t_max = pb.side_compute("vector", [
                            lambda e, nk=nk, sl=sl: e.reduce_max(
                                nmax[:, sl:sl + 1], psq[:, 0:nk], axis=AX.X, negate=True)],
                            joins=jm)
                        jm2 = (smt[qi - 2][3],) if qi - 2 in smt else ()
                        t_exp = pb.side_compute("scalar", [
                            lambda e, nk=nk, sl=sl, Ps=Ps: e.activation(
                                Ps[:, 0:nk], psq[:, 0:nk], AF.Exp,
                                bias=nmax[:, sl:sl + 1], accum_out=ssum[:, sl:sl + 1])],
                            joins=(t_max,) + jm2)
                        jm3 = (smt[qi - 2][3],) if qi - 2 in smt else ()
                        t_rcp = pb.side_compute("vector", [
                            lambda e, sl=sl: e.reciprocal(rcp[:, sl:sl + 1], ssum[:, sl:sl + 1])],
                            joins=(t_exp,) + jm3)
                        t_ts = pb.side_compute("vector", [
                            lambda e, nk=nk, sl=sl, Ps=Ps: e.tensor_scalar_mul(
                                Ps[:, 0:nk], Ps[:, 0:nk], rcp[:, sl:sl + 1])],
                            joins=(t_rcp,))
                        smt[qi] = (t_exp, t_max, t_rcp, t_ts)
                    def emit_ptpv(qi, c=c, p=p):
                        nk = 128 * (qi + 1)
                        q0 = qi * 128
                        sl = qi % 2
                        Ps = Psb2[sl]
                        pb.join(smt[qi][3], "tensor")
                        nkc = qi + 1
                        for g0_ in range(0, nkc, 4):
                            gn = min(4, nkc - g0_)
                            pb.chain_begin()
                            for j in range(gn):
                                pb.op("tensor", lambda e, j=j, g0_=g0_, Ps=Ps: e.transpose(
                                    pst[:, j * 128:(j + 1) * 128],
                                    Ps[:, (g0_ + j) * 128:(g0_ + j + 1) * 128], cm.idn_bf[:]))
                            pb.chain_end()
                            pb.op("vector", lambda e, g0_=g0_, gn=gn: e.tensor_copy(
                                Pt[:, g0_ * 128:(g0_ + gn) * 128], pst[:, 0:gn * 128]))
                        pb.chain_begin()
                        for kc in range(nkc):
                            pb.op("tensor", lambda e, kc=kc, nkc=nkc: e.matmul(
                                psv[0:64, 0:128], vtm[:, kc * 64:kc * 64 + 64],
                                Pt[:, kc * 128:kc * 128 + 128],
                                start=(kc == 0), stop=(kc == nkc - 1)))
                        pb.chain_end()
                        pb.op("vector", lambda e, q0=q0, c=c, p=p: e.tensor_copy(
                            ctx[p:p + 64, c * T + q0:c * T + q0 + 128], psv[0:64, 0:128]))
                    emit_qk(0)
                    for qi in range(16):
                        emit_smax(qi)
                        if qi + 1 < 16:
                            emit_qk(qi + 1)
                        emit_ptpv(qi)
            with pb.scope() as s4:
                rt_all = [s4.sb(128, NCH * CW, F32) for _ in range(2)]
                ln_all = s4.sb(128, NCH * CW, F32)
                wsb = s4.sb(128, NCH * NCH * 128, BF16)
                pcol = s4.ps(128, 512, F32)
                cl4 = CastLoader(pb, s4, 1024, slots=1)
                wq = []
                for hh in range(8):
                    wq.append(cl4.load(
                        wsb[:, hh * 1024:(hh + 1) * 1024],
                        W_proj[:, hh * 128:(hh + 1) * 128].rearrange("(k p) m -> p k m", p=128),
                        1024,
                        view=lambda t: t[:, 0:1024].rearrange("p (k m) -> p k m", k=NCH)))
                pb.join_many(wq, "tensor")
                xfmr = xfm.rearrange("(c p) t -> p c t", p=128)
                res1r = res1.rearrange("(c p) t -> p c t", p=128)
                loc0r = loc0.rearrange("(c p) t -> p c t", p=128)
                def rload(col, joins):
                    t0 = toff + col * CW
                    return pb.side_dma(
                        rt_all[col % 2][:].rearrange("p (c w) -> p c w", c=NCH),
                        xfmr[:, :, t0:t0 + CW], joins=joins)
                rtoks = {0: rload(0, ())}
                stoks = {}
                ltok = [None]
                for col in range(T // CW):
                    t0 = toff + col * CW
                    if col + 1 < T // CW:
                        rtoks[col + 1] = rload(col + 1, (stoks[col - 1],) if col >= 1 else ())
                    ra = rt_all[col % 2]
                    pb.join(rtoks[col], "vector")
                    for m in range(NCH):
                        pb.chain_begin()
                        for k in range(NCH):
                            woff = m * 1024 + k * 128
                            pb.op("tensor", lambda e, k=k, m=m, col=col, woff=woff: e.matmul(
                                pcol[:, 0:CW], wsb[:, woff:woff + 128],
                                ctx[:, k * T + col * CW:k * T + col * CW + CW],
                                start=(k == 0), stop=(k == NCH - 1)))
                        pb.chain_end()
                        pb.op("scalar", lambda e, m=m: e.activation(
                            pcol[:, 0:CW], pcol[:, 0:CW], AF.Identity,
                            bias=b_proj_c[:, m:m + 1]))
                        pb.op("vector", lambda e, m=m, ra=ra: e.tensor_add(
                            ra[:, m * CW:(m + 1) * CW], ra[:, m * CW:(m + 1) * CW],
                            pcol[:, 0:CW]))
                    stoks[col] = pb.side_dma(res1r[:, :, t0:t0 + CW],
                                             ra[:].rearrange("p (c w) -> p c w", c=NCH))
                    if ltok[0] is not None:
                        pb.join(ltok[0], "vector")
                    xt = [ra[:, c * CW:(c + 1) * CW] for c in range(NCH)]
                    outs = [ln_all[:, c * CW:(c + 1) * CW] for c in range(NCH)]
                    emit_ln(pb, cm, xt, CW, outs)
                    ltok[0] = pb.side_dma(loc0r[:, :, t0:t0 + CW],
                                          ln_all[:].rearrange("p (c w) -> p c w", c=NCH))


def build_l0attn():
    pb = PB("l0attn")
    cm = Common(pb, masks=True)
    xfm = pb.dram("xfm", [D, NTOK], F32)
    res1 = pb.dram("res1", [D, NTOK], F32, kind="ExternalOutput")
    loc0 = pb.dram("loc0", [D, NTOK], F32, kind="ExternalOutput")
    emit_l0attn(pb, cm, xfm, res1, loc0)
    return pb



def emit_l0ffn_layer(pb, cm, li, loc_in, loc_mid, m_prod, loc_out):
    sfx = str(li)
    Wqkv = pb.dram("Wqkv" + sfx, [D, 3 * D], F32)
    Wo = pb.dram("Wo" + sfx, [D, D], F32)
    W1a = pb.dram("W1a" + sfx, [D, 2730], F32)
    W1g = pb.dram("W1g" + sfx, [D, 2730], F32)
    W2d = pb.dram("W2" + sfx, [2730, D], F32)
    L = _Scope(pb)
    L.__enter__()
    pb.side_fence()
    qs2 = L.sb(128, 6, F32)
    pb.dma(qs2[:], pb.dram("qs2", [128, 6], F32))
    ks2 = L.sb(128, 6, F32)
    pb.dma(ks2[:], pb.dram("ks2", [128, 6], F32))
    qs2r = L.sb(128, 6, F32)
    pb.dma(qs2r[:], pb.dram("qs2r", [128, 6], F32))
    ks2r = L.sb(128, 6, F32)
    pb.dma(ks2r[:], pb.dram("ks2r", [128, 6], F32))
    ln_g = L.sb(128, NCH * 6, F32)
    pb.dma(ln_g[:].rearrange("p (c l) -> p c l", c=NCH),
           pb.dram("ln_gT", [D, 6], F32).rearrange("(c p) l -> p c l", p=128))
    ln_b = L.sb(128, NCH * 6, F32)
    pb.dma(ln_b[:].rearrange("p (c l) -> p c l", c=NCH),
           pb.dram("ln_bT", [D, 6], F32).rearrange("(c p) l -> p c l", p=128))
    b1a_c = cm.panel("b1a_p" + sfx, 22, scope=L)
    b1g_c = cm.panel("b1g_p" + sfx, 22, scope=L)
    b2_c = cm.panel("b2_p" + sfx, 8, scope=L)
    r2t = L.sb(128, 128, BF16)
    pb.dma(r2t[:], pb.dram("r2t_bf", [128, 128], BF16))

    qs_col = qs2[:, li:li + 1]
    ks_col = ks2[:, li:li + 1]
    qsr_col = qs2r[:, li:li + 1]
    ksr_col = ks2r[:, li:li + 1]
    g_cols = ln_g
    b_cols = ln_b
    tabs = {}
    for nm in ("ctq", "stq", "ctk", "stk"):
        tabs[nm] = L.sb(128, T, BF16)
    with pb.scope() as sc:
        base = sc.sb(128, T, F32)
        hilo = [sc.sb(128, 2 * T, F32) for _ in range(2)]
        names = [("ctq", qs_col, True), ("stq", qsr_col, True),
                 ("ctk", ks_col, False), ("stk", ksr_col, False)]
        def tload(i):
            nm = names[i][0]
            b = hilo[i % 2]
            t1 = pb.side_dma(b[:, 0:T], pb.dram(nm + "_hi", [128, T], F32))
            t2 = pb.side_dma(b[:, T:2 * T], pb.dram(nm + "_lo", [128, T], F32))
            return (t1, t2)
        toks = {0: tload(0)}
        for i, (nm, scol, fold8) in enumerate(names):
            if i + 1 < 4:
                toks[i + 1] = tload(i + 1)
            pb.join_many(toks[i], "vector")
            b = hilo[i % 2]
            pb.op("vector", lambda e, b=b: e.tensor_add(base[:], b[:, 0:T], b[:, T:2 * T]))
            pb.op("vector", lambda e, nm=nm, scol=scol, f8=fold8: e.tensor_scalar(
                tabs[nm][:], base[:], scol[:, 0:1], 8.0 if f8 else None,
                ALU.mult, ALU.mult if f8 else ALU.bypass))

    for s in range(NSEQ):
        toff = s * T
        with pb.scope() as sA:
            pb.side_fence()
            qsb = sA.sb(128, NCH * T, BF16)
            ksb = sA.sb(128, NCH * T, BF16)
            vsb = sA.sb(128, NCH * T, BF16)
            ctx = sA.sb(128, NCH * T, BF16)
            with pb.scope() as sX:
                xbf = sX.sb(128, NCH * T, BF16)
                with pb.scope() as sxl:
                    XW = 256
                    xst = [sxl.sb(128, NCH * XW, F32) for _ in range(2)]
                    locr_ = loc_in.rearrange("(c p) t -> p c t", p=128)
                    casts = {}
                    def xld(col):
                        t0 = toff + col * XW
                        b = xst[col % 2]
                        br = b[:].rearrange("p (c w) -> p c w", c=NCH)
                        jj = (casts[col - 2],) if col - 2 in casts else ()
                        t1 = pb.side_dma(br[:, 0:4, :], locr_[:, 0:4, t0:t0 + XW],
                                         joins=jj)
                        if col > 0:
                            t2 = pb.side_dma(br[:, 4:8, :],
                                             locr_[:, 4:8, t0 - 1:t0 - 1 + XW])
                        else:
                            for c in range(4, NCH):
                                pb.op("vector", lambda e, c=c, b=b: e.memset(
                                    b[:, c * XW:c * XW + 1], 0.0))
                            t2 = pb.side_dma(br[:, 4:8, 1:XW],
                                             locr_[:, 4:8, t0:t0 + XW - 1])
                        return (t1, t2)
                    toks = {0: xld(0)}
                    for col in range(T // XW):
                        if col + 1 < T // XW:
                            toks[col + 1] = xld(col + 1)
                        b = xst[col % 2]
                        casts[col] = pb.side_compute("scalar", [
                            lambda e, b=b, col=col, c=c: e.activation(
                                xbf[:, c * T + col * XW:c * T + col * XW + XW],
                                b[:, c * XW:(c + 1) * XW], AF.Identity)
                            for c in range(NCH)], joins=toks[col])
                    pb.join_many(list(casts.values()), "tensor")
                cols = [(col * CW, CW) for col in range(T // CW)]
                def qkv_cb(m, ci, ps, mw):
                    tgt = (qsb, ksb, vsb)[m // NCH]
                    c = m % NCH
                    off = c * T + ci * CW
                    return ("scalar", [lambda e, tgt=tgt, off=off, ps=ps: e.activation(
                        tgt[:, off:off + CW], ps[:, 0:CW], AF.Identity)])
                qtoks = emit_proj(pb, cm, Wqkv, D, 3 * D, xbf, T, cols, qkv_cb)
                pb.join_many(qtoks, "tensor")
            with pb.scope() as s3:
                vtm = s3.sb(128, 16 * 64, BF16)
                psc = s3.ps(128, 2048, F32)
                pst = s3.ps(128, 1024, BF16)
                psv = s3.ps(128, 512, F32)
                psr = s3.ps(1, 512, F32)
                rbps = s3.ps(128, 512, F32)
                rot = rbps
                ssr = s3.sb(1, T, F32)
                rcpr = s3.sb(1, T, BF16)
                sqb = s3.sb(128, T, BF16)
                t1 = sqb
                t2 = s3.sb(128, T, BF16)
                Pt = s3.sb(128, 1536, BF16)
                Psb2 = [s3.sb(128, 1536, BF16) for _ in range(2)]
                nmax = s3.sb(128, 8, F32)
                ssum = s3.sb(128, 8, F32)
                rcp = s3.sb(128, 8, F32)
                for h in range(16):
                    c, p = h // 2, (h % 2) * 64
                    for tsb, ct_, st_ in [(qsb, tabs["ctq"], tabs["stq"]),
                                          (ksb, tabs["ctk"], tabs["stk"])]:
                        th = tsb[p:p + 64, c * T:(c + 1) * T]
                        pb.op("scalar", lambda e, th=th, p=p: e.activation(
                            sqb[p:p + 64, 0:T], th, AF.Square))
                        for w in range(T // CW):
                            pb.op("tensor", lambda e, w=w, p=p: e.matmul(
                                psr[:, 0:CW], cm.ones_bf[p:p + 64, 0:1],
                                sqb[p:p + 64, w * CW:(w + 1) * CW], start=True, stop=True))
                            pb.op("vector", lambda e, w=w: e.tensor_copy(
                                ssr[:, w * CW:(w + 1) * CW], psr[:, 0:CW]))
                        pb.op("scalar", lambda e: e.activation(ssr[:], ssr[:], AF.Ln))
                        pb.op("scalar", lambda e: e.activation(ssr[:], ssr[:], AF.Exp, scale=0.5))
                        pb.op("vector", lambda e: e.tensor_scalar_max(ssr[:], ssr[:], 1e-12))
                        pb.op("vector", lambda e: e.reciprocal(ssr[:], ssr[:]))
                        pb.op("vector", lambda e: e.tensor_copy(rcpr[:], ssr[:]))
                        for w in range(T // CW):
                            pb.op("tensor", lambda e, w=w, p=p: e.matmul(
                                rbps[p:p + 64, 0:CW], cm.ones_row_bf[0:1, p:p + 64],
                                rcpr[:, w * CW:(w + 1) * CW], start=True, stop=True))
                            pb.op("vector", lambda e, w=w, th=th, p=p: e.tensor_mul(
                                th[:, w * CW:(w + 1) * CW],
                                th[:, w * CW:(w + 1) * CW], rbps[p:p + 64, 0:CW]))
                        pb.op("vector", lambda e, th=th, p=p, ct_=ct_: e.tensor_mul(
                            t1[p:p + 64, 0:T], th, ct_[p:p + 64, :]))
                        for w in range(T // CW):
                            pb.op("tensor", lambda e, w=w, th=th, p=p: e.matmul(
                                rot[p:p + 64, 0:CW], r2t[p:p + 64, p:p + 64],
                                th[:, w * CW:(w + 1) * CW], start=True, stop=True))
                            pb.op("vector", lambda e, w=w, st_=st_, p=p: e.tensor_mul(
                                t2[p:p + 64, w * CW:(w + 1) * CW], rot[p:p + 64, 0:CW],
                                st_[p:p + 64, w * CW:(w + 1) * CW]))
                        pb.op("vector", lambda e, th=th, p=p: e.tensor_add(
                            th, t1[p:p + 64, 0:T], t2[p:p + 64, 0:T]))
                    for g0 in range(0, 16, 4):
                        pb.chain_begin()
                        for j in range(4):
                            pb.op("tensor", lambda e, j=j, g0=g0, c=c, p=p: e.transpose(
                                pst[:, j * 64:(j + 1) * 64],
                                vsb[p:p + 64, c * T + (g0 + j) * 128:c * T + (g0 + j + 1) * 128],
                                cm.idn_bf[p:p + 64, p:p + 64]))
                        pb.chain_end()
                        pb.op("vector", lambda e, g0=g0: e.tensor_copy(
                            vtm[:, g0 * 64:(g0 + 4) * 64], pst[:, 0:256]))
                    def ginfo(g):
                        info = []
                        for u in range(4):
                            qi = 4 * g + u
                            kb = 128 * (qi - 2)
                            blocks = [(b, m_) for (b, m_) in
                                      [(kb, "triu"), (kb + 128, None), (kb + 256, "tril")]
                                      if b >= 0]
                            info.append((qi, blocks))
                        off = [0]
                        for u, (qi, blocks) in enumerate(info):
                            off.append(off[-1] + len(blocks))
                        return info, off
                    gsm = {}
                    def emit_qk_g(g, c=c, p=p):
                        info, off = ginfo(g)
                        if g - 1 in gsm:
                            pb.join(gsm[g - 1][0], "tensor")
                        pb.chain_begin()
                        for u, (qi, blocks) in enumerate(info):
                            q0 = qi * 128
                            k0 = blocks[0][0]
                            masked = [(j, m_) for j, (b, m_) in enumerate(blocks) if m_]
                            for mi, (j, m_) in enumerate(masked):
                                mk = cm.triu if m_ == "triu" else cm.tril
                                pb.op("tensor", lambda e, u=u, j=j, mk=mk, first=(mi == 0): e.matmul(
                                    psc[:, u * 512 + j * 128:u * 512 + (j + 1) * 128],
                                    cm.idn_bf[:], mk[:],
                                    start=first, stop=False, skip_group_check=True))
                            for j in range(len(blocks)):
                                pb.op("tensor", lambda e, u=u, j=j, q0=q0, k0=k0, c=c, p=p: e.matmul(
                                    psc[:, u * 512 + j * 128:u * 512 + (j + 1) * 128],
                                    qsb[p:p + 64, c * T + q0:c * T + q0 + 128],
                                    ksb[p:p + 64, c * T + k0 + j * 128:c * T + k0 + (j + 1) * 128],
                                    start=False, stop=True, skip_group_check=True))
                        pb.chain_end()
                    def emit_smax_g(g):
                        info, off = ginfo(g)
                        sl = g % 2
                        Ps = Psb2[sl]
                        fns = []
                        for u, (qi, blocks) in enumerate(info):
                            kw = 128 * len(blocks)
                            fns.append(lambda e, u=u, kw=kw, sl=sl: e.reduce_max(
                                nmax[:, sl * 4 + u:sl * 4 + u + 1],
                                psc[:, u * 512:u * 512 + kw], axis=AX.X, negate=True))
                        jm = (gsm[g - 2][0],) if g - 2 in gsm else ()
                        t_max = pb.side_compute("vector", fns, joins=jm)
                        fns = []
                        for u, (qi, blocks) in enumerate(info):
                            kw = 128 * len(blocks)
                            fns.append(lambda e, u=u, kw=kw, o=off[u], sl=sl, Ps=Ps: e.activation(
                                Ps[:, o * 128:o * 128 + kw], psc[:, u * 512:u * 512 + kw],
                                AF.Exp, bias=nmax[:, sl * 4 + u:sl * 4 + u + 1],
                                accum_out=ssum[:, sl * 4 + u:sl * 4 + u + 1]))
                        jm2 = (gsm[g - 2][3],) if g - 2 in gsm else ()
                        t_exp = pb.side_compute("scalar", fns, joins=(t_max,) + jm2)
                        t_rcp = pb.side_compute("vector", [
                            lambda e, sl=sl: e.reciprocal(rcp[:, sl * 4:sl * 4 + 4],
                                                          ssum[:, sl * 4:sl * 4 + 4])],
                            joins=(t_exp,) + jm2)
                        fns = []
                        for u, (qi, blocks) in enumerate(info):
                            kw = 128 * len(blocks)
                            fns.append(lambda e, u=u, kw=kw, o=off[u], sl=sl, Ps=Ps: e.tensor_scalar_mul(
                                Ps[:, o * 128:o * 128 + kw], Ps[:, o * 128:o * 128 + kw],
                                rcp[:, sl * 4 + u:sl * 4 + u + 1]))
                        t_ts = pb.side_compute("vector", fns, joins=(t_rcp,))
                        gsm[g] = (t_exp, t_max, t_rcp, t_ts)
                    def emit_ptpv_g(g, c=c, p=p):
                        info, off = ginfo(g)
                        sl = g % 2
                        Ps = Psb2[sl]
                        pb.join(gsm[g][3], "tensor")
                        nblk = off[-1]
                        for b0 in range(0, nblk, 8):
                            bn = min(8, nblk - b0)
                            pb.chain_begin()
                            for j in range(bn):
                                pb.op("tensor", lambda e, j=j, b0=b0, Ps=Ps: e.transpose(
                                    pst[:, j * 128:(j + 1) * 128],
                                    Ps[:, (b0 + j) * 128:(b0 + j + 1) * 128],
                                    cm.idn_bf[:]))
                            pb.chain_end()
                            pb.op("vector", lambda e, b0=b0, bn=bn: e.tensor_copy(
                                Pt[:, b0 * 128:(b0 + bn) * 128], pst[:, 0:bn * 128]))
                        pb.chain_begin()
                        for u, (qi, blocks) in enumerate(info):
                            for j, (b, m_) in enumerate(blocks):
                                kc = b // 128
                                pb.op("tensor", lambda e, u=u, j=j, kc=kc, o=off[u],
                                      nb=len(blocks): e.matmul(
                                    psv[0:64, u * 128:(u + 1) * 128],
                                    vtm[:, kc * 64:kc * 64 + 64],
                                    Pt[:, (o + j) * 128:(o + j + 1) * 128],
                                    start=(j == 0), stop=(j == nb - 1)))
                        pb.chain_end()
                        pb.op("vector", lambda e, g=g, c=c, p=p: e.tensor_copy(
                            ctx[p:p + 64, c * T + g * 512:c * T + (g + 1) * 512],
                            psv[0:64, 0:512]))
                    emit_qk_g(0)
                    for g in range(4):
                        emit_smax_g(g)
                        if g + 1 < 4:
                            emit_qk_g(g + 1)
                        emit_ptpv_g(g)
            with pb.scope() as s4:
                rt_all = [s4.sb(128, NCH * CW, F32) for _ in range(2)]
                wsb = s4.sb(128, NCH * NCH * 128, BF16)
                pcol = s4.ps(128, 512, F32)
                cl4 = CastLoader(pb, s4, 1024, slots=1)
                wq = []
                for hh in range(8):
                    wq.append(cl4.load(
                        wsb[:, hh * 1024:(hh + 1) * 1024],
                        Wo[:, hh * 128:(hh + 1) * 128].rearrange("(k p) m -> p k m", p=128),
                        1024,
                        view=lambda t: t[:, 0:1024].rearrange("p (k m) -> p k m", k=NCH)))
                pb.join_many(wq, "tensor")
                locr = loc_in.rearrange("(c p) t -> p c t", p=128)
                lmidr = loc_mid.rearrange("(c p) t -> p c t", p=128)
                def rload(col, joins):
                    t0 = toff + col * CW
                    return pb.side_dma(
                        rt_all[col % 2][:].rearrange("p (c w) -> p c w", c=NCH),
                        locr[:, :, t0:t0 + CW], joins=joins)
                rtoks = {0: rload(0, ())}
                stoks = {}
                for col in range(T // CW):
                    t0 = toff + col * CW
                    if col + 1 < T // CW:
                        rtoks[col + 1] = rload(col + 1, (stoks[col - 1],) if col >= 1 else ())
                    ra = rt_all[col % 2]
                    pb.join(rtoks[col], "vector")
                    for m in range(NCH):
                        pb.chain_begin()
                        for k in range(NCH):
                            woff = m * 1024 + k * 128
                            pb.op("tensor", lambda e, k=k, m=m, col=col, woff=woff: e.matmul(
                                pcol[:, 0:CW], wsb[:, woff:woff + 128],
                                ctx[:, k * T + col * CW:k * T + col * CW + CW],
                                start=(k == 0), stop=(k == NCH - 1)))
                        pb.chain_end()
                        pb.op("vector", lambda e, m=m, ra=ra: e.tensor_add(
                            ra[:, m * CW:(m + 1) * CW], ra[:, m * CW:(m + 1) * CW],
                            pcol[:, 0:CW]))
                    stoks[col] = pb.side_dma(lmidr[:, :, t0:t0 + CW],
                                             ra[:].rearrange("p (c w) -> p c w", c=NCH))
        with pb.scope() as sB:
            xbf2 = sB.sb(128, NCH * T, BF16)
            pb.side_fence()
            with pb.scope() as s5:
                xt2 = [s5.sb(128, NCH * CW, F32) for _ in range(2)]
                lmidr = loc_mid.rearrange("(c p) t -> p c t", p=128)
                def sload(col):
                    t0 = toff + col * CW
                    b = xt2[col % 2]
                    t1 = pb.side_dma(b[:].rearrange("p (c w) -> p c w", c=NCH)[:, 0:4, :],
                                     lmidr[:, 0:4, t0:t0 + CW])
                    if col > 0:
                        t2 = pb.side_dma(b[:].rearrange("p (c w) -> p c w", c=NCH)[:, 4:8, :],
                                         lmidr[:, 4:8, t0 - 1:t0 - 1 + CW])
                    else:
                        for c in range(4, NCH):
                            pb.op("vector", lambda e, c=c, b=b: e.memset(
                                b[:, c * CW:c * CW + 1], 0.0))
                        t2 = pb.side_dma(
                            b[:].rearrange("p (c w) -> p c w", c=NCH)[:, 4:8, 1:CW],
                            lmidr[:, 4:8, t0:t0 + CW - 1])
                    return (t1, t2)
                toks = {0: sload(0)}
                for col in range(T // CW):
                    if col + 1 < T // CW:
                        toks[col + 1] = sload(col + 1)
                    pb.join_many(toks[col], "tensor")
                    xa = xt2[col % 2]
                    xt = [xa[:, c * CW:(c + 1) * CW] for c in range(NCH)]
                    outs = [xbf2[:, c * T + col * CW:c * T + col * CW + CW]
                            for c in range(NCH)]
                    emit_ln(pb, cm, xt, CW, outs,
                            g_col=lambda cc: g_cols[:, cc * 6 + li:cc * 6 + li + 1],
                            b_col=lambda cc: b_cols[:, cc * 6 + li:cc * 6 + li + 1])
            with pb.scope() as s6:
                wca = [s6.sb(128, NCH * 128, BF16) for _ in range(2)]
                wcg = [s6.sb(128, NCH * 128, BF16) for _ in range(2)]
                mtl2 = [s6.sb(128, T, BF16) for _ in range(2)]
                psa = s6.ps(128, 1024, F32)
                psg = s6.ps(128, 1024, F32)
                gel = s6.sb(128, 1024, BF16)
                asb = s6.sb(128, 1024, BF16)
                cla = CastLoader(pb, s6, NCH * 128)
                clg = CastLoader(pb, s6, NCH * 128)
                def w1load(m):
                    mw = min(128, 2730 - m * 128)
                    t1 = cla.load(
                        wca[m % 2][:, 0:NCH * mw],
                        W1a[:, m * 128:m * 128 + mw].rearrange("(k p) m -> p k m", p=128),
                        NCH * mw,
                        view=lambda t, mw=mw: t[:, 0:NCH * mw].rearrange("p (k m) -> p k m", k=NCH))
                    t2 = clg.load(
                        wcg[m % 2][:, 0:NCH * mw],
                        W1g[:, m * 128:m * 128 + mw].rearrange("(k p) m -> p k m", p=128),
                        NCH * mw,
                        view=lambda t, mw=mw: t[:, 0:NCH * mw].rearrange("p (k m) -> p k m", k=NCH))
                    return (t1, t2)
                wtoks = {0: w1load(0)}
                mstoks = {}
                atoks = {}
                mtoks_ = {}
                ncol = T // CW
                units = [(m, col) for m in range(22) for col in range(ncol)]
                def emit_chain(u):
                    m, col = units[u]
                    mw = min(128, 2730 - m * 128)
                    if col == 0:
                        if m + 1 < 22:
                            wtoks[m + 1] = w1load(m + 1)
                        pb.join_many(wtoks[m], "tensor")
                    if u - 2 in atoks:
                        pb.join(atoks[u - 2], "tensor")
                    wcam, wcgm = wca[m % 2], wcg[m % 2]
                    par = (u % 2) * 512
                    pb.chain_begin()
                    for k in range(NCH):
                        pb.op("tensor", lambda e, k=k, wcam=wcam, mw=mw, col=col, par=par: e.matmul(
                            psa[0:mw, par:par + CW], wcam[:, k * mw:(k + 1) * mw],
                            xbf2[:, k * T + col * CW:k * T + col * CW + CW],
                            start=(k == 0), stop=(k == NCH - 1)))
                    for k in range(NCH):
                        pb.op("tensor", lambda e, k=k, wcgm=wcgm, mw=mw, col=col, par=par: e.matmul(
                            psg[0:mw, par:par + CW], wcgm[:, k * mw:(k + 1) * mw],
                            xbf2[:, k * T + col * CW:k * T + col * CW + CW],
                            start=(k == 0), stop=(k == NCH - 1)))
                    pb.chain_end()
                emit_chain(0)
                for u, (m, col) in enumerate(units):
                    if u + 1 < len(units):
                        emit_chain(u + 1)
                    mw = min(128, 2730 - m * 128)
                    mta = mtl2[m % 2]
                    par = (u % 2) * 512
                    gpar = (u % 2) * 512
                    joins = []
                    if col == 0 and m >= 2:
                        joins.append(mstoks[m - 2])
                    if u - 2 in mtoks_:
                        joins.append(mtoks_[u - 2])
                    atoks[u] = pb.side_compute("scalar", [
                        lambda e, m=m, mw=mw, par=par, gpar=gpar: e.activation(
                            gel[0:mw, gpar:gpar + CW], psg[0:mw, par:par + CW],
                            AF.Gelu, bias=b1g_c[0:mw, m:m + 1]),
                        lambda e, m=m, mw=mw, par=par, gpar=gpar: e.activation(
                            asb[0:mw, gpar:gpar + CW], psa[0:mw, par:par + CW],
                            AF.Identity, bias=b1a_c[0:mw, m:m + 1])],
                        joins=tuple(joins))
                    mtoks_[u] = pb.side_compute("vector", [
                        lambda e, mw=mw, mta=mta, col=col, gpar=gpar: e.tensor_mul(
                            mta[0:mw, col * CW:col * CW + CW],
                            gel[0:mw, gpar:gpar + CW], asb[0:mw, gpar:gpar + CW])],
                        joins=(atoks[u],))
                    if col == ncol - 1:
                        mstoks[m] = pb.side_dma(
                            m_prod[m * 128:m * 128 + mw, toff:toff + T],
                            mta[0:mw, 0:T],
                            joins=tuple(mtoks_[u - cc] for cc in range(ncol)))
            with pb.scope() as s7:
                pb.side_fence()
                w2sb = s7.sb(128, 22 * D, BF16)
                cl2 = CastLoader(pb, s7, D)
                w2toks = []
                for kc in range(22):
                    kw = min(128, 2730 - kc * 128)
                    w2toks.append(cl2.load(
                        w2sb[0:kw, kc * D:(kc + 1) * D],
                        W2d[kc * 128:kc * 128 + kw, :], D,
                        view=lambda t, kw=kw: t[0:kw, 0:D]))
                pb.join_many(w2toks, "tensor")
                mm2 = [s7.sb(128, 512, BF16) for _ in range(2)]
                lt2 = [s7.sb(128, NCH * CW, F32) for _ in range(2)]
                out_all = s7.sb(128, NCH * CW, F32)
                pso = [s7.ps(128, 512, F32) for _ in range(NCH)]
                lmidr = loc_mid.rearrange("(c p) t -> p c t", p=128)
                loutr = loc_out.rearrange("(c p) t -> p c t", p=128)
                ncol = T // CW
                def mload(col, kc):
                    kw = min(128, 2730 - kc * 128)
                    t0 = toff + col * CW
                    return pb.side_dma(mm2[kc % 2][0:kw, 0:CW],
                                       m_prod[kc * 128:kc * 128 + kw, t0:t0 + CW])
                def ltload(col):
                    t0 = toff + col * CW
                    return pb.side_dma(
                        lt2[col % 2][:].rearrange("p (c w) -> p c w", c=NCH),
                        lmidr[:, :, t0:t0 + CW])
                mtoks = {(0, 0): mload(0, 0)}
                lttoks = {0: ltload(0)}
                otok = [None]
                for col in range(ncol):
                    t0 = toff + col * CW
                    if col + 1 < ncol:
                        lttoks[col + 1] = ltload(col + 1)
                    for kc in range(22):
                        kw = min(128, 2730 - kc * 128)
                        if kc + 1 < 22:
                            mtoks[(col, kc + 1)] = mload(col, kc + 1)
                        elif col + 1 < ncol:
                            mtoks[(col + 1, 0)] = mload(col + 1, 0)
                        pb.join(mtoks[(col, kc)], "tensor")
                        mb = mm2[kc % 2]
                        pb.chain_begin()
                        for dc in range(NCH):
                            pb.op("tensor", lambda e, kc=kc, kw=kw, dc=dc, mb=mb: e.matmul(
                                pso[dc][:, 0:CW], w2sb[0:kw, kc * D + dc * 128:kc * D + (dc + 1) * 128],
                                mb[0:kw, 0:CW], start=(kc == 0), stop=(kc == 21)))
                        pb.chain_end()
                    pb.join(lttoks[col], "vector")
                    if otok[0] is not None:
                        pb.join(otok[0], "vector")
                    la = lt2[col % 2]
                    for dc in range(NCH):
                        pb.op("scalar", lambda e, dc=dc: e.activation(
                            pso[dc][:, 0:CW], pso[dc][:, 0:CW], AF.Identity, bias=b2_c[:, dc:dc + 1]))
                        pb.op("vector", lambda e, dc=dc, la=la: e.tensor_add(
                            out_all[:, dc * CW:(dc + 1) * CW], la[:, dc * CW:(dc + 1) * CW],
                            pso[dc][:, 0:CW]))
                    otok[0] = pb.side_dma(loutr[:, :, t0:t0 + CW],
                                          out_all[:].rearrange("p (c w) -> p c w", c=NCH))
    L.__exit__(None, None, None)


DBG = {}


def build_l0ffn_layer(li):
    pb = PB(f"l0ffn{li}")
    cm = Common(pb, masks=True)
    ik = "ExternalOutput" if DBG.get("l0ffn") else "Internal"
    loc_in = pb.dram("loc_in", [D, NTOK], F32)
    loc_out = pb.dram("loc_out", [D, NTOK], F32, kind="ExternalOutput")
    loc_mid = pb.dram("loc_mid", [D, NTOK], F32, kind=ik)
    m_prod = pb.dram("m_prod", [2816, NTOK], BF16, kind=ik)
    emit_l0ffn_layer(pb, cm, li, loc_in, loc_mid, m_prod, loc_out)
    return pb



def emit_l1attn(pb, cm, res1, loc6, res2, res3):
    W_dq = pb.dram("W_dq", [128, D], F32)
    W_uq = pb.dram("W_uq", [D, 128], F32)
    W_dkvT = pb.dram("W_dkvT", [D, 128], F32)
    W_uk = pb.dram("W_uk", [D, 128], F32)
    W_uv = pb.dram("W_uv", [D, 128], F32)
    W_oT = pb.dram("W_oT", [D, D], F32)

    A = _Scope(pb)
    A.__enter__()
    Bsb = A.sb(128, NCH * 2048, BF16)
    Vsb = A.sb(128, D, BF16)
    with pb.scope() as sc:
        _pt = []
        clp = CastLoader(pb, sc, NCH * 128)
        v3 = lambda t: t[:, 0:NCH * 128].rearrange("p (c m) -> p c m", c=NCH)
        wuq = sc.sb(128, NCH * 128, BF16)
        _pt.append(clp.load(wuq[:], W_uq.rearrange("(c p) m -> p c m", p=128),
                            NCH * 128, view=v3))
        wuk = sc.sb(128, NCH * 128, BF16)
        _pt.append(clp.load(wuk[:], W_uk.rearrange("(c p) m -> p c m", p=128),
                            NCH * 128, view=v3))
        wuv = sc.sb(128, NCH * 128, BF16)
        _pt.append(clp.load(wuv[:], W_uv.rearrange("(c p) m -> p c m", p=128),
                            NCH * 128, view=v3))
        wdq = sc.sb(128, D, BF16)
        _pt.append(clp.load(wdq[:], W_dq, D))
        wot = sc.sb(128, NCH * D, BF16)
        for cchunk in range(NCH):
            _pt.append(clp.load(
                wot[:, cchunk * 1024:(cchunk + 1) * 1024],
                W_oT[cchunk * 128:(cchunk + 1) * 128, :], D))
        pb.join_many(_pt, "tensor")
        pp = sc.ps(128, 512, F32)
        Psm = sc.sb(128, 128, BF16)
        for k in range(NCH):
            pb.op("tensor", lambda e, k=k: e.matmul(
                pp[:, 0:128], wuq[:, k * 128:(k + 1) * 128], wuk[:, k * 128:(k + 1) * 128],
                start=(k == 0), stop=(k == NCH - 1)))
        pb.op("vector", lambda e: e.tensor_copy(Psm[:], pp[:, 0:128]))
        Msb = sc.sb(128, NCH * 128, BF16)
        for i in range(NCH):
            pb.op("tensor", lambda e, i=i: e.matmul(
                pp[:, 0:128], wdq[:, i * 128:(i + 1) * 128], Psm[:], start=True, stop=True))
            pb.op("vector", lambda e, i=i: e.tensor_copy(Msb[:, i * 128:(i + 1) * 128], pp[:, 0:128]))
        Gh = sc.sb(128, 128, BF16)
        for h in range(16):
            cc, p = h // 2, (h % 2) * 64
            pb.op("tensor", lambda e, cc=cc, p=p: e.matmul(
                pp[:, 0:128], wuq[p:p + 64, cc * 128:(cc + 1) * 128],
                Msb[p:p + 64, cc * 128:(cc + 1) * 128], start=True, stop=True))
            pb.op("vector", lambda e: e.tensor_copy(Gh[:], pp[:, 0:128]))
            for i in range(NCH):
                pb.op("tensor", lambda e, i=i: e.matmul(
                    pp[:, 0:128], wdq[:, i * 128:(i + 1) * 128], Gh[:], start=True, stop=True))
                pb.op("scalar", lambda e, i=i, h=h: e.activation(
                    Bsb[:, i * 2048 + h * 128:i * 2048 + (h + 1) * 128], pp[:, 0:128],
                    AF.Identity, scale=0.125))
        for nslc in range(2):
            for k in range(NCH):
                pb.op("tensor", lambda e, k=k, nslc=nslc: e.matmul(
                    pp[:, 0:512], wuv[:, k * 128:(k + 1) * 128],
                    wot[:, k * D + nslc * 512:k * D + nslc * 512 + 512],
                    start=(k == 0), stop=(k == NCH - 1)))
            pb.op("vector", lambda e, nslc=nslc: e.tensor_copy(
                Vsb[:, nslc * 512:(nslc + 1) * 512], pp[:, 0:512]))

    pb.side_fence()
    for s in range(NSEQ):
        toff = s * T
        with pb.scope() as sA:
            pb.side_fence()
            xbf = sA.sb(128, NCH * T, BF16)
            with pb.scope() as s1:
                xt2 = [s1.sb(128, NCH * CW, F32) for _ in range(2)]
                l62 = [s1.sb(128, NCH * CW, F32) for _ in range(2)]
                res1r = res1.rearrange("(c p) t -> p c t", p=128)
                loc6r = loc6.rearrange("(c p) t -> p c t", p=128)
                res2r = res2.rearrange("(c p) t -> p c t", p=128)
                def c1load(col, joins):
                    t0 = toff + col * CW
                    t1 = pb.side_dma(xt2[col % 2][:].rearrange("p (c w) -> p c w", c=NCH),
                                     res1r[:, :, t0:t0 + CW], joins=joins)
                    t2 = pb.side_dma(l62[col % 2][:].rearrange("p (c w) -> p c w", c=NCH),
                                     loc6r[:, :, t0:t0 + CW])
                    return (t1, t2)
                toks = {0: c1load(0, ())}
                stoks = {}
                for col in range(T // CW):
                    t0 = toff + col * CW
                    if col + 1 < T // CW:
                        toks[col + 1] = c1load(col + 1, (stoks[col - 1],) if col >= 1 else ())
                    pb.join_many(toks[col], "vector")
                    xa, la = xt2[col % 2], l62[col % 2]
                    pb.op("vector", lambda e, xa=xa, la=la: e.tensor_add(xa[:], xa[:], la[:]))
                    stoks[col] = pb.side_dma(res2r[:, :, t0:t0 + CW],
                                             xa[:].rearrange("p (c w) -> p c w", c=NCH))
                    xt = [xa[:, c * CW:(c + 1) * CW] for c in range(NCH)]
                    outs = [xbf[:, c * T + col * CW:c * T + col * CW + CW]
                            for c in range(NCH)]
                    emit_ln(pb, cm, xt, CW, outs)
            ckv = sA.sb(128, T, BF16)
            ckvt = sA.sb(128, 16 * 128, BF16)
            with pb.scope() as s2:
                wdkvt = s2.sb(128, NCH * 128, BF16)
                cl2_ = CastLoader(pb, s2, NCH * 128, slots=1)
                _dk = cl2_.load(wdkvt[:], W_dkvT.rearrange("(c p) m -> p c m", p=128),
                                NCH * 128,
                                view=lambda t: t[:, 0:NCH * 128].rearrange("p (c m) -> p c m", c=NCH))
                pb.join(_dk, "tensor")
                pp = s2.ps(128, 512, F32)
                pst = s2.ps(128, 1024, BF16)
                for col in range(T // CW):
                    pb.chain_begin()
                    for k in range(NCH):
                        pb.op("tensor", lambda e, k=k, col=col: e.matmul(
                            pp[:, 0:CW], wdkvt[:, k * 128:(k + 1) * 128],
                            xbf[:, k * T + col * CW:k * T + col * CW + CW],
                            start=(k == 0), stop=(k == NCH - 1)))
                    pb.chain_end()
                    pb.op("vector", lambda e, col=col: e.tensor_copy(
                        ckv[:, col * CW:(col + 1) * CW], pp[:, 0:CW]))
                for g0 in range(0, 16, 4):
                    pb.chain_begin()
                    for j in range(4):
                        pb.op("tensor", lambda e, j=j, g0=g0: e.transpose(
                            pst[:, j * 128:(j + 1) * 128],
                            ckv[:, (g0 + j) * 128:(g0 + j + 1) * 128], cm.idn_bf[:]))
                    pb.chain_end()
                    pb.op("vector", lambda e, g0=g0: e.tensor_copy(
                        ckvt[:, g0 * 128:(g0 + 4) * 128], pst[:, 0:512]))
            with pb.scope() as s3:
                qlh = s3.sb(128, T, BF16)
                psq = s3.ps(128, 2048, F32)
                pst = s3.ps(128, 1024, BF16)
                psv = s3.ps(128, 512, F32)
                yps = s3.ps(128, 512, F32)
                Pt = s3.sb(128, 2048, BF16)
                Psb2 = [s3.sb(128, 2048, BF16) for _ in range(2)]
                nmax = s3.sb(128, 2, F32)
                ssum = s3.sb(128, 2, F32)
                rcp = s3.sb(128, 2, F32)
                ctxh = s3.sb(128, T, BF16)
                rt = s3.sb(128, T, F32)
                htoks = {}
                for h in range(16):
                    qetoks = {}
                    for col in range(T // CW):
                        par = (col % 4) * 512
                        pb.chain_begin()
                        for i in range(NCH):
                            pb.op("tensor", lambda e, i=i, col=col, h=h, par=par: e.matmul(
                                psq[:, par:par + CW], Bsb[:, i * 2048 + h * 128:i * 2048 + (h + 1) * 128],
                                xbf[:, i * T + col * CW:i * T + col * CW + CW],
                                start=(i == 0), stop=(i == NCH - 1)))
                        pb.chain_end()
                        qetoks[col] = pb.side_compute("scalar", [
                            lambda e, col=col, par=par: e.activation(
                                qlh[:, col * CW:(col + 1) * CW], psq[:, par:par + CW], AF.Identity)])
                    pb.join_many(list(qetoks.values()), "tensor")
                    smt = {}
                    def emit_qk(qi):
                        nk = 128 * (qi + 1)
                        q0 = qi * 128
                        if qi - 1 in smt:
                            pb.join(smt[qi - 1][0], "tensor")
                        pb.chain_begin()
                        pb.op("tensor", lambda e, q0=q0: e.matmul(
                            psq[:, q0:q0 + 128], cm.idn_bf[:], cm.tril[:],
                            start=True, stop=False, skip_group_check=True))
                        for ks_ in range(0, nk, 512):
                            w = min(512, nk - ks_)
                            diag = (ks_ + w == nk)
                            segs = [(ks_, w, True)] if not diag else (
                                ([(ks_, q0 - ks_, False)] if q0 > ks_ else []) + [(q0, 128, False)])
                            for (s0, sw, st_) in segs:
                                pb.op("tensor", lambda e, s0=s0, sw=sw, st_=st_, q0=q0: e.matmul(
                                    psq[:, s0:s0 + sw], qlh[:, q0:q0 + 128],
                                    ckv[:, s0:s0 + sw], start=st_, stop=True,
                                    skip_group_check=True))
                        pb.chain_end()
                    def emit_smax(qi):
                        nk = 128 * (qi + 1)
                        sl = qi % 2
                        Ps = Psb2[sl]
                        jm = (smt[qi - 2][0],) if qi - 2 in smt else ()
                        t_max = pb.side_compute("vector", [
                            lambda e, nk=nk, sl=sl: e.reduce_max(
                                nmax[:, sl:sl + 1], psq[:, 0:nk], axis=AX.X, negate=True)],
                            joins=jm)
                        jm2 = (smt[qi - 2][3],) if qi - 2 in smt else ()
                        t_exp = pb.side_compute("scalar", [
                            lambda e, nk=nk, sl=sl, Ps=Ps: e.activation(
                                Ps[:, 0:nk], psq[:, 0:nk], AF.Exp,
                                bias=nmax[:, sl:sl + 1], accum_out=ssum[:, sl:sl + 1])],
                            joins=(t_max,) + jm2)
                        jm3 = (smt[qi - 2][3],) if qi - 2 in smt else ()
                        t_rcp = pb.side_compute("vector", [
                            lambda e, sl=sl: e.reciprocal(rcp[:, sl:sl + 1], ssum[:, sl:sl + 1])],
                            joins=(t_exp,) + jm3)
                        t_ts = pb.side_compute("vector", [
                            lambda e, nk=nk, sl=sl, Ps=Ps: e.tensor_scalar_mul(
                                Ps[:, 0:nk], Ps[:, 0:nk], rcp[:, sl:sl + 1])],
                            joins=(t_rcp,))
                        smt[qi] = (t_exp, t_max, t_rcp, t_ts)
                    def emit_ptpv(qi):
                        nk = 128 * (qi + 1)
                        q0 = qi * 128
                        sl = qi % 2
                        Ps = Psb2[sl]
                        pb.join(smt[qi][3], "tensor")
                        nkc = qi + 1
                        for g0_ in range(0, nkc, 4):
                            gn = min(4, nkc - g0_)
                            pb.chain_begin()
                            for j in range(gn):
                                pb.op("tensor", lambda e, j=j, g0_=g0_, Ps=Ps: e.transpose(
                                    pst[:, j * 128:(j + 1) * 128],
                                    Ps[:, (g0_ + j) * 128:(g0_ + j + 1) * 128], cm.idn_bf[:]))
                            pb.chain_end()
                            pb.op("vector", lambda e, g0_=g0_, gn=gn: e.tensor_copy(
                                Pt[:, g0_ * 128:(g0_ + gn) * 128], pst[:, 0:gn * 128]))
                        pb.chain_begin()
                        for kc in range(nkc):
                            pb.op("tensor", lambda e, kc=kc, nkc=nkc: e.matmul(
                                psv[:, 0:128], ckvt[:, kc * 128:(kc + 1) * 128],
                                Pt[:, kc * 128:kc * 128 + 128],
                                start=(kc == 0), stop=(kc == nkc - 1)))
                        pb.chain_end()
                        pb.op("vector", lambda e, q0=q0: e.tensor_copy(
                            ctxh[:, q0:q0 + 128], psv[:, 0:128]))
                    emit_qk(0)
                    for qi in range(16):
                        emit_smax(qi)
                        if qi + 1 < 16:
                            emit_qk(qi + 1)
                        emit_ptpv(qi)
                    cc, p = h // 2, (h % 2) * 64
                    rtok = pb.side_dma(rt[p:p + 64, 0:T],
                                       res2[cc * 128 + p:cc * 128 + p + 64, toff:toff + T],
                                       joins=(htoks[h - 2],) if h >= 2 else ())
                    for col in range(T // CW):
                        pb.op("tensor", lambda e, col=col, h=h, p=p: e.matmul(
                            yps[p:p + 64, 0:CW], Vsb[:, h * 64:(h + 1) * 64],
                            ctxh[:, col * CW:(col + 1) * CW], start=True, stop=True))
                        if col == 0:
                            pb.join(rtok, "vector")
                        pb.op("vector", lambda e, p=p, col=col: e.tensor_add(
                            rt[p:p + 64, col * CW:(col + 1) * CW],
                            rt[p:p + 64, col * CW:(col + 1) * CW], yps[p:p + 64, 0:CW]))
                    htoks[h] = pb.side_dma(res3[cc * 128 + p:cc * 128 + p + 64, toff:toff + T],
                                           rt[p:p + 64, 0:T])
    A.__exit__(None, None, None)


def build_l1attn():
    pb = PB("l1attn")
    cm = Common(pb, masks=True)
    res1 = pb.dram("res1", [D, NTOK], F32)
    loc6 = pb.dram("loc6", [D, NTOK], F32)
    res3 = pb.dram("res3", [D, NTOK], F32, kind="ExternalOutput")
    res2 = pb.dram("res2", [D, NTOK], F32, kind="Internal")
    emit_l1attn(pb, cm, res1, loc6, res2, res3)
    return pb



def emit_l1ffn(pb, cm, res3, h_scr, outfm):
    wiT = pb.dram("wiT", [D, 4096], F32)
    woT = pb.dram("woT", [4096, D], F32)
    gate_wT = pb.dram("gate_wT", [D, 8], F32)
    gate_b = pb.dram("gate_b", [1, 8], F32)
    Ap = [pb.dram(f"Apack{g}", [D, 64], F32) for g in range(4)]
    Bp = [pb.dram(f"Bpack{g}", [64, 4096], F32) for g in range(4)]
    bi_c = cm.panel("bi_p", 32)
    bo_c = cm.panel("bo_p", 8)

    gates_r = pb.sb(8, NTOK, F32)
    pb.side_fence()
    with pb.scope() as sX:
        xbf = sX.sb(128, NCH * NTOK, BF16)
        with pb.scope() as s1:
            xt2 = [s1.sb(128, NCH * CW, F32) for _ in range(2)]
            res3r = res3.rearrange("(c p) t -> p c t", p=128)
            def xload(col):
                t0 = col * CW
                return pb.side_dma(xt2[col % 2][:].rearrange("p (c w) -> p c w", c=NCH),
                                   res3r[:, :, t0:t0 + CW])
            toks = {0: xload(0)}
            for col in range(NTOK // CW):
                if col + 1 < NTOK // CW:
                    toks[col + 1] = xload(col + 1)
                pb.join(toks[col], "tensor")
                xa = xt2[col % 2]
                xt = [xa[:, c * CW:(c + 1) * CW] for c in range(NCH)]
                outs = [xbf[:, c * NTOK + col * CW:c * NTOK + col * CW + CW]
                        for c in range(NCH)]
                emit_ln(pb, cm, xt, CW, outs)
        with pb.scope() as s2:
            gw = s2.sb(128, NCH * 8, BF16)
            clg_ = CastLoader(pb, s2, NCH * 8, slots=1)
            _gt = clg_.load(gw[:], gate_wT.rearrange("(c p) m -> p c m", p=128), NCH * 8,
                            view=lambda t: t[:, 0:NCH * 8].rearrange("p (c m) -> p c m", c=NCH))
            pb.join(_gt, "tensor")
            gbr = s2.sb(1, 8, F32)
            pb.dma(gbr[:], gate_b)
            gbb = s2.ps(128, 8, F32)
            pb.op("tensor", lambda e: e.matmul(gbb[:, 0:8], cm.ones_row_f[0:1, :], gbr[:], start=True, stop=True))
            gbbs = s2.sb(128, 8, F32)
            pb.op("vector", lambda e: e.tensor_copy(gbbs[:], gbb[:, 0:8]))
            pg = s2.ps(128, 8, F32)
            pt2 = s2.ps(128, 128, F32)
            lg = s2.sb(128, 8, F32)
            m1 = s2.sb(128, 1, F32)
            m2 = s2.sb(128, 1, F32)
            eqm = s2.sb(128, 8, F32)
            s2t = s2.sb(128, 8, F32)
            ssum = s2.sb(128, 1, F32)
            rcp = s2.sb(128, 1, F32)
            ind = s2.sb(128, 8, F32)
            gcols = s2.sb(128, 8, F32)
            vs = s2.sb(128, 1, F32)
            for tt in range(NTOK // 128):
                pb.chain_begin()
                for cc2 in range(NCH):
                    pb.op("tensor", lambda e, cc2=cc2, tt=tt: e.matmul(
                        pg[:, 0:8], xbf[:, cc2 * NTOK + tt * 128:cc2 * NTOK + tt * 128 + 128],
                        gw[:, cc2 * 8:(cc2 + 1) * 8], start=(cc2 == 0), stop=(cc2 == NCH - 1)))
                pb.chain_end()
                pb.op("vector", lambda e: e.tensor_add(lg[:], pg[:, 0:8], gbbs[:]))
                pb.op("vector", lambda e: e.reduce_max(m1[:], lg[:], axis=AX.X, negate=True))
                pb.op("scalar", lambda e: e.activation(lg[:], lg[:], AF.Exp, bias=m1[:, 0:1],
                                                       accum_out=ssum[:, 0:1]))
                pb.op("vector", lambda e: e.reciprocal(rcp[:], ssum[:]))
                pb.op("vector", lambda e: e.tensor_scalar_mul(lg[:], lg[:], rcp[:, 0:1]))
                pb.op("vector", lambda e: e.reduce_max(m1[:], lg[:], axis=AX.X))
                pb.op("vector", lambda e: e.tensor_scalar(eqm[:], lg[:], m1[:, 0:1], None, ALU.is_ge))
                pb.op("vector", lambda e: e.scalar_tensor_tensor(s2t[:], eqm[:], NEG, lg[:], ALU.mult, ALU.add))
                pb.op("vector", lambda e: e.reduce_max(m2[:], s2t[:], axis=AX.X))
                pb.op("vector", lambda e: e.tensor_scalar(ind[:], lg[:], m2[:, 0:1], None, ALU.is_ge))
                pb.op("vector", lambda e: e.tensor_add(vs[:], m1[:], m2[:]))
                pb.op("vector", lambda e: e.tensor_scalar_mul(gcols[:], ind[:], vs[:, 0:1]))
                pb.op("tensor", lambda e, tt=tt: e.transpose(pt2[0:8, 0:128], gcols[:], cm.idn_f[:]))
                pb.op("vector", lambda e, tt=tt: e.tensor_copy(
                    gates_r[:, tt * 128:(tt + 1) * 128], pt2[0:8, 0:128]))
        s_g = [sX.sb(64, NTOK, BF16) for _ in range(4)]
        with pb.scope() as s3:
            ap_sb2 = [s3.sb(128, NCH * 64, BF16) for _ in range(2)]
            pp = s3.ps(128, 512, F32)
            cla_ = CastLoader(pb, s3, NCH * 64)
            def aload(g):
                return cla_.load(ap_sb2[g % 2][:],
                                 Ap[g].rearrange("(c p) m -> p c m", p=128), NCH * 64,
                                 view=lambda t: t[:, 0:NCH * 64].rearrange("p (c m) -> p c m", c=NCH))
            atoks = {0: aload(0)}
            for g in range(4):
                if g + 1 < 4:
                    atoks[g + 1] = aload(g + 1)
                pb.join(atoks[g], "tensor")
                ap_sb = ap_sb2[g % 2]
                for col in range(NTOK // CW):
                    pb.chain_begin()
                    for k in range(NCH):
                        pb.op("tensor", lambda e, k=k, col=col: e.matmul(
                            pp[0:64, 0:CW], ap_sb[:, k * 64:(k + 1) * 64],
                            xbf[:, k * NTOK + col * CW:k * NTOK + col * CW + CW],
                            start=(k == 0), stop=(k == NCH - 1)))
                    pb.chain_end()
                    pb.op("vector", lambda e, g=g, col=col: e.tensor_copy(
                        s_g[g][:, col * CW:(col + 1) * CW], pp[0:64, 0:CW]))
        with pb.scope() as s4:
            bg_sb = [s4.sb(64, 4096, BF16) for _ in range(4)]
            clb_ = CastLoader(pb, s4, 2048, slots=1)
            bgtoks = [clb_.load(bg_sb[g][0:64, hh * 2048:(hh + 1) * 2048],
                                Bp[g][:, hh * 2048:(hh + 1) * 2048], 2048,
                                view=lambda t: t[0:64, 0:2048])
                      for g in range(4) for hh in range(2)]
            pb.join_many(bgtoks, "tensor")
            wcol2 = [s4.sb(128, NCH * 128, BF16) for _ in range(2)]
            psb_ = s4.ps(128, 1024, F32)
            psd = s4.ps(128, 4 * 512, F32)
            pgb = s4.ps(128, 512, F32)
            base_bf = s4.sb(128, 512, BF16)
            gb_bf = s4.sb(128, 8 * 512, BF16)
            dtl = s4.sb(128, 4 * 512, BF16)
            rg = s4.sb(128, 4 * 512, BF16)
            tpair = s4.sb(128, 2 * 512, F32)
            hacc2 = [s4.sb(128, 512, F32) for _ in range(2)]
            hbf = s4.sb(128, 512, BF16)
            sel8 = s4.sb(8, 1024, F32)
            stok = pb.side_dma(sel8[:], pb.dram("sel8", [8, 1024], F32))
            pb.join(stok, "tensor")
            hbf2 = [hbf, s4.sb(128, 512, BF16)]
            clwi = CastLoader(pb, s4, NCH * 128)
            def wiload(col, m):
                return clwi.load(
                    wcol2[m % 2][:],
                    wiT[:, m * 128:(m + 1) * 128].rearrange("(k p) m -> p k m", p=128),
                    NCH * 128,
                    view=lambda t: t[:, 0:NCH * 128].rearrange("p (k m) -> p k m", k=NCH))
            wtoks = {(0, 0): wiload(0, 0)}
            hprev = [None, None]
            ncol = NTOK // CW
            units = [(col, m) for col in range(ncol) for m in range(32)]
            def emit_base(u):
                col, m = units[u]
                if m + 1 < 32:
                    wtoks[(col, m + 1)] = wiload(col, m + 1)
                elif col + 1 < ncol:
                    wtoks[(col + 1, 0)] = wiload(col + 1, 0)
                pb.join(wtoks[(col, m)], "tensor")
                wcol = wcol2[m % 2]
                par = (u % 2) * 512
                pb.chain_begin()
                for k in range(NCH):
                    pb.op("tensor", lambda e, k=k, m=m, col=col, wcol=wcol, par=par: e.matmul(
                        psb_[:, par:par + CW], wcol[:, k * 128:(k + 1) * 128],
                        xbf[:, k * NTOK + col * CW:k * NTOK + col * CW + CW],
                        start=(k == 0), stop=(k == NCH - 1)))
                pb.chain_end()
            def emit_delta(u, r):
                col, m = units[u]
                pb.chain_begin()
                for v in range(4):
                    e8 = 4 * r + v
                    g, j = e8 // 2, e8 % 2
                    pb.op("tensor", lambda e, g=g, j=j, m=m, col=col, v=v: e.matmul(
                        psd[:, v * CW:(v + 1) * CW],
                        bg_sb[g][32 * j:32 * j + 32, m * 128:(m + 1) * 128],
                        s_g[g][32 * j:32 * j + 32, col * CW:col * CW + CW],
                        start=True, stop=True))
                pb.chain_end()
            for col in range(ncol):
                t0 = col * CW
                for e8 in range(8):
                    pb.op("tensor", lambda e, e8=e8, t0=t0: e.matmul(
                        pgb[:, 0:CW], sel8[:, e8 * 128:(e8 + 1) * 128],
                        gates_r[:, t0:t0 + CW], start=True, stop=True))
                    pb.op("vector", lambda e, e8=e8: e.tensor_copy(
                        gb_bf[:, e8 * 512:e8 * 512 + CW], pgb[:, 0:CW]))
                for m in range(32):
                    u = col * 32 + m
                    if u == 0:
                        emit_base(0)
                    par = (u % 2) * 512
                    pb.op("scalar", lambda e, m=m, par=par: e.activation(
                        base_bf[:], psb_[:, par:par + CW], AF.Identity, bias=bi_c[:, m:m + 1]))
                    emit_delta(u, 0)
                    pb.chain_begin()
                    for v in range(4):
                        pb.op("vector", lambda e, v=v: e.tensor_add(
                            dtl[:, v * CW:(v + 1) * CW], base_bf[:],
                            psd[:, v * CW:(v + 1) * CW]))
                    pb.chain_end()
                    emit_delta(u, 1)
                    if u + 1 < len(units):
                        emit_base(u + 1)
                    for r in range(2):
                        hr = hacc2[r]
                        if r == 1:
                            pb.chain_begin()
                            for v in range(4):
                                pb.op("vector", lambda e, v=v: e.tensor_add(
                                    dtl[:, v * CW:(v + 1) * CW], base_bf[:],
                                    psd[:, v * CW:(v + 1) * CW]))
                            pb.chain_end()
                        pb.chain_begin()
                        for v in range(4):
                            pb.op("gpsimd", lambda e, v=v: e.tensor_scalar_max(
                                rg[:, v * CW:(v + 1) * CW], dtl[:, v * CW:(v + 1) * CW], 0.0))
                        pb.chain_end()
                        pb.chain_begin()
                        for v in range(4):
                            e8 = 4 * r + v
                            pb.op("gpsimd", lambda e, v=v, e8=e8: e.tensor_mul(
                                rg[:, v * CW:(v + 1) * CW], rg[:, v * CW:(v + 1) * CW],
                                gb_bf[:, e8 * 512:e8 * 512 + CW]))
                        pb.chain_end()
                        pb.chain_begin()
                        pb.op("vector", lambda e: e.tensor_add(
                            tpair[:, 0:CW], rg[:, 0:CW], rg[:, CW:2 * CW]))
                        pb.op("vector", lambda e: e.tensor_add(
                            tpair[:, CW:2 * CW], rg[:, 2 * CW:3 * CW], rg[:, 3 * CW:4 * CW]))
                        pb.chain_end()
                        pb.op("vector", lambda e, hr=hr: e.tensor_add(
                            hr[:], tpair[:, 0:CW], tpair[:, CW:2 * CW]))
                    hb = hbf2[m % 2]
                    if hprev[m % 2] is not None:
                        pb.join(hprev[m % 2], "vector")
                    pb.op("vector", lambda e, hb=hb: e.tensor_add(hb[:], hacc2[0][:], hacc2[1][:]))
                    hprev[m % 2] = pb.side_dma(
                        h_scr[m * 128:(m + 1) * 128, t0:t0 + CW], hb[:])
    with pb.scope() as s5:
        pb.side_fence()
        wosb = s5.sb(128, 32 * D, BF16)
        clwo = CastLoader(pb, s5, D)
        wotoks = [clwo.load(wosb[:, kc * D:(kc + 1) * D],
                            woT[kc * 128:(kc + 1) * 128, :], D)
                  for kc in range(32)]
        pb.join_many(wotoks, "tensor")
        mm2 = [s5.sb(128, 512, BF16) for _ in range(2)]
        rt2 = [s5.sb(128, NCH * CW, F32) for _ in range(2)]
        out_all = s5.sb(128, NCH * CW, F32)
        pso = [s5.ps(128, 512, F32) for _ in range(NCH)]
        vsbs = s5.sb(128, 512, F32)
        t1 = s5.sb(128, 512, F32)
        res3r = res3.rearrange("(c p) t -> p c t", p=128)
        outr = outfm.rearrange("(c p) t -> p c t", p=128)
        ncol = NTOK // CW
        def mload(col, kc):
            t0 = col * CW
            return pb.side_dma(mm2[kc % 2][:], h_scr[kc * 128:(kc + 1) * 128, t0:t0 + CW])
        def rload(col):
            t0 = col * CW
            return pb.side_dma(rt2[col % 2][:].rearrange("p (c w) -> p c w", c=NCH),
                               res3r[:, :, t0:t0 + CW])
        mtoks = {(0, 0): mload(0, 0)}
        rtoks = {0: rload(0)}
        otok = [None]
        for col in range(ncol):
            t0 = col * CW
            if col + 1 < ncol:
                rtoks[col + 1] = rload(col + 1)
            pb.op("tensor", lambda e, t0=t0: e.matmul(
                pso[0][:, 0:CW], cm.ones_row_f[0:8, :], gates_r[:, t0:t0 + CW], start=True, stop=True))
            pb.op("vector", lambda e: e.tensor_copy(vsbs[:], pso[0][:, 0:CW]))
            for kc in range(32):
                if kc + 1 < 32:
                    mtoks[(col, kc + 1)] = mload(col, kc + 1)
                elif col + 1 < ncol:
                    mtoks[(col + 1, 0)] = mload(col + 1, 0)
                pb.join(mtoks[(col, kc)], "tensor")
                mb = mm2[kc % 2]
                pb.chain_begin()
                for dc in range(NCH):
                    pb.op("tensor", lambda e, kc=kc, dc=dc, mb=mb: e.matmul(
                        pso[dc][:, 0:CW], wosb[:, kc * D + dc * 128:kc * D + (dc + 1) * 128],
                        mb[:], start=(kc == 0), stop=(kc == 31)))
                pb.chain_end()
            pb.join(rtoks[col], "vector")
            if otok[0] is not None:
                pb.join(otok[0], "vector")
            ra = rt2[col % 2]
            for dc in range(NCH):
                pb.op("vector", lambda e, dc=dc: e.scalar_tensor_tensor(
                    t1[:], vsbs[:], bo_c[:, dc:dc + 1], pso[dc][:, 0:CW], ALU.mult, ALU.add))
                pb.op("vector", lambda e, dc=dc, ra=ra: e.tensor_add(
                    out_all[:, dc * CW:(dc + 1) * CW], ra[:, dc * CW:(dc + 1) * CW], t1[:]))
            otok[0] = pb.side_dma(outr[:, :, t0:t0 + CW],
                                  out_all[:].rearrange("p (c w) -> p c w", c=NCH))


def build_l1ffn():
    pb = PB("l1ffn")
    cm = Common(pb)
    res3 = pb.dram("res3", [D, NTOK], F32)
    outfm = pb.dram("outfm", [D, NTOK], F32, kind="ExternalOutput")
    h_scr = pb.dram("h_scr", [4096, NTOK], BF16, kind="Internal")
    emit_l1ffn(pb, cm, res3, h_scr, outfm)
    return pb


def build_fused():
    pb = PB("fused")
    cm = Common(pb, masks=True)
    xfm = pb.dram("xfm", [D, NTOK], F32)
    outfm = pb.dram("outfm", [D, NTOK], F32, kind="ExternalOutput")
    res1 = pb.dram("res1", [D, NTOK], F32, kind="Internal")
    loc_a = pb.dram("loc_a", [D, NTOK], F32, kind="Internal")
    loc_b = pb.dram("loc_b", [D, NTOK], F32, kind="Internal")
    loc_mid = pb.dram("loc_mid", [D, NTOK], F32, kind="Internal")
    res3 = pb.dram("res3", [D, NTOK], F32, kind="Internal")
    scr = pb.dram("scr", [4096, NTOK], BF16, kind="Internal")
    emit_l0attn(pb, cm, xfm, res1, loc_a)
    cur, nxt = loc_a, loc_b
    for i in range(6):
        emit_l0ffn_layer(pb, cm, i, cur, loc_mid, scr[0:2816, :], nxt)
        cur, nxt = nxt, cur
    emit_l1attn(pb, cm, res1, cur, nxt, res3)
    emit_l1ffn(pb, cm, res3, scr, outfm)
    return pb


_cache = {}


def _get(name, builder):
    if name not in _cache:
        pb = builder()
        nc = pb.finalize()
        _cache[name] = nc
    return _cache[name]


def _run(name, builder, feeds):
    import gc
    import os
    nc = _get(name, builder)
    in_maps = []
    for core in range(NCORES):
        m = {}
        for k, v in feeds.items():
            m[k] = np.asarray(v[core] if isinstance(v, list) else v)
        in_maps.append(m)
    res = run_bass_kernel_spmd(nc, in_maps, core_ids=list(range(NCORES)))
    out = res.results
    if os.environ.get("BASS_SIM_FP64"):
        del res
        gc.collect()
    return out


FUSED = True


def _full_feeds(h):
    feeds = {k: h[k] for k in ["idn_bf", "idn_f", "tril_bf", "triu_bf"]}
    feeds.update({k: h[k] for k in h if k.endswith("_hi") or k.endswith("_lo")})
    feeds.update(xfm=h["xfm"], W_attn=h["W_attn"], W_proj=h["W_proj"],
                 b_attn_p=h["b_attn_p"], b_proj_p=h["b_proj_p"],
                 qs2=h["qs2"], ks2=h["ks2"], qs2r=h["qs2r"], ks2r=h["ks2r"],
                 ln_gT=h["ln_gT"], ln_bT=h["ln_bT"], r2t_bf=h["r2t_bf"],
                 W_dq=h["W_dq"], W_uq=h["W_uq"], W_dkvT=h["W_dkvT"],
                 W_uk=h["W_uk"], W_uv=h["W_uv"], W_oT=h["W_oT"],
                 wiT=h["wiT"], woT=h["woT"], bi_p=h["bi_p"], bo_p=h["bo_p"],
                 gate_wT=h["gate_wT"], gate_b=h["gate_b"], sel8=h["sel8"])
    for i in range(6):
        for nm in ("Wqkv", "Wo", "W1a", "W1g", "W2", "b1a_p", "b1g_p", "b2_p"):
            feeds[f"{nm}{i}"] = h[f"{nm}{i}"]
    for g in range(4):
        feeds[f"Apack{g}"] = h[f"Apack{g}"]
        feeds[f"Bpack{g}"] = h[f"Bpack{g}"]
    return feeds


def kernel(**inputs):
    h = host_prep(inputs)
    if FUSED:
        r = _run("fused", build_fused, _full_feeds(h))
        out = np.empty((16, 2048, 1024), np.float32)
        for c in range(NCORES):
            out[2 * c:2 * c + 2] = r[c]["outfm"].T.reshape(2, 2048, 1024)
        return out

    const_feed = {k: h[k] for k in ["idn_bf", "idn_f", "tril_bf", "triu_bf"]}
    feeds = dict(const_feed)
    feeds.update(xfm=h["xfm"], W_attn=h["W_attn"], W_proj=h["W_proj"],
                 b_attn_p=h["b_attn_p"], b_proj_p=h["b_proj_p"])
    r = _run("l0attn", build_l0attn, feeds)
    res1 = [r[i]["res1"] for i in range(NCORES)]
    loc = [r[i]["loc0"] for i in range(NCORES)]

    for i in range(6):
        feeds = dict(const_feed)
        feeds.update({k: h[k] for k in h if k.endswith("_hi") or k.endswith("_lo")})
        feeds.update(loc_in=loc,
                     qs2=h["qs2"], ks2=h["ks2"], qs2r=h["qs2r"], ks2r=h["ks2r"],
                     ln_gT=h["ln_gT"], ln_bT=h["ln_bT"], r2t_bf=h["r2t_bf"])
        for nm in ("Wqkv", "Wo", "W1a", "W1g", "W2", "b1a_p", "b1g_p", "b2_p"):
            feeds[f"{nm}{i}"] = h[f"{nm}{i}"]
        r = _run(f"l0ffn{i}", lambda i=i: build_l0ffn_layer(i), feeds)
        loc = [r[c]["loc_out"] for c in range(NCORES)]

    feeds = dict(const_feed)
    feeds.update(res1=res1, loc6=loc, W_dq=h["W_dq"], W_uq=h["W_uq"],
                 W_dkvT=h["W_dkvT"], W_uk=h["W_uk"], W_uv=h["W_uv"], W_oT=h["W_oT"])
    r = _run("l1attn", build_l1attn, feeds)
    res3 = [r[i]["res3"] for i in range(NCORES)]

    feeds = dict(const_feed)
    feeds.update(res3=res3, wiT=h["wiT"], woT=h["woT"], bi_p=h["bi_p"],
                 bo_p=h["bo_p"], gate_wT=h["gate_wT"], gate_b=h["gate_b"], sel8=h["sel8"],
                 **{f"Apack{g}": h[f"Apack{g}"] for g in range(4)},
                 **{f"Bpack{g}": h[f"Bpack{g}"] for g in range(4)})
    r = _run("l1ffn", build_l1ffn, feeds)
    out = np.empty((16, 2048, 1024), np.float32)
    for c in range(NCORES):
        out[2 * c:2 * c + 2] = r[c]["outfm"].T.reshape(2, 2048, 1024)
    return out


# version 23 (final)
# speedup vs baseline: 1.0571x; 1.0571x over previous
import types
import numpy as np
import ml_dtypes
from contextlib import ExitStack
import concourse.bass as bass
import concourse.mybir as mybir
from concourse.bass_utils import run_bass_kernel_spmd


def _freeze(fn):
    if fn.__closure__:
        cells = []
        for c in fn.__closure__:
            try:
                cells.append(types.CellType(c.cell_contents))
            except ValueError:
                cells.append(c)
        fn = types.FunctionType(fn.__code__, fn.__globals__, fn.__name__,
                                fn.__defaults__, tuple(cells))
    return fn

F32 = mybir.dt.float32
BF16 = mybir.dt.bfloat16
AF = mybir.ActivationFunctionType
ALU = mybir.AluOpType
AX = mybir.AxisListType
BF = ml_dtypes.bfloat16

NCORES = 8
T = 2048
NSEQ = 2
NTOK = T * NSEQ
D = 1024
NCH = D // 128
CW = 512
NEG = -1e30


class _Scope:
    def __init__(self, pb):
        self.pb = pb
        self.stack = ExitStack()

    def __enter__(self):
        return self

    def sb(self, p, f, dtype=F32):
        self.pb.n_sb += 1
        return self.stack.enter_context(
            self.pb.nc.sbuf_tensor(f"sb{self.pb.n_sb}", [p, f], dtype))

    def ps(self, p, f, dtype=F32):
        self.pb.n_sb += 1
        return self.stack.enter_context(
            self.pb.nc.psum_tensor(f"ps{self.pb.n_sb}", [p, f], dtype))

    def __exit__(self, *a):
        self.stack.close()
        return False


EPOCH_TICKS = 26000


class PB:
    def __init__(self, name):
        self.name = name
        self.nc = bass.Bass("TRN2", target_bir_lowering=False, debug=False)
        self.stack = ExitStack()
        self.ops = []
        self.dram_aps = {}
        self.n_sb = 0
        self.epochs = []
        self._new_epoch()

    def _new_epoch(self):
        e = len(self.epochs)
        self.epochs.append({
            "tick": self.stack.enter_context(self.nc.semaphore(f"tick{e}")),
            "gtick": self.stack.enter_context(self.nc.semaphore(f"gtick{e}")),
            "cnt": 0, "gcnt": 0})

    def dram(self, name, shape, dtype=F32, kind="ExternalInput"):
        if name in self.dram_aps:
            return self.dram_aps[name]
        ap = self.nc.dram_tensor(name, list(shape), dtype, kind=kind).ap()
        self.dram_aps[name] = ap
        return ap

    def sb(self, p, f, dtype=F32):
        self.n_sb += 1
        return self.stack.enter_context(
            self.nc.sbuf_tensor(f"sb{self.n_sb}", [p, f], dtype))

    def ps(self, p, f, dtype=F32):
        self.n_sb += 1
        return self.stack.enter_context(
            self.nc.psum_tensor(f"ps{self.n_sb}", [p, f], dtype))

    def scope(self):
        return _Scope(self)

    _chain = None

    def chain_begin(self):
        ep = self.epochs[-1]
        if ep["cnt"] + ep["gcnt"] > EPOCH_TICKS - 500:
            self._new_epoch()
        self._chain = len(self.ops)

    def chain_end(self):
        start, self._chain = self._chain, None
        n = len(self.ops) - start
        if n <= 1:
            return
        engs = {r[1] for r in self.ops[start:]}
        whichs = {r[4] for r in self.ops[start:]}
        assert len(engs) == 1 and whichs == {"t"}, (engs, whichs)
        for i in range(start, len(self.ops)):
            rec = list(self.ops[i])
            rec[6] = "head" if i == start else ("end" if i == len(self.ops) - 1 else "mid")
            self.ops[i] = tuple(rec)

    def op(self, eng, fn, inc=1, _gdma=False):
        ep = self.epochs[-1]
        if self._chain is None and ep["cnt"] + ep["gcnt"] + inc > EPOCH_TICKS:
            self._new_epoch()
            ep = self.epochs[-1]
        self.ops.append((len(self.epochs) - 1, eng, _freeze(fn),
                         (ep["cnt"], ep["gcnt"]), "g" if _gdma else "t", inc, None))
        ep["gcnt" if _gdma else "cnt"] += inc

    def dma(self, out, in_, q="sync"):
        self.op(q, lambda e, o=out, i=in_: e.dma_start(out=o, in_=i), 16,
                _gdma=(q == "gpsimd"))

    NPOOL = 24

    def _side_init(self):
        if not hasattr(self, "_pool"):
            self._pool = [self.stack.enter_context(self.nc.semaphore(f"pool{j}"))
                          for j in range(self.NPOOL)]
            self._pool_cnt = [0] * self.NPOOL
            self._rr_hw = 0
            self._rr_sw = 0
            self._rr_c = 0

    def mark(self):
        ep = self.epochs[-1]
        return (len(self.epochs) - 1, ep["cnt"], ep["gcnt"])

    def side_dma(self, out, in_, q="sync", at=None, joins=()):
        self._side_init()
        if q == "gpsimd":
            j = 10 + self._rr_sw
            self._rr_sw = (self._rr_sw + 1) % 6
        else:
            j = self._rr_hw
            self._rr_hw = (self._rr_hw + 1) % 10
        self._pool_cnt[j] += 16
        tok = (j, self._pool_cnt[j])
        if at is None:
            at = self.mark()
        ep_i, wt, wg = at
        pool = self._pool
        def fn(e, o=out, i=in_, tok_joins=tuple(joins)):
            for (jj, val) in tok_joins:
                e.wait_ge(pool[jj], val)
            return e.dma_start(out=o, in_=i)
        sem = self._pool[j]
        self.ops.append((len(self.epochs) - 1, q, _freeze(fn),
                         (None, None), "side", 16,
                         ("side", ep_i, wt, wg, sem)))
        return tok

    def side_compute(self, eng, fns, joins=(), at=None):
        self._side_init()
        j = 16 + self._rr_c
        self._rr_c = (self._rr_c + 1) % 8
        self._pool_cnt[j] += 1
        tok = (j, self._pool_cnt[j])
        if at is None:
            at = self.mark()
        ep_i, wt, wg = at
        pool = self._pool
        def fn(e, fns=tuple(fns), tok_joins=tuple(joins)):
            for (jj, val) in tok_joins:
                e.wait_ge(pool[jj], val)
            ins = None
            for f in fns:
                ins = f(e)
            return ins
        self.ops.append((len(self.epochs) - 1, eng, _freeze(fn),
                         (None, None), "side", 1,
                         ("side", ep_i, wt, wg, self._pool[j])))
        return tok

    def join(self, tok, eng):
        self._side_init()
        j, val = tok
        sem = self._pool[j]
        self.op(eng, lambda e, sem=sem, val=val: e.wait_ge(sem, val), inc=1)

    def join_many(self, toks, eng):
        self._side_init()
        pool = self._pool
        best = {}
        for (j, val) in toks:
            best[j] = max(best.get(j, 0), val)
        snap = tuple(best.items())
        def fn(e, snap=snap):
            ins = e.wait_ge(pool[0], 0)
            for j, val in snap:
                ins = e.wait_ge(pool[j], val)
            return ins
        self.op(eng, fn, inc=1)

    def side_fence(self, eng="vector"):
        self._side_init()
        snap = tuple((j, c) for j, c in enumerate(self._pool_cnt) if c)
        pool = self._pool
        def fn(e, snap=snap):
            ins = e.wait_ge(pool[0], 0)
            for j, c in snap:
                ins = e.wait_ge(pool[j], c)
            return ins
        self.op(eng, fn, inc=1)

    def finalize(self):
        nc = self.nc
        block = self.stack.enter_context(nc.Block())
        by_eng = {}
        for rec in self.ops:
            by_eng.setdefault(rec[1], []).append(rec)
        eps = self.epochs
        for engn, sub in by_eng.items():
            @getattr(block, engn)
            def _(eng, sub=sub):
                cur_ep = 0
                last_t, last_g = -1, -1
                pend = 0
                for (epi, _e, fn, (wt, wg), which, inc, role) in sub:
                    if which == "side":
                        _tag, sep_i, swt, swg, sem = role
                        if sep_i > 0:
                            prev = eps[sep_i - 1]
                            eng.wait_ge(prev["tick"], prev["cnt"])
                            if prev["gcnt"]:
                                eng.wait_ge(prev["gtick"], prev["gcnt"])
                        if swt > 0:
                            eng.wait_ge(eps[sep_i]["tick"], swt)
                        if swg > 0:
                            eng.wait_ge(eps[sep_i]["gtick"], swg)
                        fn(eng).then_inc(sem, inc)
                        continue
                    if epi != cur_ep:
                        prev = eps[epi - 1]
                        eng.wait_ge(prev["tick"], prev["cnt"])
                        if prev["gcnt"]:
                            eng.wait_ge(prev["gtick"], prev["gcnt"])
                        cur_ep = epi
                        last_t, last_g = -1, -1
                    ep = eps[epi]
                    if role in (None, "head"):
                        if wt > last_t:
                            eng.wait_ge(ep["tick"], wt)
                            last_t = wt
                        if wg > last_g and wg > 0:
                            eng.wait_ge(ep["gtick"], wg)
                            last_g = wg
                    ins = fn(eng)
                    if role in ("head", "mid"):
                        pend += inc
                    elif role == "end":
                        ins.then_inc(ep["tick"], pend + inc)
                        pend = 0
                    else:
                        ins.then_inc(ep["gtick"] if which == "g" else ep["tick"], inc)
        @block.sync
        def _(e):
            last = eps[-1]
            e.wait_ge(last["tick"], last["cnt"])
            if last["gcnt"]:
                e.wait_ge(last["gtick"], last["gcnt"])
            if hasattr(self, "_pool"):
                for j, c in enumerate(self._pool_cnt):
                    if c:
                        e.wait_ge(self._pool[j], c)
        self.stack.close()
        return self.nc



def _hilo(x64):
    hi = np.asarray(x64, np.float32)
    lo = np.asarray(x64 - hi.astype(np.float64), np.float32)
    return hi, lo


def host_constants():
    c = {}
    c["idn_bf"] = np.eye(128, dtype=BF)
    c["idn_f"] = np.eye(128, dtype=np.float32)
    q = np.arange(128)[:, None]
    k = np.arange(128)[None, :]
    c["tril_bf"] = np.where(k <= q, 0.0, NEG).astype(BF)
    c["triu_bf"] = np.where(k >= q, 0.0, NEG).astype(BF)
    half = np.arange(0, 64, 2, dtype=np.float64)
    inv_freq = 1.0 / 10000.0 ** (half / 64.0)
    t = np.arange(T, dtype=np.float64)
    fr = t[None, :] * inv_freq[:, None]
    freqs = np.concatenate([fr, fr], 0)
    base_scale = (half + 0.4 * 64.0) / (1.4 * 64.0)
    power = (t - T // 2) / 128.0
    sc = base_scale[:, None] ** power[None, :]
    scale = np.concatenate([sc, sc], 0)
    stack2 = lambda a: np.ascontiguousarray(np.concatenate([a, a], 0))
    for nm, arr in [("ctq", np.cos(freqs) * scale), ("stq", np.sin(freqs) * scale),
                    ("ctk", np.cos(freqs) / scale), ("stk", np.sin(freqs) / scale)]:
        hi, lo = _hilo(stack2(arr))
        c[nm + "_hi"], c[nm + "_lo"] = hi, lo
    r2t = np.zeros((128, 128), np.float64)
    for b in (0, 64):
        r2t[b + np.arange(32), b + np.arange(32) + 32] = 1.0
        r2t[b + np.arange(32) + 32, b + np.arange(32)] = -1.0
    c["r2t_bf"] = r2t.astype(BF)
    sel8 = np.zeros((8, 1024), np.float32)
    for e in range(8):
        sel8[e, e * 128:(e + 1) * 128] = 1.0
    c["sel8"] = sel8
    return c


def bias_panel(b):
    b = np.asarray(b, np.float32).ravel()
    n = len(b)
    pad = ((n + 127) // 128) * 128
    out = np.zeros(pad, np.float32)
    out[:n] = b
    return np.ascontiguousarray(out.reshape(-1, 128))


def host_prep(inputs):
    h = {}
    x = np.asarray(inputs["L0_attn_x"], np.float32)
    h["xfm"] = [np.ascontiguousarray(x[2 * i:2 * i + 2].reshape(NTOK, D).T)
                for i in range(NCORES)]
    h["W_attn"] = np.ascontiguousarray(inputs["L0_attn_W_attn"], np.float32)
    h["b_attn_p"] = bias_panel(inputs["L0_attn_b_attn"])
    h["W_proj"] = np.ascontiguousarray(inputs["L0_attn_W_proj"], np.float32)
    h["b_proj_p"] = bias_panel(inputs["L0_attn_b_proj"])
    for i in range(6):
        h[f"Wqkv{i}"] = np.ascontiguousarray(inputs["L0_ffn_Wqkv"][i], np.float32)
        h[f"Wo{i}"] = np.ascontiguousarray(inputs["L0_ffn_Wo"][i], np.float32)
        W1 = np.asarray(inputs["L0_ffn_W1"][i], np.float32)
        h[f"W1a{i}"] = np.ascontiguousarray(W1[:, :2730])
        h[f"W1g{i}"] = np.ascontiguousarray(W1[:, 2730:])
        h[f"W2{i}"] = np.ascontiguousarray(inputs["L0_ffn_W2"][i], np.float32)
        b1 = np.asarray(inputs["L0_ffn_b1"][i], np.float32)
        h[f"b1a_p{i}"] = bias_panel(b1[:2730])
        h[f"b1g_p{i}"] = bias_panel(b1[2730:])
        h[f"b2_p{i}"] = bias_panel(inputs["L0_ffn_b2"][i])
    qs = np.asarray(inputs["L0_ffn_q_scale"], np.float32)
    ks = np.asarray(inputs["L0_ffn_k_scale"], np.float32)
    h["qs2"] = np.ascontiguousarray(np.concatenate([qs.T, qs.T], 0))
    h["ks2"] = np.ascontiguousarray(np.concatenate([ks.T, ks.T], 0))
    qsr = np.roll(qs, 32, axis=1)
    ksr = np.roll(ks, 32, axis=1)
    h["qs2r"] = np.ascontiguousarray(np.concatenate([qsr.T, qsr.T], 0))
    h["ks2r"] = np.ascontiguousarray(np.concatenate([ksr.T, ksr.T], 0))
    h["ln_gT"] = np.ascontiguousarray(np.asarray(inputs["L0_ffn_ln_g"], np.float32).T)
    h["ln_bT"] = np.ascontiguousarray(np.asarray(inputs["L0_ffn_ln_b"], np.float32).T)
    h["W_dq"] = np.ascontiguousarray(inputs["L1_attn_W_dq"], np.float32)
    h["W_uq"] = np.ascontiguousarray(inputs["L1_attn_W_uq"], np.float32)
    h["W_dkvT"] = np.ascontiguousarray(np.asarray(inputs["L1_attn_W_dkv"], np.float32).T)
    h["W_uk"] = np.ascontiguousarray(inputs["L1_attn_W_uk"], np.float32)
    h["W_uv"] = np.ascontiguousarray(inputs["L1_attn_W_uv"], np.float32)
    h["W_oT"] = np.ascontiguousarray(np.asarray(inputs["L1_attn_W_o"], np.float32).T)
    h["wiT"] = np.ascontiguousarray(np.asarray(inputs["L1_ffn_wi"], np.float32).T)
    h["bi_p"] = bias_panel(inputs["L1_ffn_bi"])
    h["woT"] = np.ascontiguousarray(np.asarray(inputs["L1_ffn_wo"], np.float32).T)
    h["bo_p"] = bias_panel(inputs["L1_ffn_bo"])
    lA = np.asarray(inputs["L1_ffn_lora_A"], np.float32)
    lB = np.asarray(inputs["L1_ffn_lora_B"], np.float32)
    for g in range(4):
        ap = np.zeros((D, 64), np.float32)
        bp = np.zeros((64, 4096), np.float32)
        for j in range(2):
            e = 2 * g + j
            ap[:, 32 * j:32 * j + 16] = lA[e].T
            bp[32 * j:32 * j + 16, :] = lB[e].T
        h[f"Apack{g}"] = np.ascontiguousarray(ap)
        h[f"Bpack{g}"] = np.ascontiguousarray(bp)
    h["gate_wT"] = np.ascontiguousarray(np.asarray(inputs["L1_ffn_gate_w"], np.float32).T)
    h["gate_b"] = np.ascontiguousarray(np.asarray(inputs["L1_ffn_gate_b"], np.float32).reshape(1, 8))
    h.update(host_constants())
    return h



class Common:
    def __init__(self, pb, masks=False):
        self.pb = pb
        self.idn_bf = pb.sb(128, 128, BF16)
        pb.dma(self.idn_bf[:], pb.dram("idn_bf", [128, 128], BF16))
        self.idn_f = pb.sb(128, 128, F32)
        pb.dma(self.idn_f[:], pb.dram("idn_f", [128, 128], F32))
        self.ones_f = pb.sb(128, 1, F32)
        pb.op("vector", lambda e: e.memset(self.ones_f[:], 1.0))
        self.ones_row_f = pb.sb(1, 128, F32)
        pb.op("vector", lambda e: e.memset(self.ones_row_f[:], 1.0))
        self.ones_bf = pb.sb(128, 1, BF16)
        pb.op("vector", lambda e: e.memset(self.ones_bf[:], 1.0))
        self.ones_row_bf = pb.sb(1, 128, BF16)
        pb.op("vector", lambda e: e.memset(self.ones_row_bf[:], 1.0))
        self.consts = pb.sb(128, 4, F32)
        pb.op("vector", lambda e: e.memset(self.consts[:, 0:1], 1e-5))
        self.eps_ln = self.consts[:, 0:1]
        if masks:
            self.tril = pb.sb(128, 128, BF16)
            pb.dma(self.tril[:], pb.dram("tril_bf", [128, 128], BF16))
            self.triu = pb.sb(128, 128, BF16)
            pb.dma(self.triu[:], pb.dram("triu_bf", [128, 128], BF16))

    def panel(self, name, nch, scope=None):
        pb = self.pb
        d = pb.dram(name, [nch, 128], F32)
        out = (scope or pb).sb(128, nch, F32)
        with pb.scope() as sc:
            raw = sc.sb(nch, 128, F32)
            pb.dma(raw[:], d)
            pt = sc.ps(128, 128, F32)
            pb.op("tensor", lambda e: e.transpose(pt[0:128, 0:nch], raw[:], self.idn_f[0:nch, 0:nch]))
            pb.op("vector", lambda e: e.tensor_copy(out[:], pt[:, 0:nch]))
        return out


def emit_ln(pb, cm, x_tiles, W, out_tiles, g_col=None, b_col=None):
    with pb.scope() as sc:
        s1 = sc.ps(1, W, F32)
        s2 = sc.ps(1, W, F32)
        sq = sc.sb(128, 2 * W, F32)
        pb.chain_begin()
        for c in range(NCH):
            pb.op("tensor", lambda e, c=c: e.matmul(
                s1[:], cm.ones_f[:], x_tiles[c][:, 0:W], start=(c == 0), stop=(c == NCH - 1)))
        pb.chain_end()
        for half in range(4):
            for c4 in range(2):
                pb.op("scalar", lambda e, half=half, c4=c4: e.activation(
                    sq[:, c4 * W:(c4 + 1) * W], x_tiles[half * 2 + c4][:, 0:W], AF.Square))
            pb.chain_begin()
            for c4 in range(2):
                pb.op("tensor", lambda e, half=half, c4=c4: e.matmul(
                    s2[:], cm.ones_f[:], sq[:, c4 * W:(c4 + 1) * W],
                    start=(half == 0 and c4 == 0), stop=(half == 3 and c4 == 1)))
            pb.chain_end()
        mu = sc.sb(1, W, F32)
        var = sc.sb(1, W, F32)
        rstd = sc.sb(1, W, F32)
        pb.op("vector", lambda e: e.tensor_scalar_mul(mu[:], s1[:], 1.0 / D))
        pb.op("vector", lambda e: e.tensor_scalar_mul(var[:], s2[:], 1.0 / D))
        pb.op("scalar", lambda e: e.activation(rstd[:], mu[:], AF.Square))
        pb.op("vector", lambda e: e.tensor_sub(var[:], var[:], rstd[:]))
        pb.op("scalar", lambda e: e.activation(rstd[:], var[:], AF.Ln, bias=cm.eps_ln[0:1, 0:1]))
        pb.op("scalar", lambda e: e.activation(rstd[:], rstd[:], AF.Exp, scale=-0.5))
        pb.op("vector", lambda e: e.tensor_mul(mu[:], mu[:], rstd[:]))
        pbc = sc.ps(128, 2 * W, F32)
        pb.op("tensor", lambda e: e.matmul(pbc[:, 0:W], cm.ones_row_f[:], rstd[:], start=True, stop=True))
        pb.op("tensor", lambda e: e.matmul(pbc[:, W:2 * W], cm.ones_row_f[:], mu[:], start=True, stop=True))
        tmp = sc.sb(128, W, F32)
        for c in range(NCH):
            pb.op("vector", lambda e, c=c: e.tensor_mul(tmp[:], x_tiles[c][:, 0:W], pbc[:, 0:W]))
            if g_col is not None:
                pb.op("vector", lambda e: e.tensor_sub(tmp[:], tmp[:], pbc[:, W:2 * W]))
                pb.op("vector", lambda e, c=c: e.tensor_scalar(
                    out_tiles[c], tmp[:], g_col(c), b_col(c), ALU.mult, ALU.add))
            else:
                pb.op("vector", lambda e, c=c: e.tensor_sub(out_tiles[c], tmp[:], pbc[:, W:2 * W]))


def emit_proj(pb, cm, w_dram, n_in, n_out, x_sb, xw, cols, out_cb):
    nkc = (n_in + 127) // 128
    nmc = (n_out + 127) // 128
    with pb.scope() as sc:
        wcol = [sc.sb(128, nkc * 128, BF16) for _ in range(2)]
        pcol = sc.ps(128, 1024, F32)
        def wload(m):
            mw = min(128, n_out - m * 128)
            return pb.side_dma(
                wcol[m % 2][:, 0:nkc * mw].rearrange("p (k m) -> p k m", k=nkc),
                w_dram[:, m * 128:m * 128 + mw].rearrange("(k p) m -> p k m", p=128),
                q="gpsimd")
        toks = {0: wload(0)}
        units = [(m, ci) for m in range(nmc) for ci in range(len(cols))]
        etoks = {}
        def emit_chain(u):
            m, ci = units[u]
            mw = min(128, n_out - m * 128)
            off, W = cols[ci]
            if ci == 0:
                if m + 1 < nmc:
                    toks[m + 1] = wload(m + 1)
                pb.join(toks[m], "tensor")
            if u - 2 in etoks:
                pb.join(etoks[u - 2], "tensor")
            wc = wcol[m % 2]
            par = (u % 2) * 512
            pb.chain_begin()
            for k in range(nkc):
                pb.op("tensor", lambda e, wc=wc, k=k, off=off, W=W, mw=mw, par=par: e.matmul(
                    pcol[0:mw, par:par + W], wc[:, k * mw:(k + 1) * mw],
                    x_sb[:, k * xw + off:k * xw + off + W],
                    start=(k == 0), stop=(k == nkc - 1)))
            pb.chain_end()
        emit_chain(0)
        for u, (m, ci) in enumerate(units):
            if u + 1 < len(units):
                emit_chain(u + 1)
            mw = min(128, n_out - m * 128)
            eng, fns = out_cb(m, ci, pcol[:, (u % 2) * 512:(u % 2) * 512 + 512], mw)
            etoks[u] = pb.side_compute(eng, fns)
        return list(etoks.values())


def emit_causal_softmax_pv(pb, cm, psq, Psb, Pt, pst, nmax, ssum, rcp,
                           nk, v_lhsT, ctx_out, scale=1.0):
    pb.op("vector", lambda e: e.reduce_max(nmax[:], psq[:, 0:nk], axis=AX.X, negate=True))
    if scale != 1.0:
        pb.op("vector", lambda e: e.tensor_scalar_mul(nmax[:], nmax[:], scale))
    pb.op("scalar", lambda e: e.activation(
        Psb[:, 0:nk], psq[:, 0:nk], AF.Exp, scale=scale, bias=nmax[:, 0:1], accum_out=ssum[:, 0:1]))
    pb.op("vector", lambda e: e.reciprocal(rcp[:], ssum[:]))
    pb.op("vector", lambda e: e.tensor_scalar_mul(Psb[:, 0:nk], Psb[:, 0:nk], rcp[:, 0:1]))
    nkc = nk // 128
    for g0 in range(0, nkc, 4):
        gn = min(4, nkc - g0)
        pb.chain_begin()
        for j in range(gn):
            pb.op("tensor", lambda e, j=j, g0=g0: e.transpose(
                pst[:, j * 128:(j + 1) * 128],
                Psb[:, (g0 + j) * 128:(g0 + j + 1) * 128], cm.idn_bf[:]))
        pb.chain_end()
        pb.op("vector", lambda e, g0=g0, gn=gn: e.tensor_copy(
            Pt[:, g0 * 128:(g0 + gn) * 128], pst[:, 0:gn * 128]))



def emit_l0attn(pb, cm, xfm, res1, loc0):
    W_attn = pb.dram("W_attn", [D, 3 * D], F32)
    W_proj = pb.dram("W_proj", [D, D], F32)
    b_attn_c = cm.panel("b_attn_p", 24)
    b_proj_c = cm.panel("b_proj_p", 8)
    pb.op("vector", lambda e: e.tensor_scalar_mul(
        b_attn_c[:, 0:NCH], b_attn_c[:, 0:NCH], 0.125))

    for s in range(NSEQ):
        toff = s * T
        with pb.scope() as sA:
            qsb = sA.sb(128, NCH * T, BF16)
            ksb = sA.sb(128, NCH * T, BF16)
            vsb = sA.sb(128, NCH * T, BF16)
            ctx = sA.sb(128, NCH * T, BF16)
            with pb.scope() as sX:
                xbf = sX.sb(128, NCH * T, BF16)
                with pb.scope() as s1:
                    xt2 = [s1.sb(128, NCH * CW, F32) for _ in range(2)]
                    xfmr = xfm.rearrange("(c p) t -> p c t", p=128)
                    def xload(col):
                        t0 = toff + col * CW
                        return pb.side_dma(
                            xt2[col % 2][:].rearrange("p (c w) -> p c w", c=NCH),
                            xfmr[:, :, t0:t0 + CW])
                    toks = {0: xload(0)}
                    for col in range(T // CW):
                        if col + 1 < T // CW:
                            toks[col + 1] = xload(col + 1)
                        pb.join(toks[col], "tensor")
                        xa = xt2[col % 2]
                        xt = [xa[:, c * CW:(c + 1) * CW] for c in range(NCH)]
                        outs = [xbf[:, c * T + col * CW:c * T + col * CW + CW]
                                for c in range(NCH)]
                        emit_ln(pb, cm, xt, CW, outs)
                cols = [(col * CW, CW) for col in range(T // CW)]
                def qkv_cb(m, ci, ps, mw):
                    tgt = (qsb, ksb, vsb)[m // NCH]
                    c = m % NCH
                    off = c * T + ci * CW
                    sc_ = 0.125 if m < NCH else 1.0
                    return ("scalar", [lambda e, m=m, tgt=tgt, off=off, ps=ps, sc_=sc_:
                            e.activation(tgt[:, off:off + CW], ps[:, 0:CW], AF.Identity,
                                         scale=sc_, bias=b_attn_c[:, m:m + 1])])
                qtoks = emit_proj(pb, cm, W_attn, D, 3 * D, xbf, T, cols, qkv_cb)
                pb.join_many(qtoks, "tensor")
            with pb.scope() as s3:
                vtm = s3.sb(128, 16 * 64, BF16)
                psq = s3.ps(128, 2048, F32)
                pst = s3.ps(128, 1024, BF16)
                psv = s3.ps(128, 512, F32)
                Pt = s3.sb(128, 2048, BF16)
                Psb2 = [s3.sb(128, 2048, BF16) for _ in range(2)]
                nmax = s3.sb(128, 2, F32)
                ssum = s3.sb(128, 2, F32)
                rcp = s3.sb(128, 2, F32)
                for h in range(16):
                    c, p = h // 2, (h % 2) * 64
                    for g0 in range(0, 16, 4):
                        pb.chain_begin()
                        for j in range(4):
                            pb.op("tensor", lambda e, j=j, g0=g0, c=c, p=p: e.transpose(
                                pst[:, j * 64:(j + 1) * 64],
                                vsb[p:p + 64, c * T + (g0 + j) * 128:c * T + (g0 + j + 1) * 128],
                                cm.idn_bf[p:p + 64, p:p + 64]))
                        pb.chain_end()
                        pb.op("vector", lambda e, g0=g0: e.tensor_copy(
                            vtm[:, g0 * 64:(g0 + 4) * 64], pst[:, 0:256]))
                    smt = {}
                    def emit_qk(qi, c=c, p=p):
                        nk = 128 * (qi + 1)
                        q0 = qi * 128
                        if qi - 1 in smt:
                            pb.join(smt[qi - 1][0], "tensor")
                        pb.chain_begin()
                        pb.op("tensor", lambda e, q0=q0: e.matmul(
                            psq[:, q0:q0 + 128], cm.idn_bf[:], cm.tril[:],
                            start=True, stop=False, skip_group_check=True))
                        for ks_ in range(0, nk, 512):
                            w = min(512, nk - ks_)
                            diag = (ks_ + w == nk)
                            segs = [(ks_, w, True)] if not diag else (
                                ([(ks_, q0 - ks_, False)] if q0 > ks_ else []) + [(q0, 128, False)])
                            for (s0, sw, st_) in segs:
                                pb.op("tensor", lambda e, s0=s0, sw=sw, st_=st_, q0=q0, c=c, p=p: e.matmul(
                                    psq[:, s0:s0 + sw], qsb[p:p + 64, c * T + q0:c * T + q0 + 128],
                                    ksb[p:p + 64, c * T + s0:c * T + s0 + sw],
                                    start=st_, stop=True, skip_group_check=True))
                        pb.chain_end()
                    def emit_smax(qi):
                        nk = 128 * (qi + 1)
                        sl = qi % 2
                        Ps = Psb2[sl]
                        jm = (smt[qi - 2][0],) if qi - 2 in smt else ()
                        t_max = pb.side_compute("vector", [
                            lambda e, nk=nk, sl=sl: e.reduce_max(
                                nmax[:, sl:sl + 1], psq[:, 0:nk], axis=AX.X, negate=True)],
                            joins=jm)
                        jm2 = (smt[qi - 2][3],) if qi - 2 in smt else ()
                        t_exp = pb.side_compute("scalar", [
                            lambda e, nk=nk, sl=sl, Ps=Ps: e.activation(
                                Ps[:, 0:nk], psq[:, 0:nk], AF.Exp,
                                bias=nmax[:, sl:sl + 1], accum_out=ssum[:, sl:sl + 1])],
                            joins=(t_max,) + jm2)
                        jm3 = (smt[qi - 2][3],) if qi - 2 in smt else ()
                        t_rcp = pb.side_compute("vector", [
                            lambda e, sl=sl: e.reciprocal(rcp[:, sl:sl + 1], ssum[:, sl:sl + 1])],
                            joins=(t_exp,) + jm3)
                        t_ts = pb.side_compute("vector", [
                            lambda e, nk=nk, sl=sl, Ps=Ps: e.tensor_scalar_mul(
                                Ps[:, 0:nk], Ps[:, 0:nk], rcp[:, sl:sl + 1])],
                            joins=(t_rcp,))
                        smt[qi] = (t_exp, t_max, t_rcp, t_ts)
                    def emit_ptpv(qi, c=c, p=p):
                        nk = 128 * (qi + 1)
                        q0 = qi * 128
                        sl = qi % 2
                        Ps = Psb2[sl]
                        pb.join(smt[qi][3], "tensor")
                        nkc = qi + 1
                        for g0_ in range(0, nkc, 4):
                            gn = min(4, nkc - g0_)
                            pb.chain_begin()
                            for j in range(gn):
                                pb.op("tensor", lambda e, j=j, g0_=g0_, Ps=Ps: e.transpose(
                                    pst[:, j * 128:(j + 1) * 128],
                                    Ps[:, (g0_ + j) * 128:(g0_ + j + 1) * 128], cm.idn_bf[:]))
                            pb.chain_end()
                            pb.op("vector", lambda e, g0_=g0_, gn=gn: e.tensor_copy(
                                Pt[:, g0_ * 128:(g0_ + gn) * 128], pst[:, 0:gn * 128]))
                        pb.chain_begin()
                        for kc in range(nkc):
                            pb.op("tensor", lambda e, kc=kc, nkc=nkc: e.matmul(
                                psv[0:64, 0:128], vtm[:, kc * 64:kc * 64 + 64],
                                Pt[:, kc * 128:kc * 128 + 128],
                                start=(kc == 0), stop=(kc == nkc - 1)))
                        pb.chain_end()
                        pb.op("vector", lambda e, q0=q0, c=c, p=p: e.tensor_copy(
                            ctx[p:p + 64, c * T + q0:c * T + q0 + 128], psv[0:64, 0:128]))
                    emit_qk(0)
                    for qi in range(16):
                        emit_smax(qi)
                        if qi + 1 < 16:
                            emit_qk(qi + 1)
                        emit_ptpv(qi)
            with pb.scope() as s4:
                rt_all = [s4.sb(128, NCH * CW, F32) for _ in range(2)]
                ln_all = s4.sb(128, NCH * CW, F32)
                wsb = s4.sb(128, NCH * NCH * 128, BF16)
                pcol = s4.ps(128, 512, F32)
                wtok = pb.side_dma(wsb[:].rearrange("p (k m) -> p k m", k=NCH),
                                   W_proj.rearrange("(k p) m -> p k m", p=128), q="gpsimd")
                pb.join(wtok, "tensor")
                xfmr = xfm.rearrange("(c p) t -> p c t", p=128)
                res1r = res1.rearrange("(c p) t -> p c t", p=128)
                loc0r = loc0.rearrange("(c p) t -> p c t", p=128)
                def rload(col, joins):
                    t0 = toff + col * CW
                    return pb.side_dma(
                        rt_all[col % 2][:].rearrange("p (c w) -> p c w", c=NCH),
                        xfmr[:, :, t0:t0 + CW], joins=joins)
                rtoks = {0: rload(0, ())}
                stoks = {}
                ltok = [None]
                for col in range(T // CW):
                    t0 = toff + col * CW
                    if col + 1 < T // CW:
                        rtoks[col + 1] = rload(col + 1, (stoks[col - 1],) if col >= 1 else ())
                    ra = rt_all[col % 2]
                    pb.join(rtoks[col], "vector")
                    for m in range(NCH):
                        pb.chain_begin()
                        for k in range(NCH):
                            pb.op("tensor", lambda e, k=k, m=m, col=col: e.matmul(
                                pcol[:, 0:CW], wsb[:, k * 1024 + m * 128:k * 1024 + (m + 1) * 128],
                                ctx[:, k * T + col * CW:k * T + col * CW + CW],
                                start=(k == 0), stop=(k == NCH - 1)))
                        pb.chain_end()
                        pb.op("scalar", lambda e, m=m: e.activation(
                            pcol[:, 0:CW], pcol[:, 0:CW], AF.Identity,
                            bias=b_proj_c[:, m:m + 1]))
                        pb.op("vector", lambda e, m=m, ra=ra: e.tensor_add(
                            ra[:, m * CW:(m + 1) * CW], ra[:, m * CW:(m + 1) * CW],
                            pcol[:, 0:CW]))
                    stoks[col] = pb.side_dma(res1r[:, :, t0:t0 + CW],
                                             ra[:].rearrange("p (c w) -> p c w", c=NCH))
                    if ltok[0] is not None:
                        pb.join(ltok[0], "vector")
                    xt = [ra[:, c * CW:(c + 1) * CW] for c in range(NCH)]
                    outs = [ln_all[:, c * CW:(c + 1) * CW] for c in range(NCH)]
                    emit_ln(pb, cm, xt, CW, outs)
                    ltok[0] = pb.side_dma(loc0r[:, :, t0:t0 + CW],
                                          ln_all[:].rearrange("p (c w) -> p c w", c=NCH))


def build_l0attn():
    pb = PB("l0attn")
    cm = Common(pb, masks=True)
    xfm = pb.dram("xfm", [D, NTOK], F32)
    res1 = pb.dram("res1", [D, NTOK], F32, kind="ExternalOutput")
    loc0 = pb.dram("loc0", [D, NTOK], F32, kind="ExternalOutput")
    emit_l0attn(pb, cm, xfm, res1, loc0)
    return pb



def emit_l0ffn_layer(pb, cm, li, loc_in, loc_mid, m_prod, loc_out):
    sfx = str(li)
    Wqkv = pb.dram("Wqkv" + sfx, [D, 3 * D], F32)
    Wo = pb.dram("Wo" + sfx, [D, D], F32)
    W1a = pb.dram("W1a" + sfx, [D, 2730], F32)
    W1g = pb.dram("W1g" + sfx, [D, 2730], F32)
    W2d = pb.dram("W2" + sfx, [2730, D], F32)
    L = _Scope(pb)
    L.__enter__()
    pb.side_fence()
    qs2 = L.sb(128, 6, F32)
    pb.dma(qs2[:], pb.dram("qs2", [128, 6], F32))
    ks2 = L.sb(128, 6, F32)
    pb.dma(ks2[:], pb.dram("ks2", [128, 6], F32))
    qs2r = L.sb(128, 6, F32)
    pb.dma(qs2r[:], pb.dram("qs2r", [128, 6], F32))
    ks2r = L.sb(128, 6, F32)
    pb.dma(ks2r[:], pb.dram("ks2r", [128, 6], F32))
    ln_g = L.sb(128, NCH * 6, F32)
    pb.dma(ln_g[:].rearrange("p (c l) -> p c l", c=NCH),
           pb.dram("ln_gT", [D, 6], F32).rearrange("(c p) l -> p c l", p=128))
    ln_b = L.sb(128, NCH * 6, F32)
    pb.dma(ln_b[:].rearrange("p (c l) -> p c l", c=NCH),
           pb.dram("ln_bT", [D, 6], F32).rearrange("(c p) l -> p c l", p=128))
    b1a_c = cm.panel("b1a_p" + sfx, 22, scope=L)
    b1g_c = cm.panel("b1g_p" + sfx, 22, scope=L)
    b2_c = cm.panel("b2_p" + sfx, 8, scope=L)
    r2t = L.sb(128, 128, BF16)
    pb.dma(r2t[:], pb.dram("r2t_bf", [128, 128], BF16))

    qs_col = qs2[:, li:li + 1]
    ks_col = ks2[:, li:li + 1]
    qsr_col = qs2r[:, li:li + 1]
    ksr_col = ks2r[:, li:li + 1]
    g_cols = ln_g
    b_cols = ln_b
    tabs = {}
    for nm in ("ctq", "stq", "ctk", "stk"):
        tabs[nm] = L.sb(128, T, BF16)
    with pb.scope() as sc:
        base = sc.sb(128, T, F32)
        hilo = [sc.sb(128, 2 * T, F32) for _ in range(2)]
        names = [("ctq", qs_col, True), ("stq", qsr_col, True),
                 ("ctk", ks_col, False), ("stk", ksr_col, False)]
        def tload(i):
            nm = names[i][0]
            b = hilo[i % 2]
            t1 = pb.side_dma(b[:, 0:T], pb.dram(nm + "_hi", [128, T], F32))
            t2 = pb.side_dma(b[:, T:2 * T], pb.dram(nm + "_lo", [128, T], F32))
            return (t1, t2)
        toks = {0: tload(0)}
        for i, (nm, scol, fold8) in enumerate(names):
            if i + 1 < 4:
                toks[i + 1] = tload(i + 1)
            pb.join_many(toks[i], "vector")
            b = hilo[i % 2]
            pb.op("vector", lambda e, b=b: e.tensor_add(base[:], b[:, 0:T], b[:, T:2 * T]))
            pb.op("vector", lambda e, nm=nm, scol=scol, f8=fold8: e.tensor_scalar(
                tabs[nm][:], base[:], scol[:, 0:1], 8.0 if f8 else None,
                ALU.mult, ALU.mult if f8 else ALU.bypass))

    for s in range(NSEQ):
        toff = s * T
        with pb.scope() as sA:
            qsb = sA.sb(128, NCH * T, BF16)
            ksb = sA.sb(128, NCH * T, BF16)
            vsb = sA.sb(128, NCH * T, BF16)
            ctx = sA.sb(128, NCH * T, BF16)
            with pb.scope() as sX:
                xbf = sX.sb(128, NCH * T, BF16)
                btoks = []
                for c in range(NCH):
                    if c < 4:
                        btoks.append(pb.side_dma(
                            xbf[:, c * T:(c + 1) * T],
                            loc_in[c * 128:(c + 1) * 128, toff:toff + T], q="gpsimd"))
                    else:
                        pb.op("vector", lambda e, c=c: e.memset(xbf[:, c * T:c * T + 1], 0.0))
                        btoks.append(pb.side_dma(
                            xbf[:, c * T + 1:(c + 1) * T],
                            loc_in[c * 128:(c + 1) * 128, toff:toff + T - 1], q="gpsimd"))
                pb.join_many(btoks, "tensor")
                cols = [(col * CW, CW) for col in range(T // CW)]
                def qkv_cb(m, ci, ps, mw):
                    tgt = (qsb, ksb, vsb)[m // NCH]
                    c = m % NCH
                    off = c * T + ci * CW
                    return ("scalar", [lambda e, tgt=tgt, off=off, ps=ps: e.activation(
                        tgt[:, off:off + CW], ps[:, 0:CW], AF.Identity)])
                qtoks = emit_proj(pb, cm, Wqkv, D, 3 * D, xbf, T, cols, qkv_cb)
                pb.join_many(qtoks, "tensor")
            with pb.scope() as s3:
                vtm = s3.sb(128, 16 * 64, BF16)
                psc = s3.ps(128, 2048, F32)
                pst = s3.ps(128, 1024, BF16)
                psv = s3.ps(128, 512, F32)
                psr = s3.ps(1, 512, F32)
                rbps = s3.ps(128, 512, F32)
                rot = rbps
                ssr = s3.sb(1, T, F32)
                rcpr = s3.sb(1, T, BF16)
                qn = s3.sb(128, T, BF16)
                sqb = s3.sb(128, T, BF16)
                t1 = s3.sb(128, T, BF16)
                t2 = s3.sb(128, T, BF16)
                Pt = s3.sb(128, 1536, BF16)
                Psb2 = [s3.sb(128, 1536, BF16) for _ in range(2)]
                nmax = s3.sb(128, 8, F32)
                ssum = s3.sb(128, 8, F32)
                rcp = s3.sb(128, 8, F32)
                for h in range(16):
                    c, p = h // 2, (h % 2) * 64
                    for tsb, ct_, st_ in [(qsb, tabs["ctq"], tabs["stq"]),
                                          (ksb, tabs["ctk"], tabs["stk"])]:
                        th = tsb[p:p + 64, c * T:(c + 1) * T]
                        pb.op("scalar", lambda e, th=th, p=p: e.activation(
                            sqb[p:p + 64, 0:T], th, AF.Square))
                        for w in range(T // CW):
                            pb.op("tensor", lambda e, w=w, p=p: e.matmul(
                                psr[:, 0:CW], cm.ones_bf[p:p + 64, 0:1],
                                sqb[p:p + 64, w * CW:(w + 1) * CW], start=True, stop=True))
                            pb.op("vector", lambda e, w=w: e.tensor_copy(
                                ssr[:, w * CW:(w + 1) * CW], psr[:, 0:CW]))
                        pb.op("scalar", lambda e: e.activation(ssr[:], ssr[:], AF.Sqrt))
                        pb.op("vector", lambda e: e.reciprocal(ssr[:], ssr[:]))
                        pb.op("vector", lambda e: e.tensor_copy(rcpr[:], ssr[:]))
                        for w in range(T // CW):
                            pb.op("tensor", lambda e, w=w, p=p: e.matmul(
                                rbps[p:p + 64, 0:CW], cm.ones_row_bf[0:1, p:p + 64],
                                rcpr[:, w * CW:(w + 1) * CW], start=True, stop=True))
                            pb.op("vector", lambda e, w=w, th=th, p=p: e.tensor_mul(
                                qn[p:p + 64, w * CW:(w + 1) * CW],
                                th[:, w * CW:(w + 1) * CW], rbps[p:p + 64, 0:CW]))
                        pb.op("vector", lambda e, p=p, ct_=ct_: e.tensor_mul(
                            t1[p:p + 64, 0:T], qn[p:p + 64, 0:T], ct_[p:p + 64, :]))
                        for w in range(T // CW):
                            pb.op("tensor", lambda e, w=w, p=p: e.matmul(
                                rot[p:p + 64, 0:CW], r2t[p:p + 64, p:p + 64],
                                qn[p:p + 64, w * CW:(w + 1) * CW], start=True, stop=True))
                            pb.op("vector", lambda e, w=w, st_=st_, p=p: e.tensor_mul(
                                t2[p:p + 64, w * CW:(w + 1) * CW], rot[p:p + 64, 0:CW],
                                st_[p:p + 64, w * CW:(w + 1) * CW]))
                        pb.op("vector", lambda e, th=th, p=p: e.tensor_add(
                            th, t1[p:p + 64, 0:T], t2[p:p + 64, 0:T]))
                    for g0 in range(0, 16, 4):
                        pb.chain_begin()
                        for j in range(4):
                            pb.op("tensor", lambda e, j=j, g0=g0, c=c, p=p: e.transpose(
                                pst[:, j * 64:(j + 1) * 64],
                                vsb[p:p + 64, c * T + (g0 + j) * 128:c * T + (g0 + j + 1) * 128],
                                cm.idn_bf[p:p + 64, p:p + 64]))
                        pb.chain_end()
                        pb.op("vector", lambda e, g0=g0: e.tensor_copy(
                            vtm[:, g0 * 64:(g0 + 4) * 64], pst[:, 0:256]))
                    def ginfo(g):
                        info = []
                        for u in range(4):
                            qi = 4 * g + u
                            kb = 128 * (qi - 2)
                            blocks = [(b, m_) for (b, m_) in
                                      [(kb, "triu"), (kb + 128, None), (kb + 256, "tril")]
                                      if b >= 0]
                            info.append((qi, blocks))
                        off = [0]
                        for u, (qi, blocks) in enumerate(info):
                            off.append(off[-1] + len(blocks))
                        return info, off
                    gsm = {}
                    def emit_qk_g(g, c=c, p=p):
                        info, off = ginfo(g)
                        if g - 1 in gsm:
                            pb.join(gsm[g - 1][0], "tensor")
                        pb.chain_begin()
                        for u, (qi, blocks) in enumerate(info):
                            q0 = qi * 128
                            k0 = blocks[0][0]
                            masked = [(j, m_) for j, (b, m_) in enumerate(blocks) if m_]
                            for mi, (j, m_) in enumerate(masked):
                                mk = cm.triu if m_ == "triu" else cm.tril
                                pb.op("tensor", lambda e, u=u, j=j, mk=mk, first=(mi == 0): e.matmul(
                                    psc[:, u * 512 + j * 128:u * 512 + (j + 1) * 128],
                                    cm.idn_bf[:], mk[:],
                                    start=first, stop=False, skip_group_check=True))
                            for j in range(len(blocks)):
                                pb.op("tensor", lambda e, u=u, j=j, q0=q0, k0=k0, c=c, p=p: e.matmul(
                                    psc[:, u * 512 + j * 128:u * 512 + (j + 1) * 128],
                                    qsb[p:p + 64, c * T + q0:c * T + q0 + 128],
                                    ksb[p:p + 64, c * T + k0 + j * 128:c * T + k0 + (j + 1) * 128],
                                    start=False, stop=True, skip_group_check=True))
                        pb.chain_end()
                    def emit_smax_g(g):
                        info, off = ginfo(g)
                        sl = g % 2
                        Ps = Psb2[sl]
                        fns = []
                        for u, (qi, blocks) in enumerate(info):
                            kw = 128 * len(blocks)
                            fns.append(lambda e, u=u, kw=kw, sl=sl: e.reduce_max(
                                nmax[:, sl * 4 + u:sl * 4 + u + 1],
                                psc[:, u * 512:u * 512 + kw], axis=AX.X, negate=True))
                        jm = (gsm[g - 2][0],) if g - 2 in gsm else ()
                        t_max = pb.side_compute("vector", fns, joins=jm)
                        fns = []
                        for u, (qi, blocks) in enumerate(info):
                            kw = 128 * len(blocks)
                            fns.append(lambda e, u=u, kw=kw, o=off[u], sl=sl, Ps=Ps: e.activation(
                                Ps[:, o * 128:o * 128 + kw], psc[:, u * 512:u * 512 + kw],
                                AF.Exp, bias=nmax[:, sl * 4 + u:sl * 4 + u + 1],
                                accum_out=ssum[:, sl * 4 + u:sl * 4 + u + 1]))
                        jm2 = (gsm[g - 2][3],) if g - 2 in gsm else ()
                        t_exp = pb.side_compute("scalar", fns, joins=(t_max,) + jm2)
                        t_rcp = pb.side_compute("vector", [
                            lambda e, sl=sl: e.reciprocal(rcp[:, sl * 4:sl * 4 + 4],
                                                          ssum[:, sl * 4:sl * 4 + 4])],
                            joins=(t_exp,) + jm2)
                        fns = []
                        for u, (qi, blocks) in enumerate(info):
                            kw = 128 * len(blocks)
                            fns.append(lambda e, u=u, kw=kw, o=off[u], sl=sl, Ps=Ps: e.tensor_scalar_mul(
                                Ps[:, o * 128:o * 128 + kw], Ps[:, o * 128:o * 128 + kw],
                                rcp[:, sl * 4 + u:sl * 4 + u + 1]))
                        t_ts = pb.side_compute("vector", fns, joins=(t_rcp,))
                        gsm[g] = (t_exp, t_max, t_rcp, t_ts)
                    def emit_ptpv_g(g, c=c, p=p):
                        info, off = ginfo(g)
                        sl = g % 2
                        Ps = Psb2[sl]
                        pb.join(gsm[g][3], "tensor")
                        nblk = off[-1]
                        for b0 in range(0, nblk, 8):
                            bn = min(8, nblk - b0)
                            pb.chain_begin()
                            for j in range(bn):
                                pb.op("tensor", lambda e, j=j, b0=b0, Ps=Ps: e.transpose(
                                    pst[:, j * 128:(j + 1) * 128],
                                    Ps[:, (b0 + j) * 128:(b0 + j + 1) * 128],
                                    cm.idn_bf[:]))
                            pb.chain_end()
                            pb.op("vector", lambda e, b0=b0, bn=bn: e.tensor_copy(
                                Pt[:, b0 * 128:(b0 + bn) * 128], pst[:, 0:bn * 128]))
                        pb.chain_begin()
                        for u, (qi, blocks) in enumerate(info):
                            for j, (b, m_) in enumerate(blocks):
                                kc = b // 128
                                pb.op("tensor", lambda e, u=u, j=j, kc=kc, o=off[u],
                                      nb=len(blocks): e.matmul(
                                    psv[0:64, u * 128:(u + 1) * 128],
                                    vtm[:, kc * 64:kc * 64 + 64],
                                    Pt[:, (o + j) * 128:(o + j + 1) * 128],
                                    start=(j == 0), stop=(j == nb - 1)))
                        pb.chain_end()
                        pb.op("vector", lambda e, g=g, c=c, p=p: e.tensor_copy(
                            ctx[p:p + 64, c * T + g * 512:c * T + (g + 1) * 512],
                            psv[0:64, 0:512]))
                    emit_qk_g(0)
                    for g in range(4):
                        emit_smax_g(g)
                        if g + 1 < 4:
                            emit_qk_g(g + 1)
                        emit_ptpv_g(g)
            with pb.scope() as s4:
                rt_all = [s4.sb(128, NCH * CW, F32) for _ in range(2)]
                wsb = s4.sb(128, NCH * NCH * 128, BF16)
                pcol = s4.ps(128, 512, F32)
                wtok = pb.side_dma(wsb[:].rearrange("p (k m) -> p k m", k=NCH),
                                   Wo.rearrange("(k p) m -> p k m", p=128), q="gpsimd")
                pb.join(wtok, "tensor")
                locr = loc_in.rearrange("(c p) t -> p c t", p=128)
                lmidr = loc_mid.rearrange("(c p) t -> p c t", p=128)
                def rload(col, joins):
                    t0 = toff + col * CW
                    return pb.side_dma(
                        rt_all[col % 2][:].rearrange("p (c w) -> p c w", c=NCH),
                        locr[:, :, t0:t0 + CW], joins=joins)
                rtoks = {0: rload(0, ())}
                stoks = {}
                for col in range(T // CW):
                    t0 = toff + col * CW
                    if col + 1 < T // CW:
                        rtoks[col + 1] = rload(col + 1, (stoks[col - 1],) if col >= 1 else ())
                    ra = rt_all[col % 2]
                    pb.join(rtoks[col], "vector")
                    for m in range(NCH):
                        pb.chain_begin()
                        for k in range(NCH):
                            pb.op("tensor", lambda e, k=k, m=m, col=col: e.matmul(
                                pcol[:, 0:CW], wsb[:, k * 1024 + m * 128:k * 1024 + (m + 1) * 128],
                                ctx[:, k * T + col * CW:k * T + col * CW + CW],
                                start=(k == 0), stop=(k == NCH - 1)))
                        pb.chain_end()
                        pb.op("vector", lambda e, m=m, ra=ra: e.tensor_add(
                            ra[:, m * CW:(m + 1) * CW], ra[:, m * CW:(m + 1) * CW],
                            pcol[:, 0:CW]))
                    stoks[col] = pb.side_dma(lmidr[:, :, t0:t0 + CW],
                                             ra[:].rearrange("p (c w) -> p c w", c=NCH))
        with pb.scope() as sB:
            xbf2 = sB.sb(128, NCH * T, BF16)
            pb.side_fence()
            with pb.scope() as s5:
                xt2 = [s5.sb(128, NCH * CW, F32) for _ in range(2)]
                lmidr = loc_mid.rearrange("(c p) t -> p c t", p=128)
                def sload(col):
                    t0 = toff + col * CW
                    b = xt2[col % 2]
                    t1 = pb.side_dma(b[:].rearrange("p (c w) -> p c w", c=NCH)[:, 0:4, :],
                                     lmidr[:, 0:4, t0:t0 + CW])
                    if col > 0:
                        t2 = pb.side_dma(b[:].rearrange("p (c w) -> p c w", c=NCH)[:, 4:8, :],
                                         lmidr[:, 4:8, t0 - 1:t0 - 1 + CW])
                    else:
                        for c in range(4, NCH):
                            pb.op("vector", lambda e, c=c, b=b: e.memset(
                                b[:, c * CW:c * CW + 1], 0.0))
                        t2 = pb.side_dma(
                            b[:].rearrange("p (c w) -> p c w", c=NCH)[:, 4:8, 1:CW],
                            lmidr[:, 4:8, t0:t0 + CW - 1])
                    return (t1, t2)
                toks = {0: sload(0)}
                for col in range(T // CW):
                    if col + 1 < T // CW:
                        toks[col + 1] = sload(col + 1)
                    pb.join_many(toks[col], "tensor")
                    xa = xt2[col % 2]
                    xt = [xa[:, c * CW:(c + 1) * CW] for c in range(NCH)]
                    outs = [xbf2[:, c * T + col * CW:c * T + col * CW + CW]
                            for c in range(NCH)]
                    emit_ln(pb, cm, xt, CW, outs,
                            g_col=lambda cc: g_cols[:, cc * 6 + li:cc * 6 + li + 1],
                            b_col=lambda cc: b_cols[:, cc * 6 + li:cc * 6 + li + 1])
            with pb.scope() as s6:
                wca = [s6.sb(128, NCH * 128, BF16) for _ in range(2)]
                wcg = [s6.sb(128, NCH * 128, BF16) for _ in range(2)]
                mtl2 = [s6.sb(128, T, BF16) for _ in range(2)]
                psa = s6.ps(128, 1024, F32)
                psg = s6.ps(128, 1024, F32)
                gel = s6.sb(128, 1024, BF16)
                asb = s6.sb(128, 1024, BF16)
                def w1load(m):
                    mw = min(128, 2730 - m * 128)
                    t1 = pb.side_dma(
                        wca[m % 2][:, 0:NCH * mw].rearrange("p (k m) -> p k m", k=NCH),
                        W1a[:, m * 128:m * 128 + mw].rearrange("(k p) m -> p k m", p=128),
                        q="gpsimd")
                    t2 = pb.side_dma(
                        wcg[m % 2][:, 0:NCH * mw].rearrange("p (k m) -> p k m", k=NCH),
                        W1g[:, m * 128:m * 128 + mw].rearrange("(k p) m -> p k m", p=128),
                        q="gpsimd")
                    return (t1, t2)
                wtoks = {0: w1load(0)}
                mstoks = {}
                atoks = {}
                mtoks_ = {}
                ncol = T // CW
                units = [(m, col) for m in range(22) for col in range(ncol)]
                def emit_chain(u):
                    m, col = units[u]
                    mw = min(128, 2730 - m * 128)
                    if col == 0:
                        if m + 1 < 22:
                            wtoks[m + 1] = w1load(m + 1)
                        pb.join_many(wtoks[m], "tensor")
                    if u - 2 in atoks:
                        pb.join(atoks[u - 2], "tensor")
                    wcam, wcgm = wca[m % 2], wcg[m % 2]
                    par = (u % 2) * 512
                    pb.chain_begin()
                    for k in range(NCH):
                        pb.op("tensor", lambda e, k=k, wcam=wcam, mw=mw, col=col, par=par: e.matmul(
                            psa[0:mw, par:par + CW], wcam[:, k * mw:(k + 1) * mw],
                            xbf2[:, k * T + col * CW:k * T + col * CW + CW],
                            start=(k == 0), stop=(k == NCH - 1)))
                    for k in range(NCH):
                        pb.op("tensor", lambda e, k=k, wcgm=wcgm, mw=mw, col=col, par=par: e.matmul(
                            psg[0:mw, par:par + CW], wcgm[:, k * mw:(k + 1) * mw],
                            xbf2[:, k * T + col * CW:k * T + col * CW + CW],
                            start=(k == 0), stop=(k == NCH - 1)))
                    pb.chain_end()
                emit_chain(0)
                for u, (m, col) in enumerate(units):
                    if u + 1 < len(units):
                        emit_chain(u + 1)
                    mw = min(128, 2730 - m * 128)
                    mta = mtl2[m % 2]
                    par = (u % 2) * 512
                    gpar = (u % 2) * 512
                    joins = []
                    if col == 0 and m >= 2:
                        joins.append(mstoks[m - 2])
                    if u - 2 in mtoks_:
                        joins.append(mtoks_[u - 2])
                    atoks[u] = pb.side_compute("scalar", [
                        lambda e, m=m, mw=mw, par=par, gpar=gpar: e.activation(
                            gel[0:mw, gpar:gpar + CW], psg[0:mw, par:par + CW],
                            AF.Gelu, bias=b1g_c[0:mw, m:m + 1]),
                        lambda e, m=m, mw=mw, par=par, gpar=gpar: e.activation(
                            asb[0:mw, gpar:gpar + CW], psa[0:mw, par:par + CW],
                            AF.Identity, bias=b1a_c[0:mw, m:m + 1])],
                        joins=tuple(joins))
                    mtoks_[u] = pb.side_compute("vector", [
                        lambda e, mw=mw, mta=mta, col=col, gpar=gpar: e.tensor_mul(
                            mta[0:mw, col * CW:col * CW + CW],
                            gel[0:mw, gpar:gpar + CW], asb[0:mw, gpar:gpar + CW])],
                        joins=(atoks[u],))
                    if col == ncol - 1:
                        mstoks[m] = pb.side_dma(
                            m_prod[m * 128:m * 128 + mw, toff:toff + T],
                            mta[0:mw, 0:T],
                            joins=tuple(mtoks_[u - cc] for cc in range(ncol)))
            with pb.scope() as s7:
                pb.side_fence()
                w2sb = s7.sb(128, 22 * D, BF16)
                w2toks = []
                for kc in range(22):
                    kw = min(128, 2730 - kc * 128)
                    w2toks.append(pb.side_dma(
                        w2sb[0:kw, kc * D:(kc + 1) * D],
                        W2d[kc * 128:kc * 128 + kw, :], q="gpsimd"))
                pb.join_many(w2toks, "tensor")
                mm2 = [s7.sb(128, 512, BF16) for _ in range(2)]
                lt2 = [s7.sb(128, NCH * CW, F32) for _ in range(2)]
                out_all = s7.sb(128, NCH * CW, F32)
                pso = [s7.ps(128, 512, F32) for _ in range(NCH)]
                lmidr = loc_mid.rearrange("(c p) t -> p c t", p=128)
                loutr = loc_out.rearrange("(c p) t -> p c t", p=128)
                ncol = T // CW
                def mload(col, kc):
                    kw = min(128, 2730 - kc * 128)
                    t0 = toff + col * CW
                    return pb.side_dma(mm2[kc % 2][0:kw, 0:CW],
                                       m_prod[kc * 128:kc * 128 + kw, t0:t0 + CW])
                def ltload(col):
                    t0 = toff + col * CW
                    return pb.side_dma(
                        lt2[col % 2][:].rearrange("p (c w) -> p c w", c=NCH),
                        lmidr[:, :, t0:t0 + CW])
                mtoks = {(0, 0): mload(0, 0)}
                lttoks = {0: ltload(0)}
                otok = [None]
                for col in range(ncol):
                    t0 = toff + col * CW
                    if col + 1 < ncol:
                        lttoks[col + 1] = ltload(col + 1)
                    for kc in range(22):
                        kw = min(128, 2730 - kc * 128)
                        if kc + 1 < 22:
                            mtoks[(col, kc + 1)] = mload(col, kc + 1)
                        elif col + 1 < ncol:
                            mtoks[(col + 1, 0)] = mload(col + 1, 0)
                        pb.join(mtoks[(col, kc)], "tensor")
                        mb = mm2[kc % 2]
                        pb.chain_begin()
                        for dc in range(NCH):
                            pb.op("tensor", lambda e, kc=kc, kw=kw, dc=dc, mb=mb: e.matmul(
                                pso[dc][:, 0:CW], w2sb[0:kw, kc * D + dc * 128:kc * D + (dc + 1) * 128],
                                mb[0:kw, 0:CW], start=(kc == 0), stop=(kc == 21)))
                        pb.chain_end()
                    pb.join(lttoks[col], "vector")
                    if otok[0] is not None:
                        pb.join(otok[0], "vector")
                    la = lt2[col % 2]
                    for dc in range(NCH):
                        pb.op("scalar", lambda e, dc=dc: e.activation(
                            pso[dc][:, 0:CW], pso[dc][:, 0:CW], AF.Identity, bias=b2_c[:, dc:dc + 1]))
                        pb.op("vector", lambda e, dc=dc, la=la: e.tensor_add(
                            out_all[:, dc * CW:(dc + 1) * CW], la[:, dc * CW:(dc + 1) * CW],
                            pso[dc][:, 0:CW]))
                    otok[0] = pb.side_dma(loutr[:, :, t0:t0 + CW],
                                          out_all[:].rearrange("p (c w) -> p c w", c=NCH))
    L.__exit__(None, None, None)


def build_l0ffn_layer(li):
    pb = PB(f"l0ffn{li}")
    cm = Common(pb, masks=True)
    loc_in = pb.dram("loc_in", [D, NTOK], F32)
    loc_out = pb.dram("loc_out", [D, NTOK], F32, kind="ExternalOutput")
    loc_mid = pb.dram("loc_mid", [D, NTOK], F32, kind="Internal")
    m_prod = pb.dram("m_prod", [2816, NTOK], BF16, kind="Internal")
    emit_l0ffn_layer(pb, cm, li, loc_in, loc_mid, m_prod, loc_out)
    return pb



def emit_l1attn(pb, cm, res1, loc6, res2, res3):
    W_dq = pb.dram("W_dq", [128, D], F32)
    W_uq = pb.dram("W_uq", [D, 128], F32)
    W_dkvT = pb.dram("W_dkvT", [D, 128], F32)
    W_uk = pb.dram("W_uk", [D, 128], F32)
    W_uv = pb.dram("W_uv", [D, 128], F32)
    W_oT = pb.dram("W_oT", [D, D], F32)

    A = _Scope(pb)
    A.__enter__()
    Bsb = A.sb(128, NCH * 2048, BF16)
    Vsb = A.sb(128, D, BF16)
    with pb.scope() as sc:
        _pt = []
        wuq = sc.sb(128, NCH * 128, BF16)
        _pt.append(pb.side_dma(wuq[:].rearrange("p (c m) -> p c m", c=NCH),
                   W_uq.rearrange("(c p) m -> p c m", p=128), q="gpsimd"))
        wuk = sc.sb(128, NCH * 128, BF16)
        _pt.append(pb.side_dma(wuk[:].rearrange("p (c m) -> p c m", c=NCH),
                   W_uk.rearrange("(c p) m -> p c m", p=128), q="gpsimd"))
        wuv = sc.sb(128, NCH * 128, BF16)
        _pt.append(pb.side_dma(wuv[:].rearrange("p (c m) -> p c m", c=NCH),
                   W_uv.rearrange("(c p) m -> p c m", p=128), q="gpsimd"))
        wdq = sc.sb(128, D, BF16)
        _pt.append(pb.side_dma(wdq[:], W_dq, q="gpsimd"))
        wot = sc.sb(128, NCH * D, BF16)
        _pt.append(pb.side_dma(wot[:].rearrange("p (c m) -> p c m", c=NCH),
                   W_oT.rearrange("(c p) m -> p c m", p=128), q="gpsimd"))
        pb.join_many(_pt, "tensor")
        pp = sc.ps(128, 512, F32)
        Psm = sc.sb(128, 128, BF16)
        for k in range(NCH):
            pb.op("tensor", lambda e, k=k: e.matmul(
                pp[:, 0:128], wuq[:, k * 128:(k + 1) * 128], wuk[:, k * 128:(k + 1) * 128],
                start=(k == 0), stop=(k == NCH - 1)))
        pb.op("vector", lambda e: e.tensor_copy(Psm[:], pp[:, 0:128]))
        Msb = sc.sb(128, NCH * 128, BF16)
        for i in range(NCH):
            pb.op("tensor", lambda e, i=i: e.matmul(
                pp[:, 0:128], wdq[:, i * 128:(i + 1) * 128], Psm[:], start=True, stop=True))
            pb.op("vector", lambda e, i=i: e.tensor_copy(Msb[:, i * 128:(i + 1) * 128], pp[:, 0:128]))
        Gh = sc.sb(128, 128, BF16)
        for h in range(16):
            cc, p = h // 2, (h % 2) * 64
            pb.op("tensor", lambda e, cc=cc, p=p: e.matmul(
                pp[:, 0:128], wuq[p:p + 64, cc * 128:(cc + 1) * 128],
                Msb[p:p + 64, cc * 128:(cc + 1) * 128], start=True, stop=True))
            pb.op("vector", lambda e: e.tensor_copy(Gh[:], pp[:, 0:128]))
            for i in range(NCH):
                pb.op("tensor", lambda e, i=i: e.matmul(
                    pp[:, 0:128], wdq[:, i * 128:(i + 1) * 128], Gh[:], start=True, stop=True))
                pb.op("scalar", lambda e, i=i, h=h: e.activation(
                    Bsb[:, i * 2048 + h * 128:i * 2048 + (h + 1) * 128], pp[:, 0:128],
                    AF.Identity, scale=0.125))
        for nslc in range(2):
            for k in range(NCH):
                pb.op("tensor", lambda e, k=k, nslc=nslc: e.matmul(
                    pp[:, 0:512], wuv[:, k * 128:(k + 1) * 128],
                    wot[:, k * D + nslc * 512:k * D + nslc * 512 + 512],
                    start=(k == 0), stop=(k == NCH - 1)))
            pb.op("vector", lambda e, nslc=nslc: e.tensor_copy(
                Vsb[:, nslc * 512:(nslc + 1) * 512], pp[:, 0:512]))

    pb.side_fence()
    for s in range(NSEQ):
        toff = s * T
        with pb.scope() as sA:
            xbf = sA.sb(128, NCH * T, BF16)
            with pb.scope() as s1:
                xt2 = [s1.sb(128, NCH * CW, F32) for _ in range(2)]
                l62 = [s1.sb(128, NCH * CW, F32) for _ in range(2)]
                res1r = res1.rearrange("(c p) t -> p c t", p=128)
                loc6r = loc6.rearrange("(c p) t -> p c t", p=128)
                res2r = res2.rearrange("(c p) t -> p c t", p=128)
                def c1load(col, joins):
                    t0 = toff + col * CW
                    t1 = pb.side_dma(xt2[col % 2][:].rearrange("p (c w) -> p c w", c=NCH),
                                     res1r[:, :, t0:t0 + CW], joins=joins)
                    t2 = pb.side_dma(l62[col % 2][:].rearrange("p (c w) -> p c w", c=NCH),
                                     loc6r[:, :, t0:t0 + CW])
                    return (t1, t2)
                toks = {0: c1load(0, ())}
                stoks = {}
                for col in range(T // CW):
                    t0 = toff + col * CW
                    if col + 1 < T // CW:
                        toks[col + 1] = c1load(col + 1, (stoks[col - 1],) if col >= 1 else ())
                    pb.join_many(toks[col], "vector")
                    xa, la = xt2[col % 2], l62[col % 2]
                    pb.op("vector", lambda e, xa=xa, la=la: e.tensor_add(xa[:], xa[:], la[:]))
                    stoks[col] = pb.side_dma(res2r[:, :, t0:t0 + CW],
                                             xa[:].rearrange("p (c w) -> p c w", c=NCH))
                    xt = [xa[:, c * CW:(c + 1) * CW] for c in range(NCH)]
                    outs = [xbf[:, c * T + col * CW:c * T + col * CW + CW]
                            for c in range(NCH)]
                    emit_ln(pb, cm, xt, CW, outs)
            ckv = sA.sb(128, T, BF16)
            ckvt = sA.sb(128, 16 * 128, BF16)
            with pb.scope() as s2:
                wdkvt = s2.sb(128, NCH * 128, BF16)
                _dk = pb.side_dma(wdkvt[:].rearrange("p (c m) -> p c m", c=NCH),
                                  W_dkvT.rearrange("(c p) m -> p c m", p=128), q="gpsimd")
                pb.join(_dk, "tensor")
                pp = s2.ps(128, 512, F32)
                pst = s2.ps(128, 1024, BF16)
                for col in range(T // CW):
                    pb.chain_begin()
                    for k in range(NCH):
                        pb.op("tensor", lambda e, k=k, col=col: e.matmul(
                            pp[:, 0:CW], wdkvt[:, k * 128:(k + 1) * 128],
                            xbf[:, k * T + col * CW:k * T + col * CW + CW],
                            start=(k == 0), stop=(k == NCH - 1)))
                    pb.chain_end()
                    pb.op("vector", lambda e, col=col: e.tensor_copy(
                        ckv[:, col * CW:(col + 1) * CW], pp[:, 0:CW]))
                for g0 in range(0, 16, 4):
                    pb.chain_begin()
                    for j in range(4):
                        pb.op("tensor", lambda e, j=j, g0=g0: e.transpose(
                            pst[:, j * 128:(j + 1) * 128],
                            ckv[:, (g0 + j) * 128:(g0 + j + 1) * 128], cm.idn_bf[:]))
                    pb.chain_end()
                    pb.op("vector", lambda e, g0=g0: e.tensor_copy(
                        ckvt[:, g0 * 128:(g0 + 4) * 128], pst[:, 0:512]))
            with pb.scope() as s3:
                qlh = s3.sb(128, T, BF16)
                psq = s3.ps(128, 2048, F32)
                pst = s3.ps(128, 1024, BF16)
                psv = s3.ps(128, 512, F32)
                yps = s3.ps(128, 512, F32)
                Pt = s3.sb(128, 2048, BF16)
                Psb2 = [s3.sb(128, 2048, BF16) for _ in range(2)]
                nmax = s3.sb(128, 2, F32)
                ssum = s3.sb(128, 2, F32)
                rcp = s3.sb(128, 2, F32)
                ctxh = s3.sb(128, T, BF16)
                rt = s3.sb(128, T, F32)
                htoks = {}
                for h in range(16):
                    qetoks = {}
                    for col in range(T // CW):
                        par = (col % 4) * 512
                        pb.chain_begin()
                        for i in range(NCH):
                            pb.op("tensor", lambda e, i=i, col=col, h=h, par=par: e.matmul(
                                psq[:, par:par + CW], Bsb[:, i * 2048 + h * 128:i * 2048 + (h + 1) * 128],
                                xbf[:, i * T + col * CW:i * T + col * CW + CW],
                                start=(i == 0), stop=(i == NCH - 1)))
                        pb.chain_end()
                        qetoks[col] = pb.side_compute("scalar", [
                            lambda e, col=col, par=par: e.activation(
                                qlh[:, col * CW:(col + 1) * CW], psq[:, par:par + CW], AF.Identity)])
                    pb.join_many(list(qetoks.values()), "tensor")
                    smt = {}
                    def emit_qk(qi):
                        nk = 128 * (qi + 1)
                        q0 = qi * 128
                        if qi - 1 in smt:
                            pb.join(smt[qi - 1][0], "tensor")
                        pb.chain_begin()
                        pb.op("tensor", lambda e, q0=q0: e.matmul(
                            psq[:, q0:q0 + 128], cm.idn_bf[:], cm.tril[:],
                            start=True, stop=False, skip_group_check=True))
                        for ks_ in range(0, nk, 512):
                            w = min(512, nk - ks_)
                            diag = (ks_ + w == nk)
                            segs = [(ks_, w, True)] if not diag else (
                                ([(ks_, q0 - ks_, False)] if q0 > ks_ else []) + [(q0, 128, False)])
                            for (s0, sw, st_) in segs:
                                pb.op("tensor", lambda e, s0=s0, sw=sw, st_=st_, q0=q0: e.matmul(
                                    psq[:, s0:s0 + sw], qlh[:, q0:q0 + 128],
                                    ckv[:, s0:s0 + sw], start=st_, stop=True,
                                    skip_group_check=True))
                        pb.chain_end()
                    def emit_smax(qi):
                        nk = 128 * (qi + 1)
                        sl = qi % 2
                        Ps = Psb2[sl]
                        jm = (smt[qi - 2][0],) if qi - 2 in smt else ()
                        t_max = pb.side_compute("vector", [
                            lambda e, nk=nk, sl=sl: e.reduce_max(
                                nmax[:, sl:sl + 1], psq[:, 0:nk], axis=AX.X, negate=True)],
                            joins=jm)
                        jm2 = (smt[qi - 2][3],) if qi - 2 in smt else ()
                        t_exp = pb.side_compute("scalar", [
                            lambda e, nk=nk, sl=sl, Ps=Ps: e.activation(
                                Ps[:, 0:nk], psq[:, 0:nk], AF.Exp,
                                bias=nmax[:, sl:sl + 1], accum_out=ssum[:, sl:sl + 1])],
                            joins=(t_max,) + jm2)
                        jm3 = (smt[qi - 2][3],) if qi - 2 in smt else ()
                        t_rcp = pb.side_compute("vector", [
                            lambda e, sl=sl: e.reciprocal(rcp[:, sl:sl + 1], ssum[:, sl:sl + 1])],
                            joins=(t_exp,) + jm3)
                        t_ts = pb.side_compute("vector", [
                            lambda e, nk=nk, sl=sl, Ps=Ps: e.tensor_scalar_mul(
                                Ps[:, 0:nk], Ps[:, 0:nk], rcp[:, sl:sl + 1])],
                            joins=(t_rcp,))
                        smt[qi] = (t_exp, t_max, t_rcp, t_ts)
                    def emit_ptpv(qi):
                        nk = 128 * (qi + 1)
                        q0 = qi * 128
                        sl = qi % 2
                        Ps = Psb2[sl]
                        pb.join(smt[qi][3], "tensor")
                        nkc = qi + 1
                        for g0_ in range(0, nkc, 4):
                            gn = min(4, nkc - g0_)
                            pb.chain_begin()
                            for j in range(gn):
                                pb.op("tensor", lambda e, j=j, g0_=g0_, Ps=Ps: e.transpose(
                                    pst[:, j * 128:(j + 1) * 128],
                                    Ps[:, (g0_ + j) * 128:(g0_ + j + 1) * 128], cm.idn_bf[:]))
                            pb.chain_end()
                            pb.op("vector", lambda e, g0_=g0_, gn=gn: e.tensor_copy(
                                Pt[:, g0_ * 128:(g0_ + gn) * 128], pst[:, 0:gn * 128]))
                        pb.chain_begin()
                        for kc in range(nkc):
                            pb.op("tensor", lambda e, kc=kc, nkc=nkc: e.matmul(
                                psv[:, 0:128], ckvt[:, kc * 128:(kc + 1) * 128],
                                Pt[:, kc * 128:kc * 128 + 128],
                                start=(kc == 0), stop=(kc == nkc - 1)))
                        pb.chain_end()
                        pb.op("vector", lambda e, q0=q0: e.tensor_copy(
                            ctxh[:, q0:q0 + 128], psv[:, 0:128]))
                    emit_qk(0)
                    for qi in range(16):
                        emit_smax(qi)
                        if qi + 1 < 16:
                            emit_qk(qi + 1)
                        emit_ptpv(qi)
                    cc, p = h // 2, (h % 2) * 64
                    rtok = pb.side_dma(rt[p:p + 64, 0:T],
                                       res2[cc * 128 + p:cc * 128 + p + 64, toff:toff + T],
                                       joins=(htoks[h - 2],) if h >= 2 else ())
                    for col in range(T // CW):
                        pb.op("tensor", lambda e, col=col, h=h, p=p: e.matmul(
                            yps[p:p + 64, 0:CW], Vsb[:, h * 64:(h + 1) * 64],
                            ctxh[:, col * CW:(col + 1) * CW], start=True, stop=True))
                        if col == 0:
                            pb.join(rtok, "vector")
                        pb.op("vector", lambda e, p=p, col=col: e.tensor_add(
                            rt[p:p + 64, col * CW:(col + 1) * CW],
                            rt[p:p + 64, col * CW:(col + 1) * CW], yps[p:p + 64, 0:CW]))
                    htoks[h] = pb.side_dma(res3[cc * 128 + p:cc * 128 + p + 64, toff:toff + T],
                                           rt[p:p + 64, 0:T])
    A.__exit__(None, None, None)


def build_l1attn():
    pb = PB("l1attn")
    cm = Common(pb, masks=True)
    res1 = pb.dram("res1", [D, NTOK], F32)
    loc6 = pb.dram("loc6", [D, NTOK], F32)
    res3 = pb.dram("res3", [D, NTOK], F32, kind="ExternalOutput")
    res2 = pb.dram("res2", [D, NTOK], F32, kind="Internal")
    emit_l1attn(pb, cm, res1, loc6, res2, res3)
    return pb



def emit_l1ffn(pb, cm, res3, h_scr, outfm):
    wiT = pb.dram("wiT", [D, 4096], F32)
    woT = pb.dram("woT", [4096, D], F32)
    gate_wT = pb.dram("gate_wT", [D, 8], F32)
    gate_b = pb.dram("gate_b", [1, 8], F32)
    Ap = [pb.dram(f"Apack{g}", [D, 64], F32) for g in range(4)]
    Bp = [pb.dram(f"Bpack{g}", [64, 4096], F32) for g in range(4)]
    bi_c = cm.panel("bi_p", 32)
    bo_c = cm.panel("bo_p", 8)

    gates_r = pb.sb(8, NTOK, F32)
    vs2_r = pb.sb(1, NTOK, F32)
    pb.side_fence()
    with pb.scope() as sX:
        xbf = sX.sb(128, NCH * NTOK, BF16)
        with pb.scope() as s1:
            xt2 = [s1.sb(128, NCH * CW, F32) for _ in range(2)]
            res3r = res3.rearrange("(c p) t -> p c t", p=128)
            def xload(col):
                t0 = col * CW
                return pb.side_dma(xt2[col % 2][:].rearrange("p (c w) -> p c w", c=NCH),
                                   res3r[:, :, t0:t0 + CW])
            toks = {0: xload(0)}
            for col in range(NTOK // CW):
                if col + 1 < NTOK // CW:
                    toks[col + 1] = xload(col + 1)
                pb.join(toks[col], "tensor")
                xa = xt2[col % 2]
                xt = [xa[:, c * CW:(c + 1) * CW] for c in range(NCH)]
                outs = [xbf[:, c * NTOK + col * CW:c * NTOK + col * CW + CW]
                        for c in range(NCH)]
                emit_ln(pb, cm, xt, CW, outs)
        with pb.scope() as s2:
            gw = s2.sb(128, NCH * 8, BF16)
            pb.dma(gw[:].rearrange("p (c m) -> p c m", c=NCH),
                   gate_wT.rearrange("(c p) m -> p c m", p=128), q="gpsimd")
            gbr = s2.sb(1, 8, F32)
            pb.dma(gbr[:], gate_b)
            gbb = s2.ps(128, 8, F32)
            pb.op("tensor", lambda e: e.matmul(gbb[:, 0:8], cm.ones_row_f[:], gbr[:], start=True, stop=True))
            gbbs = s2.sb(128, 8, F32)
            pb.op("vector", lambda e: e.tensor_copy(gbbs[:], gbb[:, 0:8]))
            pg = s2.ps(128, 8, F32)
            pt2 = s2.ps(128, 128, F32)
            lg = s2.sb(128, 8, F32)
            m1 = s2.sb(128, 1, F32)
            m2 = s2.sb(128, 1, F32)
            eqm = s2.sb(128, 8, F32)
            s2t = s2.sb(128, 8, F32)
            ssum = s2.sb(128, 1, F32)
            rcp = s2.sb(128, 1, F32)
            ind = s2.sb(128, 8, F32)
            gcols = s2.sb(128, 8, F32)
            vs = s2.sb(128, 1, F32)
            for tt in range(NTOK // 128):
                pb.chain_begin()
                for cc2 in range(NCH):
                    pb.op("tensor", lambda e, cc2=cc2, tt=tt: e.matmul(
                        pg[:, 0:8], xbf[:, cc2 * NTOK + tt * 128:cc2 * NTOK + tt * 128 + 128],
                        gw[:, cc2 * 8:(cc2 + 1) * 8], start=(cc2 == 0), stop=(cc2 == NCH - 1)))
                pb.chain_end()
                pb.op("vector", lambda e: e.tensor_add(lg[:], pg[:, 0:8], gbbs[:]))
                pb.op("vector", lambda e: e.reduce_max(m1[:], lg[:], axis=AX.X, negate=True))
                pb.op("scalar", lambda e: e.activation(lg[:], lg[:], AF.Exp, bias=m1[:, 0:1],
                                                       accum_out=ssum[:, 0:1]))
                pb.op("vector", lambda e: e.reciprocal(rcp[:], ssum[:]))
                pb.op("vector", lambda e: e.tensor_scalar_mul(lg[:], lg[:], rcp[:, 0:1]))
                pb.op("vector", lambda e: e.reduce_max(m1[:], lg[:], axis=AX.X))
                pb.op("vector", lambda e: e.tensor_scalar(eqm[:], lg[:], m1[:, 0:1], None, ALU.is_ge))
                pb.op("vector", lambda e: e.scalar_tensor_tensor(s2t[:], eqm[:], NEG, lg[:], ALU.mult, ALU.add))
                pb.op("vector", lambda e: e.reduce_max(m2[:], s2t[:], axis=AX.X))
                pb.op("vector", lambda e: e.tensor_scalar(ind[:], lg[:], m2[:, 0:1], None, ALU.is_ge))
                pb.op("vector", lambda e: e.tensor_add(vs[:], m1[:], m2[:]))
                pb.op("vector", lambda e: e.tensor_scalar_mul(gcols[:], ind[:], vs[:, 0:1]))
                pb.op("tensor", lambda e, tt=tt: e.transpose(pt2[0:8, 0:128], gcols[:], cm.idn_f[:]))
                pb.op("vector", lambda e, tt=tt: e.tensor_copy(
                    gates_r[:, tt * 128:(tt + 1) * 128], pt2[0:8, 0:128]))
                pb.op("vector", lambda e: e.reduce_sum(vs[:], gcols[:], axis=AX.X))
                pb.op("tensor", lambda e, tt=tt: e.transpose(pt2[0:1, 0:128], vs[:], cm.idn_f[:]))
                pb.op("vector", lambda e, tt=tt: e.tensor_copy(
                    vs2_r[:, tt * 128:(tt + 1) * 128], pt2[0:1, 0:128]))
        s_g = [sX.sb(64, NTOK, BF16) for _ in range(4)]
        with pb.scope() as s3:
            ap_sb2 = [s3.sb(128, NCH * 64, BF16) for _ in range(2)]
            pp = s3.ps(128, 512, F32)
            def aload(g):
                return pb.side_dma(ap_sb2[g % 2][:].rearrange("p (c m) -> p c m", c=NCH),
                                   Ap[g].rearrange("(c p) m -> p c m", p=128), q="gpsimd")
            atoks = {0: aload(0)}
            for g in range(4):
                if g + 1 < 4:
                    atoks[g + 1] = aload(g + 1)
                pb.join(atoks[g], "tensor")
                ap_sb = ap_sb2[g % 2]
                for col in range(NTOK // CW):
                    pb.chain_begin()
                    for k in range(NCH):
                        pb.op("tensor", lambda e, k=k, col=col: e.matmul(
                            pp[0:64, 0:CW], ap_sb[:, k * 64:(k + 1) * 64],
                            xbf[:, k * NTOK + col * CW:k * NTOK + col * CW + CW],
                            start=(k == 0), stop=(k == NCH - 1)))
                    pb.chain_end()
                    pb.op("vector", lambda e, g=g, col=col: e.tensor_copy(
                        s_g[g][:, col * CW:(col + 1) * CW], pp[0:64, 0:CW]))
        with pb.scope() as s4:
            bg_sb = [s4.sb(64, 4096, BF16) for _ in range(4)]
            bgtoks = [pb.side_dma(bg_sb[g][:], Bp[g], q="gpsimd") for g in range(4)]
            pb.join_many(bgtoks, "tensor")
            wcol2 = [s4.sb(128, NCH * 128, BF16) for _ in range(2)]
            psb_ = s4.ps(128, 1024, F32)
            psd = s4.ps(128, 4 * 512, F32)
            pgb = s4.ps(128, 512, F32)
            base_bf = s4.sb(128, 512, BF16)
            gb_bf = s4.sb(128, 8 * 512, BF16)
            dtl = s4.sb(128, 4 * 512, BF16)
            rg = s4.sb(128, 4 * 512, BF16)
            tpair = s4.sb(128, 2 * 512, F32)
            hacc2 = [s4.sb(128, 512, F32) for _ in range(2)]
            hbf = s4.sb(128, 512, BF16)
            sel8 = s4.sb(8, 1024, F32)
            stok = pb.side_dma(sel8[:], pb.dram("sel8", [8, 1024], F32))
            pb.join(stok, "tensor")
            hbf2 = [hbf, s4.sb(128, 512, BF16)]
            def wiload(col, m):
                return pb.side_dma(
                    wcol2[m % 2][:].rearrange("p (k m) -> p k m", k=NCH),
                    wiT[:, m * 128:(m + 1) * 128].rearrange("(k p) m -> p k m", p=128),
                    q="gpsimd")
            wtoks = {(0, 0): wiload(0, 0)}
            hprev = [None, None]
            ncol = NTOK // CW
            units = [(col, m) for col in range(ncol) for m in range(32)]
            def emit_base(u):
                col, m = units[u]
                if m + 1 < 32:
                    wtoks[(col, m + 1)] = wiload(col, m + 1)
                elif col + 1 < ncol:
                    wtoks[(col + 1, 0)] = wiload(col + 1, 0)
                pb.join(wtoks[(col, m)], "tensor")
                wcol = wcol2[m % 2]
                par = (u % 2) * 512
                pb.chain_begin()
                for k in range(NCH):
                    pb.op("tensor", lambda e, k=k, m=m, col=col, wcol=wcol, par=par: e.matmul(
                        psb_[:, par:par + CW], wcol[:, k * 128:(k + 1) * 128],
                        xbf[:, k * NTOK + col * CW:k * NTOK + col * CW + CW],
                        start=(k == 0), stop=(k == NCH - 1)))
                pb.chain_end()
            def emit_delta(u, r):
                col, m = units[u]
                pb.chain_begin()
                for v in range(4):
                    e8 = 4 * r + v
                    g, j = e8 // 2, e8 % 2
                    pb.op("tensor", lambda e, g=g, j=j, m=m, col=col, v=v: e.matmul(
                        psd[:, v * CW:(v + 1) * CW],
                        bg_sb[g][32 * j:32 * j + 32, m * 128:(m + 1) * 128],
                        s_g[g][32 * j:32 * j + 32, col * CW:col * CW + CW],
                        start=True, stop=True))
                pb.chain_end()
            for col in range(ncol):
                t0 = col * CW
                for e8 in range(8):
                    pb.op("tensor", lambda e, e8=e8, t0=t0: e.matmul(
                        pgb[:, 0:CW], sel8[:, e8 * 128:(e8 + 1) * 128],
                        gates_r[:, t0:t0 + CW], start=True, stop=True))
                    pb.op("vector", lambda e, e8=e8: e.tensor_copy(
                        gb_bf[:, e8 * 512:e8 * 512 + CW], pgb[:, 0:CW]))
                for m in range(32):
                    u = col * 32 + m
                    if u == 0:
                        emit_base(0)
                    par = (u % 2) * 512
                    pb.op("scalar", lambda e, m=m, par=par: e.activation(
                        base_bf[:], psb_[:, par:par + CW], AF.Identity, bias=bi_c[:, m:m + 1]))
                    emit_delta(u, 0)
                    pb.chain_begin()
                    for v in range(4):
                        pb.op("vector", lambda e, v=v: e.tensor_add(
                            dtl[:, v * CW:(v + 1) * CW], base_bf[:],
                            psd[:, v * CW:(v + 1) * CW]))
                    pb.chain_end()
                    emit_delta(u, 1)
                    if u + 1 < len(units):
                        emit_base(u + 1)
                    for r in range(2):
                        hr = hacc2[r]
                        if r == 1:
                            pb.chain_begin()
                            for v in range(4):
                                pb.op("vector", lambda e, v=v: e.tensor_add(
                                    dtl[:, v * CW:(v + 1) * CW], base_bf[:],
                                    psd[:, v * CW:(v + 1) * CW]))
                            pb.chain_end()
                        pb.chain_begin()
                        for v in range(4):
                            pb.op("gpsimd", lambda e, v=v: e.tensor_scalar_max(
                                rg[:, v * CW:(v + 1) * CW], dtl[:, v * CW:(v + 1) * CW], 0.0))
                        pb.chain_end()
                        pb.chain_begin()
                        for v in range(4):
                            e8 = 4 * r + v
                            pb.op("gpsimd", lambda e, v=v, e8=e8: e.tensor_mul(
                                rg[:, v * CW:(v + 1) * CW], rg[:, v * CW:(v + 1) * CW],
                                gb_bf[:, e8 * 512:e8 * 512 + CW]))
                        pb.chain_end()
                        pb.chain_begin()
                        pb.op("vector", lambda e: e.tensor_add(
                            tpair[:, 0:CW], rg[:, 0:CW], rg[:, CW:2 * CW]))
                        pb.op("vector", lambda e: e.tensor_add(
                            tpair[:, CW:2 * CW], rg[:, 2 * CW:3 * CW], rg[:, 3 * CW:4 * CW]))
                        pb.chain_end()
                        pb.op("vector", lambda e, hr=hr: e.tensor_add(
                            hr[:], tpair[:, 0:CW], tpair[:, CW:2 * CW]))
                    hb = hbf2[m % 2]
                    if hprev[m % 2] is not None:
                        pb.join(hprev[m % 2], "vector")
                    pb.op("vector", lambda e, hb=hb: e.tensor_add(hb[:], hacc2[0][:], hacc2[1][:]))
                    hprev[m % 2] = pb.side_dma(
                        h_scr[m * 128:(m + 1) * 128, t0:t0 + CW], hb[:])
    with pb.scope() as s5:
        pb.side_fence()
        wosb = s5.sb(128, 32 * D, BF16)
        wotoks = [pb.side_dma(wosb[:, kc * D:(kc + 1) * D],
                              woT[kc * 128:(kc + 1) * 128, :], q="gpsimd")
                  for kc in range(32)]
        pb.join_many(wotoks, "tensor")
        mm2 = [s5.sb(128, 512, BF16) for _ in range(2)]
        rt2 = [s5.sb(128, NCH * CW, F32) for _ in range(2)]
        out_all = s5.sb(128, NCH * CW, F32)
        pso = [s5.ps(128, 512, F32) for _ in range(NCH)]
        vsbs = s5.sb(128, 512, F32)
        t1 = s5.sb(128, 512, F32)
        res3r = res3.rearrange("(c p) t -> p c t", p=128)
        outr = outfm.rearrange("(c p) t -> p c t", p=128)
        ncol = NTOK // CW
        def mload(col, kc):
            t0 = col * CW
            return pb.side_dma(mm2[kc % 2][:], h_scr[kc * 128:(kc + 1) * 128, t0:t0 + CW])
        def rload(col):
            t0 = col * CW
            return pb.side_dma(rt2[col % 2][:].rearrange("p (c w) -> p c w", c=NCH),
                               res3r[:, :, t0:t0 + CW])
        mtoks = {(0, 0): mload(0, 0)}
        rtoks = {0: rload(0)}
        otok = [None]
        for col in range(ncol):
            t0 = col * CW
            if col + 1 < ncol:
                rtoks[col + 1] = rload(col + 1)
            pb.op("tensor", lambda e, t0=t0: e.matmul(
                pso[0][:, 0:CW], cm.ones_row_f[:], vs2_r[:, t0:t0 + CW], start=True, stop=True))
            pb.op("vector", lambda e: e.tensor_copy(vsbs[:], pso[0][:, 0:CW]))
            for kc in range(32):
                if kc + 1 < 32:
                    mtoks[(col, kc + 1)] = mload(col, kc + 1)
                elif col + 1 < ncol:
                    mtoks[(col + 1, 0)] = mload(col + 1, 0)
                pb.join(mtoks[(col, kc)], "tensor")
                mb = mm2[kc % 2]
                pb.chain_begin()
                for dc in range(NCH):
                    pb.op("tensor", lambda e, kc=kc, dc=dc, mb=mb: e.matmul(
                        pso[dc][:, 0:CW], wosb[:, kc * D + dc * 128:kc * D + (dc + 1) * 128],
                        mb[:], start=(kc == 0), stop=(kc == 31)))
                pb.chain_end()
            pb.join(rtoks[col], "vector")
            if otok[0] is not None:
                pb.join(otok[0], "vector")
            ra = rt2[col % 2]
            for dc in range(NCH):
                pb.op("vector", lambda e, dc=dc: e.scalar_tensor_tensor(
                    t1[:], vsbs[:], bo_c[:, dc:dc + 1], pso[dc][:, 0:CW], ALU.mult, ALU.add))
                pb.op("vector", lambda e, dc=dc, ra=ra: e.tensor_add(
                    out_all[:, dc * CW:(dc + 1) * CW], ra[:, dc * CW:(dc + 1) * CW], t1[:]))
            otok[0] = pb.side_dma(outr[:, :, t0:t0 + CW],
                                  out_all[:].rearrange("p (c w) -> p c w", c=NCH))


def build_l1ffn():
    pb = PB("l1ffn")
    cm = Common(pb)
    res3 = pb.dram("res3", [D, NTOK], F32)
    outfm = pb.dram("outfm", [D, NTOK], F32, kind="ExternalOutput")
    h_scr = pb.dram("h_scr", [4096, NTOK], BF16, kind="Internal")
    emit_l1ffn(pb, cm, res3, h_scr, outfm)
    return pb


def build_fused():
    pb = PB("fused")
    cm = Common(pb, masks=True)
    xfm = pb.dram("xfm", [D, NTOK], F32)
    outfm = pb.dram("outfm", [D, NTOK], F32, kind="ExternalOutput")
    res1 = pb.dram("res1", [D, NTOK], F32, kind="Internal")
    loc_a = pb.dram("loc_a", [D, NTOK], F32, kind="Internal")
    loc_b = pb.dram("loc_b", [D, NTOK], F32, kind="Internal")
    loc_mid = pb.dram("loc_mid", [D, NTOK], F32, kind="Internal")
    res3 = pb.dram("res3", [D, NTOK], F32, kind="Internal")
    scr = pb.dram("scr", [4096, NTOK], BF16, kind="Internal")
    emit_l0attn(pb, cm, xfm, res1, loc_a)
    cur, nxt = loc_a, loc_b
    for i in range(6):
        emit_l0ffn_layer(pb, cm, i, cur, loc_mid, scr[0:2816, :], nxt)
        cur, nxt = nxt, cur
    emit_l1attn(pb, cm, res1, cur, nxt, res3)
    emit_l1ffn(pb, cm, res3, scr, outfm)
    return pb


_cache = {}


def _get(name, builder):
    if name not in _cache:
        pb = builder()
        nc = pb.finalize()
        _cache[name] = nc
    return _cache[name]


def _run(name, builder, feeds):
    import gc
    import os
    nc = _get(name, builder)
    in_maps = []
    for core in range(NCORES):
        m = {}
        for k, v in feeds.items():
            m[k] = np.asarray(v[core] if isinstance(v, list) else v)
        in_maps.append(m)
    res = run_bass_kernel_spmd(nc, in_maps, core_ids=list(range(NCORES)))
    out = res.results
    if os.environ.get("BASS_SIM_FP64"):
        del res
        gc.collect()
    return out


FUSED = True


def _full_feeds(h):
    feeds = {k: h[k] for k in ["idn_bf", "idn_f", "tril_bf", "triu_bf"]}
    feeds.update({k: h[k] for k in h if k.endswith("_hi") or k.endswith("_lo")})
    feeds.update(xfm=h["xfm"], W_attn=h["W_attn"], W_proj=h["W_proj"],
                 b_attn_p=h["b_attn_p"], b_proj_p=h["b_proj_p"],
                 qs2=h["qs2"], ks2=h["ks2"], qs2r=h["qs2r"], ks2r=h["ks2r"],
                 ln_gT=h["ln_gT"], ln_bT=h["ln_bT"], r2t_bf=h["r2t_bf"],
                 W_dq=h["W_dq"], W_uq=h["W_uq"], W_dkvT=h["W_dkvT"],
                 W_uk=h["W_uk"], W_uv=h["W_uv"], W_oT=h["W_oT"],
                 wiT=h["wiT"], woT=h["woT"], bi_p=h["bi_p"], bo_p=h["bo_p"],
                 gate_wT=h["gate_wT"], gate_b=h["gate_b"], sel8=h["sel8"])
    for i in range(6):
        for nm in ("Wqkv", "Wo", "W1a", "W1g", "W2", "b1a_p", "b1g_p", "b2_p"):
            feeds[f"{nm}{i}"] = h[f"{nm}{i}"]
    for g in range(4):
        feeds[f"Apack{g}"] = h[f"Apack{g}"]
        feeds[f"Bpack{g}"] = h[f"Bpack{g}"]
    return feeds


def kernel(**inputs):
    h = host_prep(inputs)
    if FUSED:
        r = _run("fused", build_fused, _full_feeds(h))
        out = np.empty((16, 2048, 1024), np.float32)
        for c in range(NCORES):
            out[2 * c:2 * c + 2] = r[c]["outfm"].T.reshape(2, 2048, 1024)
        return out

    const_feed = {k: h[k] for k in ["idn_bf", "idn_f", "tril_bf", "triu_bf"]}
    feeds = dict(const_feed)
    feeds.update(xfm=h["xfm"], W_attn=h["W_attn"], W_proj=h["W_proj"],
                 b_attn_p=h["b_attn_p"], b_proj_p=h["b_proj_p"])
    r = _run("l0attn", build_l0attn, feeds)
    res1 = [r[i]["res1"] for i in range(NCORES)]
    loc = [r[i]["loc0"] for i in range(NCORES)]

    for i in range(6):
        feeds = dict(const_feed)
        feeds.update({k: h[k] for k in h if k.endswith("_hi") or k.endswith("_lo")})
        feeds.update(loc_in=loc,
                     qs2=h["qs2"], ks2=h["ks2"], qs2r=h["qs2r"], ks2r=h["ks2r"],
                     ln_gT=h["ln_gT"], ln_bT=h["ln_bT"], r2t_bf=h["r2t_bf"])
        for nm in ("Wqkv", "Wo", "W1a", "W1g", "W2", "b1a_p", "b1g_p", "b2_p"):
            feeds[f"{nm}{i}"] = h[f"{nm}{i}"]
        r = _run(f"l0ffn{i}", lambda i=i: build_l0ffn_layer(i), feeds)
        loc = [r[c]["loc_out"] for c in range(NCORES)]

    feeds = dict(const_feed)
    feeds.update(res1=res1, loc6=loc, W_dq=h["W_dq"], W_uq=h["W_uq"],
                 W_dkvT=h["W_dkvT"], W_uk=h["W_uk"], W_uv=h["W_uv"], W_oT=h["W_oT"])
    r = _run("l1attn", build_l1attn, feeds)
    res3 = [r[i]["res3"] for i in range(NCORES)]

    feeds = dict(const_feed)
    feeds.update(res3=res3, wiT=h["wiT"], woT=h["woT"], bi_p=h["bi_p"],
                 bo_p=h["bo_p"], gate_wT=h["gate_wT"], gate_b=h["gate_b"], sel8=h["sel8"],
                 **{f"Apack{g}": h[f"Apack{g}"] for g in range(4)},
                 **{f"Bpack{g}": h[f"Bpack{g}"] for g in range(4)})
    r = _run("l1ffn", build_l1ffn, feeds)
    out = np.empty((16, 2048, 1024), np.float32)
    for c in range(NCORES):
        out[2 * c:2 * c + 2] = r[c]["outfm"].T.reshape(2, 2048, 1024)
    return out
